# version 07
# baseline (speedup 1.0000x reference)
import contextlib
import numpy as np
import ml_dtypes
import concourse.bass as bass
import concourse.mybir as mybir
from concourse.bass_utils import run_bass_kernel_spmd

F32 = mybir.dt.float32
BF16 = mybir.dt.bfloat16
AF = mybir.ActivationFunctionType
ALU = mybir.AluOpType
AX = mybir.AxisListType
BF = ml_dtypes.bfloat16

D = 1024
NCH = 8
DH = 4096
NCH_H = 32
H = 16
HD = 64
S = 2048
BPC = 2
T = BPC * S
NCORE = 8

ENGS = ("tensor", "vector", "scalar", "sync", "gpsimd")


def make_consts():
    c = {}
    c["ident"] = np.eye(128, dtype=np.float32).astype(BF)
    P = np.zeros((128, 128), np.float32)
    for blk in range(2):
        o = blk * 64
        for m in range(32):
            P[o + m + 32, o + m] = 1.0
            P[o + m, o + m + 32] = -1.0
    c["pswap"] = P.astype(BF)
    c["ones_f32"] = np.ones((1, 128), np.float32)
    c["ones_col"] = np.ones((128, 1), BF)
    c["ones_row"] = np.ones((1, 128), BF)
    inv = 1.0 / 10000.0 ** (np.arange(0, HD, 2, dtype=np.float64) / HD)
    tpos = np.arange(S, dtype=np.float64)
    ang = tpos[None, :] * inv[(np.arange(128) % 32), None]
    c["trig_s"] = np.sin(ang).astype(np.float32).astype(BF)
    c["trig_c"] = np.cos(ang).astype(np.float32).astype(BF)
    kp = np.zeros((128, S), np.float64)
    kp[0:32] = np.cos(tpos[None, :] * inv[:, None])
    kp[32:64] = np.sin(tpos[None, :] * inv[:, None])
    kp[64:128] = kp[0:64]
    c["kposdup"] = kp.astype(np.float32).astype(BF)
    kp2 = np.zeros((128, S), np.float64)
    kp2[0:32] = -np.sin(tpos[None, :] * inv[:, None])
    kp2[32:64] = np.cos(tpos[None, :] * inv[:, None])
    kp2[64:128] = kp2[0:64]
    c["kposdup2"] = kp2.astype(np.float32).astype(BF)
    mb = np.zeros((128, 4 * 512), np.float32)
    pidx = np.arange(128)[:, None]
    cc = np.arange(512)[None, :]
    for a in range(4):
        mb[:, a * 512:(a + 1) * 512] = np.where(cc <= 128 * a + pidx, 0.0, -1e30)
    c["mask_b"] = mb.astype(BF)
    j = np.arange(384)[None, :]
    rel = (j // 128 - 1) * 128 + (j % 128) - pidx
    c["mask_a"] = np.where(np.abs(rel) <= 64, 0.0, -1e30).astype(np.float32).astype(BF)
    return c


CONSTS = make_consts()


def cols_np(v):
    v = np.asarray(v, np.float32)
    return np.ascontiguousarray(v.reshape(-1, 128).T)


def wt_np(w):
    return np.ascontiguousarray(np.asarray(w, np.float32).T)


def slab_np(w, osz):
    WT = np.asarray(w, np.float32).T
    I, O = WT.shape
    nic = I // 128
    n = O // osz
    return np.ascontiguousarray(WT.reshape(nic, 128, n, osz).transpose(2, 1, 0, 3).reshape(n, 128, nic * osz))


class Prog:
    def __init__(self, nc):
        self.nc = nc
        self.streams = {e: [] for e in ENGS}
        self.cnt = {}
        self.sem = {}
        self.iic = {e: 0 for e in ENGS}
        self.war = {}

    def add_sem(self, stk, name):
        self.sem[name] = stk.enter_context(self.nc.semaphore(name))
        self.cnt[name] = 0

    def op(self, eng, fn, inc=None, inc_by=1):
        self.iic[eng] += 1
        if inc is None:
            self.streams[eng].append(fn)
            return None
        self.cnt[inc] += inc_by
        h = self.sem[inc]
        def wrapped(E, fn=fn, h=h, n=inc_by):
            fn(E).then_inc(h, n)
        self.streams[eng].append(wrapped)
        return self.cnt[inc]

    def wait(self, eng, sem, val):
        if val is None or val <= 0:
            return
        h = self.sem[sem]
        self.streams[eng].append(lambda E, h=h, v=val: E.wait_ge(h, v))

    def waitk(self, eng, key):
        v = self.war.get(key)
        if v:
            for semname, val in v.items():
                self.wait(eng, semname, val)

    def mark(self, key, semname, val):
        d = self.war.setdefault(key, {})
        d[semname] = max(d.get(semname, 0), val)

    def cleark(self, key):
        self.war.pop(key, None)

    def barrier(self, engs=ENGS):
        vals = {}
        for e in engs:
            sname = "b_" + e
            self.cnt[sname] += 1
            vals[e] = self.cnt[sname]
            h = self.sem[sname]
            self.streams[e].append(lambda E, h=h: E.nop().then_inc(h, 1))
        for e in engs:
            for o in engs:
                if o != e:
                    self.wait(e, "b_" + o, vals[o])


class MK:
    def __init__(self, stages=("A", "B", "C", "D"), c_layers=4, c_attn_only=False):
        self.stages = stages
        self.c_layers = c_layers
        self.c_attn_only = c_attn_only
        self.nc = bass.Bass("TRN2", target_bir_lowering=False)
        self.inp = {}
        self.dram = {}

    def declare(self):
        nc = self.nc
        add = lambda n, shp, dt=F32: self.inp.__setitem__(
            n, nc.dram_tensor(n, shp, dt, kind="ExternalInput").ap())
        add("x_in", [NCH, 128, T])
        for n, a in CONSTS.items():
            dt = BF16 if a.dtype == BF else F32
            self.inp[n] = nc.dram_tensor(n, list(a.shape), dt, kind="ExternalInput").ap()
        if "A" in self.stages:
            add("aqkvT", [D, 3 * D]); add("aoutT", [D, D])
            add("aw1T", [D, DH]); add("aw2T", [DH, D])
            add("ab1c", [128, NCH_H]); add("ab2c", [128, NCH])
            add("alngc", [128, NCH]); add("alnbc", [128, NCH])
        if "B" in self.stages or "D" in self.stages:
            add("fw1T", [DH // 128, 128, NCH * 128]); add("fw2T", [DH // 128, 128, NCH * 128])
            add("fh1T", [DH // 128, 128, NCH_H * 128]); add("fh2T", [DH // 128, 128, NCH_H * 128])
            add("fw3T", [D // 128, 128, NCH_H * 128])
            add("fs1c", [128, NCH_H]); add("fs2c", [128, NCH_H])
            add("fhs1c", [128, NCH_H]); add("fhs2c", [128, NCH_H]); add("fs3c", [128, NCH])
        if "C" in self.stages:
            for l in range(4):
                add(f"bWqT{l}", [2, 128, NCH * 512]); add(f"bWkT{l}", [2, 128, NCH * 512])
                add(f"bWvT{l}", [2, 128, NCH * 512]); add(f"bWoT{l}", [2, 128, NCH * 512])
                add(f"bbqc{l}", [128, NCH]); add(f"bbkc{l}", [128, NCH])
                add(f"bbvr{l}", [1, D]); add(f"bboc{l}", [128, NCH])
                add(f"buc{l}", [128, NCH])
                add(f"bW1T{l}", [DH // 512, 128, NCH * 512]); add(f"bW2T{l}", [D // 128, 128, NCH_H * 128])
                add(f"bb1c{l}", [128, NCH_H]); add(f"bb2c{l}", [128, NCH])
                add(f"bln1wc{l}", [128, NCH]); add(f"bln1bc{l}", [128, NCH])
                add(f"bln2wc{l}", [128, NCH]); add(f"bln2bc{l}", [128, NCH])
        self.y = nc.dram_tensor("y", [NCH, 128, T], F32, kind="ExternalOutput").ap()
        names = ["s_mid", "s_x1"]
        if "A" in self.stages:
            names += ["s_na"]
        if "C" in self.stages:
            names += ["s_x2", "s_x3", "s_h"]
        if "D" in self.stages and "s_x2" not in names:
            names += ["s_x2", "s_x3"]
        for n in names:
            self.dram[n] = nc.dram_tensor(n, [NCH, 128, T], F32, kind="Internal").ap()
        if "A" in self.stages:
            for n in ("qa", "ka"):
                self.dram[n] = nc.dram_tensor(n, [NCH, 128, T], BF16, kind="Internal").ap()
            self.dram["va"] = nc.dram_tensor("va", [T // 128, 128, D], BF16, kind="Internal").ap()

    def sbuf(self, stk, name, shape, dtype):
        self._uid = getattr(self, "_uid", 0) + 1
        return stk.enter_context(self.nc.sbuf_tensor(f"{name}_u{self._uid}", shape, dtype))

    def emit_ln_tile(self, P, g, src, t0, TT, xn, eps, gc=None, bc=None, xkeep=None,
                     ready=None, fsrc=None):
        lx = g["ln_x"]; sq = g["ln_sq"]
        rows = g["rows"]
        if xkeep is not None:
            xsl = xkeep
        else:
            xsl = [lx[:, c * 512:c * 512 + TT] for c in range(NCH)]
        P.waitk("tensor", "pb_stats")
        n_sum = n_sq = 0
        if xkeep is None:
            P.waitk("gpsimd", "ln_x")
            lx2 = lx.rearrange("p (c t) -> p c t", t=512)[:, :, 0:TT]
            nq = P.op("gpsimd", lambda gg, lx2=lx2: gg.dma_start(
                out=lx2, in_=src[:, :, t0:t0 + TT].rearrange("c p t -> p c t")),
                inc="qx0", inc_by=16)
            P.wait("scalar", "qx0", nq)
            P.wait("tensor", "qx0", nq)
        for c in range(NCH):
            buf = c % 2
            if False:
                pass
            elif ready is not None and ready[c] is not None:
                P.wait("scalar", ready[c][0], ready[c][1])
                P.wait("tensor", ready[c][0], ready[c][1])
            P.waitk("scalar", "ln_sq0")
            nsq = P.op("scalar", lambda s, c=c, buf=buf: s.activation(
                out=sq[buf][:, 0:TT], in_=xsl[c], func=AF.Square, bias=0.0, scale=1.0), inc="sc")
            nmm_s = P.op("tensor", lambda t, c=c: t.matmul(
                g["pb_sum"][0:1, 0:TT], g["ones_col"][:], xsl[c],
                start=(c == 0), stop=(c == NCH - 1)), inc="tc")
            P.wait("tensor", "sc", nsq)
            nmm_q = P.op("tensor", lambda t, c=c, buf=buf: t.matmul(
                g["pb_sq"][0:1, 0:TT], g["ones_col"][:], sq[buf][:, 0:TT],
                start=(c == 0), stop=(c == NCH - 1)), inc="tc")
            P.war["ln_sq0"] = {"tc": nmm_q}
            if c == NCH - 1:
                n_sum, n_sq, n_sq_sc = nmm_s, nmm_q, nsq
                if xkeep is None:
                    P.mark("ln_x", "tc", nmm_s)
                    P.mark("ln_x", "sc", nsq)
        mean = rows[0:1, 0:TT]; m2 = rows[0:1, 512:512 + TT]
        var = rows[0:1, 512:512 + TT]
        rstd = rows[0:1, 1024:1024 + TT]
        mr = rows[0:1, 1536:1536 + TT]; msq = rows[0:1, 2048:2048 + TT]
        lnv = rows[0:1, 2048:2048 + TT]
        P.waitk("vector", "ln_rows")
        P.wait("vector", "tc", n_sum)
        n1 = P.op("vector", lambda v: v.tensor_scalar_mul(mean, g["pb_sum"][0:1, 0:TT], 1.0 / D), inc="vc")
        P.wait("vector", "tc", n_sq)
        n2 = P.op("vector", lambda v: v.tensor_scalar_mul(m2, g["pb_sq"][0:1, 0:TT], 1.0 / D), inc="vc")
        P.war["pb_stats"] = {"vc": n2}
        P.wait("vector", "vc", n1)
        n3 = P.op("vector", lambda v: v.tensor_tensor(out=msq, in0=mean, in1=mean, op=ALU.mult), inc="vc")
        P.wait("vector", "vc", max(n2, n3))
        n4 = P.op("vector", lambda v: v.scalar_tensor_tensor(
            out=var, in0=m2, scalar=float(eps), in1=msq, op0=ALU.add, op1=ALU.subtract), inc="vc")
        P.wait("scalar", "vc", n4)
        n5 = P.op("scalar", lambda s: s.activation(out=lnv, in_=var, func=AF.Ln, bias=0.0, scale=1.0), inc="sc")
        P.wait("scalar", "sc", n5)
        n6 = P.op("scalar", lambda s: s.activation(out=rstd, in_=lnv, func=AF.Exp, bias=0.0, scale=-0.5), inc="sc")
        P.wait("vector", "sc", n6)
        n7 = P.op("vector", lambda v: v.tensor_tensor(out=mr, in0=mean, in1=rstd, op=ALU.mult), inc="vc")
        P.waitk("tensor", "pb_bcast")
        P.wait("tensor", "sc", n6)
        P.wait("tensor", "vc", n7)
        P.op("tensor", lambda t: t.matmul(g["pb_rb"][:, 0:TT], g["ones_f32"][:], rstd, start=True, stop=True))
        nb = P.op("tensor", lambda t: t.matmul(g["pb_mrb"][:, 0:TT], g["ones_f32"][:], mr, start=True, stop=True), inc="tc")
        P.war["ln_rows"] = {"tc": nb, "sc": n6}
        P.wait("scalar", "tc", nb)
        P.waitk("scalar", "rbsb")
        P.op("scalar", lambda s: s.activation(out=g["rb_sb"][:, 0:TT], in_=g["pb_rb"][:, 0:TT],
                                              func=AF.Identity, bias=0.0, scale=1.0), inc="sc")
        nbe = P.op("scalar", lambda s: s.activation(out=g["mrb_sb"][:, 0:TT], in_=g["pb_mrb"][:, 0:TT],
                                                    func=AF.Identity, bias=0.0, scale=1.0), inc="sc")
        P.war["pb_bcast"] = {"sc": nbe}
        outs = []
        t1 = g["lnt1f"] if fsrc is not None else g["lnt1"]
        t2 = g["lnt2"]
        prev_t1 = prev_t2 = None
        for c in range(NCH):
            if fsrc == "dram":
                P.waitk("sync", "ln_xf0")
                nf = P.op("sync", lambda sy, c=c: sy.dma_start(
                    out=g["ln_xf"][0][:, 0:TT], in_=src[c, :, t0:t0 + TT]),
                    inc="qf0", inc_by=16)
                asrc = g["ln_xf"][0][:, 0:TT]
                P.wait("vector", "qf0", nf)
            elif fsrc is not None:
                asrc, rsem, rval = fsrc[c]
                if rsem is not None:
                    P.wait("vector", rsem, rval)
            else:
                asrc = xsl[c]
            P.wait("vector", "sc", nbe)
            if prev_t1 is not None:
                P.wait("vector", "vc", prev_t1)
            na = P.op("vector", lambda v, c=c, asrc=asrc: v.tensor_tensor(
                out=t1[:, 0:TT], in0=asrc, in1=g["rb_sb"][:, 0:TT], op=ALU.mult), inc="vc")
            if fsrc == "dram":
                P.war["ln_xf0"] = {"vc": na}
            if xkeep is None and fsrc is None:
                P.mark("ln_x", "vc", na)
            P.wait("vector", "vc", na)
            if gc is None:
                nn = P.op("vector", lambda v, c=c: v.scalar_tensor_tensor(
                    out=xn[c], in0=g["mrb_sb"][:, 0:TT], scalar=-1.0, in1=t1[:, 0:TT],
                    op0=ALU.mult, op1=ALU.add), inc="vc")
                prev_t1 = nn
            else:
                if prev_t2 is not None:
                    P.wait("vector", "vc", prev_t2)
                nn0 = P.op("vector", lambda v: v.scalar_tensor_tensor(
                    out=t2[:, 0:TT], in0=g["mrb_sb"][:, 0:TT], scalar=-1.0, in1=t1[:, 0:TT],
                    op0=ALU.mult, op1=ALU.add), inc="vc")
                prev_t1 = nn0
                P.wait("vector", "vc", nn0)
                nn = P.op("vector", lambda v, c=c: v.tensor_scalar(
                    out=xn[c], in0=t2[:, 0:TT], scalar1=gc[:, c:c + 1], scalar2=bc[:, c:c + 1],
                    op0=ALU.mult, op1=ALU.add), inc="vc")
                prev_t2 = nn
            outs.append(nn)
        P.war["rbsb"] = {"vc": outs[-1]}
        if xkeep is not None:
            P.war["ln_keep"] = {"tc": n_sq, "sc": n_sq_sc, "vc": outs[-1]}
        return outs

    def emit_ffn_ternary(self, P, g, tag, x_src, x_dst):
        TT = 512
        PT = 1024
        npass = T // PT
        with contextlib.ExitStack() as stk:
            nb3 = [self.sbuf(stk, f"{tag}_nb{i}", [128, NCH * TT], BF16) for i in range(3)]
            A = self.sbuf(stk, f"{tag}_A", [128, NCH_H * PT], BF16)
            Bb = self.sbuf(stk, f"{tag}_B", [128, NCH_H * PT], BF16)
            t1t = self.sbuf(stk, f"{tag}_t1", [128, TT], BF16)
            t2t = self.sbuf(stk, f"{tag}_t2", [128, TT], BF16)
            xr = [self.sbuf(stk, f"{tag}_xr0", [128, TT], F32)] * 2
            xo = [self.sbuf(stk, f"{tag}_xo0", [128, TT], F32)] * 2
            wsl = [self.sbuf(stk, f"{tag}_w{i}", [128, NCH_H * 128], BF16) for i in range(2)]
            scol = {}
            for n in ("fs1c", "fs2c", "fhs1c", "fhs2c", "fs3c"):
                shp = [128, NCH_H] if n != "fs3c" else [128, NCH]
                scol[n] = self.sbuf(stk, f"{tag}_" + n, shp, F32)
            for n, sb in scol.items():
                P.op("sync", lambda sy, n=n, sb=sb: sy.dma_start(out=sb[:], in_=self.inp[n][:]),
                     inc="qc", inc_by=16)
            q_cols = P.cnt["qc"]
            P.wait("scalar", "qc", q_cols)
            P.wait("vector", "qc", q_cols)
            P.wait("tensor", "qc", q_cols)

            SL = ([("fw1T", NCH, oc) for oc in range(NCH_H) for _ in (0,)] and [])
            SL = []
            for oc in range(NCH_H):
                SL.append(("fw1T", NCH, oc))
                SL.append(("fw2T", NCH, oc))
            SL += [("fh1T", NCH_H, oc) for oc in range(NCH_H)]
            SL += [("fh2T", NCH_H, oc) for oc in range(NCH_H)]
            SL += [("fw3T", NCH_H, oc) for oc in range(NCH)]
            NSL = len(SL)
            NTOT = npass * NSL
            slab_load = {}
            pf = {"i": 0}

            def emit_slab_load(o):
                mat, nic, oc = SL[o % NSL]
                buf = o % 2
                P.waitk("gpsimd", f"{tag}_wsl{buf}")
                n = P.op("gpsimd", lambda gg, mat=mat, nic=nic, oc=oc, buf=buf: gg.dma_start(
                    out=wsl[buf][:, 0:nic * 128],
                    in_=self.inp[mat][oc]),
                    inc=f"qw{buf}", inc_by=16)
                slab_load[o] = (f"qw{buf}", n)

            def pump(limit):
                while pf["i"] < min(limit, NTOT):
                    emit_slab_load(pf["i"])
                    pf["i"] += 1

            pump(2)
            prev_no = [None, None]
            prev_store = [None, None]
            ln_done = {}

            def emit_ln_half(ht):
                slot = ht % 3
                P.waitk("vector", f"{tag}_nb{slot}")
                xnl = [nb3[slot][:, c * TT:(c + 1) * TT] for c in range(NCH)]
                outs = self.emit_ln_tile(P, g, x_src, ht * TT, TT, xnl, 1e-5)
                ln_done[ht] = outs[-1]

            for ps in range(npass):
                t0 = ps * PT
                base = ps * NSL
                if ps == 0:
                    emit_ln_half(0)
                    emit_ln_half(1)
                xn_counts = [[ln_done[2 * ps]], [ln_done[2 * ps + 1]]]

                a_ready = []
                sidx = 0
                prev_e1 = prev_e2 = None
                for oc in range(NCH_H):
                    o1, o2 = base + sidx, base + sidx + 1
                    b1, b2 = o1 % 2, o2 % 2
                    for half in range(2):
                        toff = half * TT
                        P.wait("tensor", slab_load[o1][0], slab_load[o1][1])
                        P.wait("tensor", slab_load[o2][0], slab_load[o2][1])
                        P.wait("tensor", "vc", ln_done[2 * ps + half])
                        P.waitk("tensor", "pb0")
                        slot = (2 * ps + half) % 3
                        for ic in range(NCH):
                            P.op("tensor", lambda t, ic=ic, b1=b1, slot=slot: t.matmul(
                                g["pb"][0][:, 0:TT],
                                wsl[b1].rearrange("p (ic o) -> p ic o", o=128)[:, ic, :],
                                nb3[slot][:, ic * TT:(ic + 1) * TT],
                                start=(ic == 0), stop=(ic == NCH - 1)),
                                inc="tc" if ic == NCH - 1 else None)
                        nmm1 = P.cnt["tc"]
                        P.waitk("tensor", "pb1")
                        for ic in range(NCH):
                            P.op("tensor", lambda t, ic=ic, b2=b2, slot=slot: t.matmul(
                                g["pb"][1][:, 0:TT],
                                wsl[b2].rearrange("p (ic o) -> p ic o", o=128)[:, ic, :],
                                nb3[slot][:, ic * TT:(ic + 1) * TT],
                                start=(ic == 0), stop=(ic == NCH - 1)),
                                inc="tc" if ic == NCH - 1 else None)
                        nmm2 = P.cnt["tc"]
                        P.mark(f"{tag}_nb{slot}", "tc", nmm2)
                        P.wait("scalar", "tc", nmm1)
                        if prev_e1 is not None:
                            P.wait("scalar", "vc", prev_e1)
                        e1 = P.op("scalar", lambda s, oc=oc: s.activation(
                            out=t1t[:, 0:TT], in_=g["pb"][0][:, 0:TT], func=AF.Silu,
                            bias=0.0, scale=scol["fs1c"][:, oc:oc + 1]), inc="sc")
                        P.war["pb0"] = {"sc": e1}
                        P.wait("scalar", "tc", nmm2)
                        e2 = P.op("scalar", lambda s, oc=oc: s.activation(
                            out=t2t[:, 0:TT], in_=g["pb"][1][:, 0:TT], func=AF.Identity,
                            bias=0.0, scale=scol["fs2c"][:, oc:oc + 1]), inc="sc")
                        P.war["pb1"] = {"sc": e2}
                        P.wait("vector", "sc", e2)
                        nA = P.op("vector", lambda v, oc=oc, toff=toff: v.tensor_tensor(
                            out=A[:, oc * PT + toff: oc * PT + toff + TT],
                            in0=t1t[:, 0:TT], in1=t2t[:, 0:TT], op=ALU.mult), inc="vc")
                        prev_e1 = prev_e2 = nA
                        a_ready.append(nA)
                    P.mark(f"{tag}_wsl{b1}", "tc", nmm1)
                    P.mark(f"{tag}_wsl{b2}", "tc", P.cnt["tc"])
                    sidx += 2
                    pump(base + sidx + 2)
                a_all = max(a_ready)

                def h_layer(src_buf, dst_buf, scol_name, ready_sem, ready_val, sidx0):
                    sidx = sidx0
                    done = []
                    for oc in range(NCH_H):
                        o = base + sidx
                        bw = o % 2
                        for half in range(2):
                            toff = half * TT
                            bname = f"pb{half}"
                            P.wait("tensor", slab_load[o][0], slab_load[o][1])
                            P.wait("tensor", ready_sem, ready_val)
                            P.waitk("tensor", bname)
                            for ic in range(NCH_H):
                                P.op("tensor", lambda t, ic=ic, bw=bw, toff=toff, half=half: t.matmul(
                                    g["pb"][half][:, 0:TT],
                                    wsl[bw].rearrange("p (ic o) -> p ic o", o=128)[:, ic, :],
                                    src_buf[:, ic * PT + toff: ic * PT + toff + TT],
                                    start=(ic == 0), stop=(ic == NCH_H - 1)),
                                    inc="tc" if ic == NCH_H - 1 else None)
                            nmm = P.cnt["tc"]
                            P.wait("scalar", "tc", nmm)
                            ne = P.op("scalar", lambda s, oc=oc, toff=toff, half=half: s.activation(
                                out=dst_buf[:, oc * PT + toff: oc * PT + toff + TT],
                                in_=g["pb"][half][:, 0:TT], func=AF.Gelu,
                                bias=0.0, scale=scol[scol_name][:, oc:oc + 1]), inc="sc")
                            P.war[bname] = {"sc": ne}
                            done.append(ne)
                        P.mark(f"{tag}_wsl{bw}", "tc", P.cnt["tc"])
                        sidx += 1
                        pump(base + sidx + 2)
                    return max(done), sidx

                b_all, sidx = h_layer(A, Bb, "fhs1c", "vc", a_all, sidx)
                a2_all, sidx = h_layer(Bb, A, "fhs2c", "sc", b_all, sidx)

                if ps + 1 < npass:
                    emit_ln_half(2 * ps + 2)
                    emit_ln_half(2 * ps + 3)

                for oc in range(NCH):
                    o = base + sidx
                    bw = o % 2
                    for half in range(2):
                        toff = half * TT
                        bname = f"pb{half}"
                        if prev_no[0] is not None:
                            P.wait("sync", "vc", prev_no[0])
                        nf = P.op("sync", lambda sy, oc=oc, t0=t0, toff=toff: sy.dma_start(
                            out=xr[0][:, 0:TT], in_=x_src[oc, :, t0 + toff:t0 + toff + TT]),
                            inc="qr0", inc_by=16)
                        P.wait("tensor", slab_load[o][0], slab_load[o][1])
                        P.wait("tensor", "sc", a2_all)
                        P.waitk("tensor", bname)
                        for ic in range(NCH_H):
                            P.op("tensor", lambda t, ic=ic, bw=bw, toff=toff, half=half: t.matmul(
                                g["pb"][half][:, 0:TT],
                                wsl[bw].rearrange("p (ic o) -> p ic o", o=128)[:, ic, :],
                                A[:, ic * PT + toff: ic * PT + toff + TT],
                                start=(ic == 0), stop=(ic == NCH_H - 1)),
                                inc="tc" if ic == NCH_H - 1 else None)
                        nmm = P.cnt["tc"]
                        P.wait("vector", "tc", nmm)
                        P.wait("vector", "qr0", nf)
                        if prev_store[0] is not None:
                            P.wait("vector", "qo0", prev_store[0])
                        no = P.op("vector", lambda v, oc=oc, toff=toff, half=half: v.scalar_tensor_tensor(
                            out=xo[0][:, 0:TT], in0=g["pb"][half][:, 0:TT],
                            scalar=scol["fs3c"][:, oc:oc + 1],
                            in1=xr[0][:, 0:TT], op0=ALU.mult, op1=ALU.add), inc="vc")
                        P.war[bname] = {"vc": no}
                        prev_no[0] = no
                        P.wait("sync", "vc", no)
                        nst = P.op("sync", lambda sy, oc=oc, t0=t0, toff=toff: sy.dma_start(
                            out=x_dst[oc, :, t0 + toff:t0 + toff + TT], in_=xo[0][:, 0:TT]),
                            inc="qo0", inc_by=16)
                        prev_store[0] = nst
                    P.mark(f"{tag}_wsl{bw}", "tc", P.cnt["tc"])
                    sidx += 1
                    pump(base + sidx + 2)
            for sname in ("qo0",):
                P.wait("sync", sname, P.cnt[sname])
                P.wait("gpsimd", sname, P.cnt[sname])
            P.barrier()


    def emit_stage_a(self, P, g, x_src, x_dst):
        nc = self.nc
        TT = 512
        qa, ka, va = self.dram["qa"], self.dram["ka"], self.dram["va"]
        s_na, s_mid = self.dram["s_na"], self.dram["s_mid"]

        with contextlib.ExitStack() as stk:
            qkvw = self.sbuf(stk, "a0_qkvw", [128, NCH * 3 * D], BF16)
            na_f = self.sbuf(stk, "a0_naf", [128, NCH * TT], F32)
            na_b = self.sbuf(stk, "a0_nab", [128, NCH * TT], BF16)
            xn = self.sbuf(stk, "a0_xn", [128, NCH * TT], BF16)
            qet = [self.sbuf(stk, f"a0_qe{i}", [128, TT], BF16) for i in range(2)]
            ket = [self.sbuf(stk, f"a0_ke{i}", [128, TT], BF16) for i in range(2)]
            vti = [self.sbuf(stk, f"a0_vt{i}", [128, 1024], BF16) for i in range(2)]
            gcol = self.sbuf(stk, "a0_gc", [128, NCH], F32)
            bcol = self.sbuf(stk, "a0_bc", [128, NCH], F32)
            for ic in range(NCH):
                P.op("gpsimd", lambda gg, ic=ic: gg.dma_start(
                    out=qkvw[:, ic * 3 * D:(ic + 1) * 3 * D],
                    in_=self.inp["aqkvT"][ic * 128:(ic + 1) * 128, :]), inc="qg0", inc_by=16)
            P.op("sync", lambda sy: sy.dma_start(out=gcol[:], in_=self.inp["alngc"][:]), inc="qc", inc_by=16)
            P.op("sync", lambda sy: sy.dma_start(out=bcol[:], in_=self.inp["alnbc"][:]), inc="qc", inc_by=16)
            P.wait("tensor", "qg0", P.cnt["qg0"])
            P.wait("tensor", "qc", P.cnt["qc"])
            P.wait("vector", "qc", P.cnt["qc"])
            P.wait("vector", "qg0", P.cnt["qg0"])
            P.wait("scalar", "qc", P.cnt["qc"])

            for b in range(BPC):
                for tile in range(S // TT):
                    t0 = b * S + tile * TT
                    naps = [na_f[:, c * TT:(c + 1) * TT] for c in range(NCH)]
                    P.waitk("vector", "a0_naf")
                    outs0 = self.emit_ln_tile(P, g, x_src, t0, TT, naps, 1e-5, fsrc="dram")
                    stc = []
                    P.waitk("vector", "a0_nab")
                    casts = []
                    for c in range(NCH):
                        P.wait("sync", "vc", outs0[c])
                        nst = P.op("sync", lambda sy, c=c, t0=t0: sy.dma_start(
                            out=s_na[c, :, t0:t0 + TT], in_=na_f[:, c * TT:(c + 1) * TT]),
                            inc="qd1", inc_by=16)
                        stc.append(nst)
                        P.wait("vector", "vc", outs0[c])
                        ncst = P.op("vector", lambda v, c=c: v.tensor_copy(
                            na_b[:, c * TT:(c + 1) * TT], na_f[:, c * TT:(c + 1) * TT]), inc="vc")
                        casts.append(ncst)
                    P.war["a0_naf"] = {"qd1": stc[-1], "vc": casts[-1]}
                    P.waitk("vector", "a0_xn")
                    nb_sl = [na_b[:, c * TT:(c + 1) * TT] for c in range(NCH)]
                    xn_sl = [xn[:, c * TT:(c + 1) * TT] for c in range(NCH)]
                    outs1 = self.emit_ln_tile(P, g, None, t0, TT, xn_sl, 1e-5,
                                              gc=gcol, bc=bcol, xkeep=nb_sl,
                                              ready=[("vc", casts[c]) for c in range(NCH)])
                    P.war["a0_nab"] = dict(P.war["ln_keep"])
                    xn_ready = outs1[-1]
                    for which, dst, scale, ring, qbase in (
                            (0, qa, 0.125, qet, 2), (1, ka, 1.0, ket, 4)):
                        for oc in range(NCH):
                            og = which * D + oc * 128
                            bank = g["pb"][oc % 2]
                            qsem = f"qd{qbase + oc % 2}"
                            P.waitk("tensor", f"pb{oc % 2}")
                            P.wait("tensor", "vc", xn_ready)
                            for ic in range(NCH):
                                P.op("tensor", lambda t, ic=ic, og=og, bank=bank: t.matmul(
                                    bank[:, 0:TT], qkvw[:, ic * 3 * D + og: ic * 3 * D + og + 128],
                                    xn[:, ic * TT:(ic + 1) * TT],
                                    start=(ic == 0), stop=(ic == NCH - 1)),
                                    inc="tc" if ic == NCH - 1 else None)
                            nmm = P.cnt["tc"]
                            rb = ring[oc % 2]
                            P.wait("scalar", "tc", nmm)
                            P.waitk("scalar", f"a0r_{qbase}{oc % 2}")
                            ne = P.op("scalar", lambda s, rb=rb, bank=bank, scale=scale: s.activation(
                                out=rb[:, 0:TT], in_=bank[:, 0:TT], func=AF.Identity,
                                bias=0.0, scale=scale), inc="sc")
                            P.war[f"pb{oc % 2}"] = {"sc": ne}
                            P.wait("sync", "sc", ne)
                            nst = P.op("sync", lambda sy, rb=rb, oc=oc, t0=t0, dstd=dst: sy.dma_start(
                                out=dstd[oc, :, t0:t0 + TT], in_=rb[:, 0:TT]), inc=qsem, inc_by=16)
                            P.war[f"a0r_{qbase}{oc % 2}"] = {qsem: nst}
                    for ts in range(TT // 128):
                        vb = vti[ts % 2]
                        P.waitk("scalar", f"a0_vt{ts % 2}")
                        ev = []
                        for oh in range(2):
                            bank = g["pb"][oh]
                            P.waitk("tensor", f"pb{oh}")
                            P.wait("tensor", "vc", xn_ready)
                            for ic in range(NCH):
                                P.op("tensor", lambda t, ic=ic, ts=ts, oh=oh, bank=bank: t.matmul(
                                    bank[:, 0:TT],
                                    xn[:, ic * TT + ts * 128: ic * TT + ts * 128 + 128],
                                    qkvw[:, ic * 3 * D + 2 * D + oh * TT: ic * 3 * D + 2 * D + oh * TT + TT],
                                    start=(ic == 0), stop=(ic == NCH - 1)),
                                    inc="tc" if ic == NCH - 1 else None)
                            nmm = P.cnt["tc"]
                            P.wait("scalar", "tc", nmm)
                            ne = P.op("scalar", lambda s, vb=vb, oh=oh, bank=bank: s.activation(
                                out=vb[:, oh * TT:(oh + 1) * TT], in_=bank[:, 0:TT],
                                func=AF.Identity, bias=0.0, scale=1.0), inc="sc")
                            P.war[f"pb{oh}"] = {"sc": ne}
                            ev.append(ne)
                        P.wait("sync", "sc", ev[-1])
                        nst = P.op("sync", lambda sy, vb=vb, t0=t0, ts=ts: sy.dma_start(
                            out=va[(t0 + ts * 128) // 128, :, :], in_=vb[:]), inc=f"qd{6 + ts % 2}", inc_by=16)
                        P.war[f"a0_vt{ts % 2}"] = {f"qd{6 + ts % 2}": nst}
            for sname in ("qd1", "qd2", "qd3", "qd4", "qd5", "qd6", "qd7"):
                P.wait("sync", sname, P.cnt[sname])
                P.wait("gpsimd", sname, P.cnt[sname])
            P.barrier()

        with contextlib.ExitStack() as stk:
            outw = self.sbuf(stk, "a1_outw", [128, NCH * D], BF16)
            qbuf = [self.sbuf(stk, f"a1_q{i}", [128, NCH * TT], BF16) for i in range(2)]
            kbuf = [self.sbuf(stk, f"a1_k{i}", [128, NCH * 768], BF16) for i in range(2)]
            vbuf = [self.sbuf(stk, f"a1_v{i}", [128, 6 * 1024], BF16) for i in range(2)]
            nabuf = [self.sbuf(stk, f"a1_na{i}", [128, NCH * TT], F32) for i in range(2)]
            ctx = self.sbuf(stk, "a1_ctx", [128, NCH * TT], BF16)
            ident = self.sbuf(stk, "a1_id", [128, 128], BF16)
            maska = self.sbuf(stk, "a1_mask", [128, 384], BF16)
            s_s = self.sbuf(stk, "a1_ss", [128, 384], F32)
            pbuf = self.sbuf(stk, "a1_p", [128, 384], BF16)
            ptb = self.sbuf(stk, "a1_pt", [128, 384], BF16)
            stats = self.sbuf(stk, "a1_st", [128, 4], F32)
            outt = [self.sbuf(stk, f"a1_o{i}", [128, TT], F32) for i in range(2)]
            for ic in range(NCH):
                P.op("gpsimd", lambda gg, ic=ic: gg.dma_start(
                    out=outw[:, ic * D:(ic + 1) * D],
                    in_=self.inp["aoutT"][ic * 128:(ic + 1) * 128, :]), inc="qg1", inc_by=16)
            P.op("sync", lambda sy: sy.dma_start(out=ident[:], in_=self.inp["ident"][:]), inc="qc", inc_by=16)
            P.op("sync", lambda sy: sy.dma_start(out=maska[:], in_=self.inp["mask_a"][:]), inc="qc", inc_by=16)
            P.wait("tensor", "qg1", P.cnt["qg1"])
            P.wait("tensor", "qc", P.cnt["qc"])

            NTILE = S // TT
            for b in range(BPC):
                for tile in range(NTILE):
                    t0 = b * S + tile * TT
                    r = (b * NTILE + tile) % 2
                    klo = max(b * S, t0 - 128)
                    khi = min(b * S + S, t0 + TT + 128)
                    kbase = t0 - 128
                    P.waitk("gpsimd", f"a1_q{r}")
                    nq = P.op("gpsimd", lambda gg, r=r, t0=t0: gg.dma_start(
                        out=qbuf[r].rearrange("p (c t) -> p c t", t=TT),
                        in_=qa[:, :, t0:t0 + TT].rearrange("c p t -> p c t")), inc=f"qg{3 + r}", inc_by=16)
                    P.waitk("gpsimd", f"a1_k{r}")
                    nk = P.op("gpsimd", lambda gg, r=r, klo=klo, khi=khi, kbase=kbase: gg.dma_start(
                        out=kbuf[r].rearrange("p (c t) -> p c t", t=768)[:, :, klo - kbase:khi - kbase],
                        in_=ka[:, :, klo:khi].rearrange("c p t -> p c t")), inc=f"qg{5 + r}", inc_by=16)
                    P.waitk("gpsimd", f"a1_v{r}")
                    nv = P.op("gpsimd", lambda gg, r=r, klo=klo, khi=khi, kbase=kbase: gg.dma_start(
                        out=vbuf[r].rearrange("p (c d) -> p c d", d=1024)[:, (klo - kbase) // 128:(khi - kbase) // 128, :],
                        in_=va[klo // 128:khi // 128, :, :].rearrange("c p d -> p c d")), inc=f"qg{7 + r}", inc_by=16)
                    P.waitk("sync", f"a1_na{r}")
                    nna = P.op("sync", lambda sy, r=r, t0=t0: sy.dma_start(
                        out=nabuf[r].rearrange("p (c t) -> p c t", t=TT),
                        in_=s_na[:, :, t0:t0 + TT].rearrange("c p t -> p c t")), inc=f"qd{12 + r}", inc_by=16)
                    P.wait("tensor", f"qg{3 + r}", nq)
                    P.wait("tensor", f"qg{5 + r}", nk)
                    P.wait("tensor", f"qg{7 + r}", nv)
                    last_q = last_k = last_v = 0
                    ctx_done = []
                    for hh in range(H):
                        hb = (hh % 2) * 64
                        hc = hh // 2
                        for j in range(4):
                            lo = 1 if (tile == 0 and j == 0) else 0
                            hi = 1 if (tile == NTILE - 1 and j == 3) else 2
                            W = (hi - lo + 1) * 128
                            bank = g["pb"][j % 2]
                            P.waitk("tensor", f"pb{j % 2}")
                            for dk in range(lo, hi + 1):
                                kcol = hc * 768 + (j + dk) * 128
                                P.op("tensor", lambda t, hb=hb, hc=hc, j=j, dk=dk, lo=lo, r=r, kcol=kcol, bank=bank: t.matmul(
                                    bank[:, 0:0] if False else bank[:, (dk - lo) * 128:(dk - lo) * 128 + 128],
                                    qbuf[r][hb:hb + 64, hc * TT + j * 128: hc * TT + j * 128 + 128],
                                    kbuf[r][hb:hb + 64, kcol:kcol + 128],
                                    start=(dk == lo), stop=False, tile_position=(hb, 0)))
                            nmm = P.op("tensor", lambda t, lo=lo, hi=hi, W=W, bank=bank: t.matmul(
                                bank[:, 0:W], ident[:], maska[:, lo * 128:lo * 128 + W],
                                start=False, stop=True), inc="tc")
                            last_q = nmm
                            P.wait("vector", "tc", nmm)
                            P.waitk("vector", "a1_ss")
                            nev = P.op("vector", lambda v, W=W, bank=bank: v.tensor_scalar(
                                out=s_s[:, 0:W], in0=bank[:, 0:W], scalar1=0.0, scalar2=None,
                                op0=ALU.bypass, op1=ALU.max, accum_out=stats[:, 0:1]), inc="vc")
                            P.war[f"pb{j % 2}"] = {"vc": nev}
                            P.wait("vector", "vc", nev)
                            nbias = P.op("vector", lambda v: v.tensor_scalar(
                                out=stats[:, 2:3], in0=stats[:, 0:1], scalar1=-1.0, scalar2=None,
                                op0=ALU.mult, op1=ALU.min, accum_out=stats[:, 1:2]), inc="vc")
                            P.wait("scalar", "vc", nbias)
                            P.waitk("scalar", "a1_p")
                            nexp = P.op("scalar", lambda s, W=W: s.activation(
                                out=pbuf[:, 0:W], in_=s_s[:, 0:W], func=AF.Exp,
                                bias=stats[:, 1:2], scale=1.0, accum_out=stats[:, 3:4]), inc="sc")
                            P.war["a1_ss"] = {"sc": nexp}
                            P.wait("vector", "sc", nexp)
                            nrs = P.op("vector", lambda v: v.reciprocal(stats[:, 2:3], stats[:, 3:4]), inc="vc")
                            P.wait("vector", "vc", nrs)
                            npn = P.op("vector", lambda v, W=W: v.tensor_scalar_mul(
                                pbuf[:, 0:W], pbuf[:, 0:W], stats[:, 2:3]), inc="vc")
                            P.wait("tensor", "vc", npn)
                            P.waitk("tensor", "pbt")
                            for rel in range(W // 128):
                                ntr = P.op("tensor", lambda t, rel=rel: t.transpose(
                                    g["pb_t"][:, rel * 128:rel * 128 + 128],
                                    pbuf[:, rel * 128:rel * 128 + 128], ident[:]),
                                    inc="tc" if rel == W // 128 - 1 else None)
                            P.war["a1_p"] = {"tc": ntr}
                            P.wait("vector", "tc", ntr)
                            P.waitk("vector", "a1_ptb")
                            npt = P.op("vector", lambda v, W=W: v.tensor_copy(ptb[:, 0:W], g["pb_t"][:, 0:W]), inc="vc")
                            P.war["pbt"] = {"vc": npt}
                            P.wait("tensor", "vc", npt)
                            P.waitk("tensor", "pb2")
                            for rel in range(W // 128):
                                vcol = (j + lo + rel) * 1024 + hh * 64
                                npv = P.op("tensor", lambda t, rel=rel, vcol=vcol, r=r, W=W: t.matmul(
                                    g["pb"][2][0:64, 0:128],
                                    vbuf[r][:, vcol:vcol + 64],
                                    ptb[:, rel * 128:rel * 128 + 128],
                                    start=(rel == 0), stop=(rel == W // 128 - 1)),
                                    inc="tc" if rel == W // 128 - 1 else None)
                            P.war["a1_ptb"] = {"tc": npv}
                            last_v = npv
                            P.wait("vector", "tc", npv)
                            nctx = P.op("vector", lambda v, hb=hb, hc=hc, j=j: v.tensor_copy(
                                ctx[hb:hb + 64, hc * TT + j * 128: hc * TT + j * 128 + 128],
                                g["pb"][2][0:64, 0:128]), inc="vc")
                            P.war["pb2"] = {"vc": nctx}
                            ctx_done.append(nctx)
                    ctx_all = max(ctx_done)
                    for oc in range(NCH):
                        bank = g["pb"][oc % 2]
                        P.waitk("tensor", f"pb{oc % 2}")
                        P.wait("tensor", "vc", ctx_all)
                        for ic in range(NCH):
                            P.op("tensor", lambda t, ic=ic, oc=oc, bank=bank: t.matmul(
                                bank[:, 0:TT], outw[:, ic * D + oc * 128: ic * D + oc * 128 + 128],
                                ctx[:, ic * TT:(ic + 1) * TT],
                                start=(ic == 0), stop=(ic == NCH - 1)),
                                inc="tc" if ic == NCH - 1 else None)
                        nmm = P.cnt["tc"]
                        ot = outt[oc % 2]
                        P.wait("vector", "tc", nmm)
                        P.wait("vector", f"qd{12 + r}", nna)
                        P.waitk("vector", f"a1_o{oc % 2}")
                        no = P.op("vector", lambda v, oc=oc, r=r, ot=ot, bank=bank: v.tensor_tensor(
                            out=ot[:, 0:TT], in0=bank[:, 0:TT],
                            in1=nabuf[r][:, oc * TT:(oc + 1) * TT], op=ALU.add), inc="vc")
                        P.war[f"pb{oc % 2}"] = {"vc": no}
                        P.mark(f"a1_na{r}", "vc", no)
                        P.wait("sync", "vc", no)
                        nst = P.op("sync", lambda sy, oc=oc, t0=t0, ot=ot: sy.dma_start(
                            out=s_mid[oc, :, t0:t0 + TT], in_=ot[:, 0:TT]), inc=f"qd{14 + oc % 2}", inc_by=16)
                        P.war[f"a1_o{oc % 2}"] = {f"qd{14 + oc % 2}": nst}
                    P.mark(f"a1_q{r}", "tc", last_q)
                    P.mark(f"a1_k{r}", "tc", last_q)
                    P.mark(f"a1_v{r}", "tc", last_v)
            for sname in ("qd14", "qd15"):
                P.wait("sync", sname, P.cnt[sname])
                P.wait("gpsimd", sname, P.cnt[sname])
            P.barrier()

        with contextlib.ExitStack() as stk:
            w1 = self.sbuf(stk, "a2_w1", [128, NCH * DH], BF16)
            w2 = self.sbuf(stk, "a2_w2", [128, NCH_H * D], BF16)
            b1c = self.sbuf(stk, "a2_b1", [128, NCH_H], F32)
            b2c = self.sbuf(stk, "a2_b2", [128, NCH], F32)
            midb = [self.sbuf(stk, "a2_mb0", [128, NCH * TT], BF16)]
            midf = [self.sbuf(stk, "a2_mf0", [128, TT], F32)]
            x0f = [self.sbuf(stk, "a2_x00", [128, TT], F32)]
            Ab = self.sbuf(stk, "a2_A", [128, NCH_H * TT], BF16)
            outb = [self.sbuf(stk, "a2_o0", [128, TT], F32)]
            for ic in range(NCH):
                P.op("gpsimd", lambda gg, ic=ic: gg.dma_start(
                    out=w1[:, ic * DH:(ic + 1) * DH],
                    in_=self.inp["aw1T"][ic * 128:(ic + 1) * 128, :]), inc="qg2", inc_by=16)
            for ic in range(NCH_H):
                P.op("gpsimd", lambda gg, ic=ic: gg.dma_start(
                    out=w2[:, ic * D:(ic + 1) * D],
                    in_=self.inp["aw2T"][ic * 128:(ic + 1) * 128, :]), inc="qg2", inc_by=16)
            P.op("sync", lambda sy: sy.dma_start(out=b1c[:], in_=self.inp["ab1c"][:]), inc="qc", inc_by=16)
            P.op("sync", lambda sy: sy.dma_start(out=b2c[:], in_=self.inp["ab2c"][:]), inc="qc", inc_by=16)
            P.wait("tensor", "qg2", P.cnt["qg2"])
            P.wait("scalar", "qc", P.cnt["qc"])
            P.wait("vector", "qc", P.cnt["qc"])

            for b in range(BPC):
                for tile in range(S // TT):
                    t0 = b * S + tile * TT
                    r = (b * (S // TT) + tile) % 2
                    P.waitk("gpsimd", "a2_mb0")
                    nmb = P.op("gpsimd", lambda gg, t0=t0: gg.dma_start(
                        out=midb[0].rearrange("p (c t) -> p c t", t=TT),
                        in_=s_mid[:, :, t0:t0 + TT].rearrange("c p t -> p c t")), inc="qg3", inc_by=16)
                    P.wait("tensor", "qg3", nmb)
                    gel = []
                    for oc in range(NCH_H):
                        bank = g["pb"][oc % 2]
                        P.waitk("tensor", f"pb{oc % 2}")
                        for ic in range(NCH):
                            P.op("tensor", lambda t, ic=ic, oc=oc, r=r, bank=bank: t.matmul(
                                bank[:, 0:TT], w1[:, ic * DH + oc * 128: ic * DH + oc * 128 + 128],
                                midb[0][:, ic * TT:(ic + 1) * TT],
                                start=(ic == 0), stop=(ic == NCH - 1)),
                                inc="tc" if ic == NCH - 1 else None)
                        nmm = P.cnt["tc"]
                        P.wait("scalar", "tc", nmm)
                        ne = P.op("scalar", lambda s, oc=oc, bank=bank: s.activation(
                            out=Ab[:, oc * TT:(oc + 1) * TT], in_=bank[:, 0:TT], func=AF.Gelu,
                            bias=b1c[:, oc:oc + 1], scale=1.0), inc="sc")
                        P.war[f"pb{oc % 2}"] = {"sc": ne}
                        gel.append(ne)
                    P.mark("a2_mb0", "tc", P.cnt["tc"])
                    a_all = max(gel)
                    for oc in range(NCH):
                        bank = g["pb"][oc % 2]
                        P.waitk("tensor", f"pb{oc % 2}")
                        P.wait("tensor", "sc", a_all)
                        P.waitk("sync", "a2_mf0")
                        nmf = P.op("sync", lambda sy, oc=oc, t0=t0: sy.dma_start(
                            out=midf[0][:, 0:TT], in_=s_mid[oc, :, t0:t0 + TT]), inc="qd8", inc_by=16)
                        P.waitk("sync", "a2_x00")
                        nx0 = P.op("sync", lambda sy, oc=oc, t0=t0: sy.dma_start(
                            out=x0f[0][:, 0:TT], in_=x_src[oc, :, t0:t0 + TT]), inc="qd10", inc_by=16)
                        for ic in range(NCH_H):
                            P.op("tensor", lambda t, ic=ic, oc=oc, bank=bank: t.matmul(
                                bank[:, 0:TT], w2[:, ic * D + oc * 128: ic * D + oc * 128 + 128],
                                Ab[:, ic * TT:(ic + 1) * TT],
                                start=(ic == 0), stop=(ic == NCH_H - 1)),
                                inc="tc" if ic == NCH_H - 1 else None)
                        nmm = P.cnt["tc"]
                        P.wait("vector", "tc", nmm)
                        P.wait("vector", "qd8", nmf)
                        P.wait("vector", "qd10", nx0)
                        P.waitk("vector", "a2_tmp")
                        ob = outb[0]
                        P.waitk("vector", "a2_ob0")
                        nt = P.op("vector", lambda v, oc=oc, bank=bank, ob=ob: v.scalar_tensor_tensor(
                            out=ob[:, 0:TT], in0=bank[:, 0:TT], scalar=b2c[:, oc:oc + 1],
                            in1=midf[0][:, 0:TT], op0=ALU.add, op1=ALU.add), inc="vc")
                        P.war[f"pb{oc % 2}"] = {"vc": nt}
                        P.mark("a2_mf0", "vc", nt)
                        P.wait("vector", "vc", nt)
                        no = P.op("vector", lambda v, ob=ob: v.tensor_tensor(
                            out=ob[:, 0:TT], in0=ob[:, 0:TT], in1=x0f[0][:, 0:TT], op=ALU.add), inc="vc")
                        P.mark("a2_x00", "vc", no)
                        P.wait("sync", "vc", no)
                        nst = P.op("sync", lambda sy, oc=oc, t0=t0, ob=ob: sy.dma_start(
                            out=x_dst[oc, :, t0:t0 + TT], in_=ob[:, 0:TT]), inc="qd14", inc_by=16)
                        P.war["a2_ob0"] = {"qd14": nst}
            for sname in ("qd14", "qd15"):
                P.wait("sync", sname, P.cnt[sname])
                P.wait("gpsimd", sname, P.cnt[sname])
            P.barrier()


    def _c_proj(self, P, g, l, b, tb, Q, K, Vx, ones_all, bqe, que, bkc, bvr16, l1w, l1b, ident):
        TT = 512
        NTQ = S // TT
        s_h = self.dram["s_h"]
        with contextlib.ExitStack() as pk:
            xn = [self.sbuf(pk, "cp_xn0", [128, NCH * TT], BF16)] * 2
            wsl = [self.sbuf(pk, f"cp_w{i}", [128, NCH * TT], BF16) for i in range(6)]
            wload = {}
            WMATS = [(f"bWqT{l}", 0), (f"bWqT{l}", 1), (f"bWkT{l}", 0), (f"bWkT{l}", 1),
                     (f"bWvT{l}", 0), (f"bWvT{l}", 1)]
            for i in range(6):
                mat, hf = WMATS[i]
                P.waitk("gpsimd", f"cp_w{i}")
                n = P.op("gpsimd", lambda gg, mat=mat, hf=hf, i=i: gg.dma_start(
                    out=wsl[i][:], in_=self.inp[mat][hf]),
                    inc=f"qg{i}", inc_by=16)
                wload[i] = (f"qg{i}", n)
            q0_ready = None
            for tile in range(NTQ):
                t0 = tb + tile * TT
                r = 0
                P.waitk("vector", "cp_xn0")
                xsl = [xn[r][:, c * TT:(c + 1) * TT] for c in range(NCH)]
                outs = self.emit_ln_tile(P, g, s_h, t0, TT, xsl, 1e-6, gc=l1w, bc=l1b)
                xn_ready = outs[-1]
                for wi, (dstbuf, biascol, scale) in enumerate((
                        (Q, bqe, 0.125), (K, bkc, 1.0))):
                    for hf in range(2):
                        si = wi * 2 + hf
                        sname, nld = wload[si]
                        buf = si
                        for oc4 in range(4):
                            oc = hf * 4 + oc4
                            bank = g["pb"][oc4 % 2]
                            P.waitk("tensor", f"pb{oc4 % 2}")
                            P.wait("tensor", sname, nld)
                            P.wait("tensor", "vc", xn_ready)
                            for ic in range(NCH):
                                P.op("tensor", lambda t, ic=ic, oc4=oc4, buf=buf, r=r, bank=bank: t.matmul(
                                    bank[:, 0:TT],
                                    wsl[buf][:, ic * TT + oc4 * 128: ic * TT + oc4 * 128 + 128],
                                    xn[r][:, ic * TT:(ic + 1) * TT],
                                    start=(ic == 0), stop=(ic == NCH - 1)),
                                    inc="tc" if ic == NCH - 1 else None)
                            nmm = P.cnt["tc"]
                            P.wait("scalar", "tc", nmm)
                            P.waitk("scalar", "qkevac")
                            ne = P.op("scalar", lambda s, dstbuf=dstbuf, oc=oc, tile=tile, bank=bank,
                                      biascol=biascol, scale=scale: s.activation(
                                out=dstbuf[:, oc * S + tile * TT: oc * S + tile * TT + TT],
                                in_=bank[:, 0:TT], func=AF.Identity,
                                bias=biascol[:, oc:oc + 1], scale=scale), inc="sc")
                            P.war[f"pb{oc4 % 2}"] = {"sc": ne}
                for hf in range(2):
                    si = 4 + hf
                    sname, nld = wload[si]
                    buf = si
                    for ts in range(TT // 128):
                        kc = (tile * TT + ts * 128) // 128
                        bank = g["pb"][ts % 2]
                        P.waitk("tensor", f"pb{ts % 2}")
                        P.wait("tensor", sname, nld)
                        P.wait("tensor", "vc", xn_ready)
                        P.wait("tensor", "vc", ones_all)
                        for ic in range(NCH):
                            P.op("tensor", lambda t, ic=ic, ts=ts, buf=buf, r=r, bank=bank: t.matmul(
                                bank[:, 0:TT],
                                xn[r][:, ic * TT + ts * 128: ic * TT + ts * 128 + 128],
                                wsl[buf][:, ic * TT:(ic + 1) * TT],
                                start=(ic == 0), stop=False),
                                inc=None)
                        nmm = P.op("tensor", lambda t, hf=hf, bank=bank: t.matmul(
                            bank[:, 0:TT], g["ones_row"][:],
                            bvr16[0:1, hf * TT:(hf + 1) * TT],
                            start=False, stop=True), inc="tc")
                        P.wait("scalar", "tc", nmm)
                        P.waitk("scalar", "qkevac")
                        for hh2 in range(8):
                            hh = hf * 8 + hh2
                            ne = P.op("scalar", lambda s, kc=kc, hh=hh, hh2=hh2, bank=bank: s.activation(
                                out=Vx[:, kc * (H * 64) + hh * 64: kc * (H * 64) + hh * 64 + 64],
                                in_=bank[:, hh2 * 64: hh2 * 64 + 64], func=AF.Identity,
                                bias=0.0, scale=1.0), inc="sc")
                        P.war[f"pb{ts % 2}"] = {"sc": ne}
                P.mark("cp_xn0", "tc", P.cnt["tc"])
                if tile == 0:
                    q0_ready = P.cnt["sc"]
            for i in range(6):
                P.mark(f"cp_w{i}", "tc", P.cnt["tc"])
            P.war["qkevac"] = {}
        return q0_ready
        P.barrier()



    def _c_t12(self, P, g, tq2, war_sc, Q, que, t12):
        TT = 512
        T1b, T2b, qut, trg_s, trg_c = t12
        st_loc = tq2 * TT
        P.waitk("sync", "ca_ts")
        nts = P.op("sync", lambda sy, st_loc=st_loc, trg_s=trg_s: sy.dma_start(
            out=trg_s[:], in_=self.inp["trig_s"][:, st_loc:st_loc + TT]), inc="qd10", inc_by=16)
        ntc2 = P.op("sync", lambda sy, st_loc=st_loc, trg_c=trg_c: sy.dma_start(
            out=trg_c[:], in_=self.inp["trig_c"][:, st_loc:st_loc + TT]), inc="qd11", inc_by=16)
        P.wait("gpsimd", "qd10", nts)
        P.wait("gpsimd", "qd11", ntc2)
        if war_sc:
            P.wait("gpsimd", "sc", war_sc)
        for c in range(NCH):
            P.waitk("gpsimd", "ca_T12")
            P.waitk("gpsimd", "ca_qu")
            nqu = P.op("gpsimd", lambda gg, c=c, tq2=tq2, Q=Q, qut=qut, que=que: gg.tensor_scalar(
                out=qut[:, 0:TT], in0=Q[:, c * S + tq2 * TT: c * S + tq2 * TT + TT],
                scalar1=que[:, c:c + 1], scalar2=None,
                op0=ALU.add, op1=ALU.bypass), inc="gc")
            P.wait("gpsimd", "gc", nqu)
            n1 = P.op("gpsimd", lambda gg, c=c, T1b=T1b, qut=qut, trg_s=trg_s: gg.tensor_tensor(
                out=T1b[:, c * TT:(c + 1) * TT], in0=qut[:, 0:TT],
                in1=trg_s[:, 0:TT], op=ALU.mult), inc="gc")
            n2 = P.op("gpsimd", lambda gg, c=c, T2b=T2b, qut=qut, trg_c=trg_c: gg.tensor_tensor(
                out=T2b[:, c * TT:(c + 1) * TT], in0=qut[:, 0:TT],
                in1=trg_c[:, 0:TT], op=ALU.mult), inc="gc")
            P.war["ca_qu"] = {"gc": n2}
        P.war["ca_ts"] = {"gc": P.cnt["gc"]}
        return P.cnt["gc"]

    def _c_attn(self, P, g, l, b, tb, Q, K, Vx, ident, maskb, pswap, que, boc, s_h, t12, t12_ready0):
        T1b, T2b, qut, trg_s, trg_c = t12
        TT = 512
        NTQ = S // TT
        NQT = S // 128
        with contextlib.ExitStack() as ak:
            kpos = self.sbuf(ak, "ca_kp", [128, S], BF16)
            kpos2 = self.sbuf(ak, "ca_kp2", [128, S], BF16)
            maskb = self.sbuf(ak, "ca_mb", [128, 4 * 512], BF16)
            wo = self.sbuf(ak, "ca_wo", [128, NCH * TT], BF16)
            s_s = [self.sbuf(ak, "ca_ss0", [128, S], F32)] * 2
            pb_ = [self.sbuf(ak, f"ca_p{i}", [128, S], BF16) for i in range(2)]
            ptb = self.sbuf(ak, "ca_pt", [128, S], BF16)
            stats = self.sbuf(ak, "ca_st", [128, 24], F32)
            ctx = self.sbuf(ak, "ca_ctx", [128, NCH * TT], BF16)
            hres = [self.sbuf(ak, "ca_hr0", [128, TT], F32)] * 2
            hout = [self.sbuf(ak, "ca_ho0", [128, TT], F32)] * 2
            P.waitk("sync", "ca_kpos")
            nkp = P.op("sync", lambda sy: sy.dma_start(out=kpos[:], in_=self.inp["kposdup"][:]), inc="qd12", inc_by=16)
            nkp2 = P.op("sync", lambda sy: sy.dma_start(out=kpos2[:], in_=self.inp["kposdup2"][:]), inc="qd12", inc_by=16)
            nmb = P.op("sync", lambda sy, maskb=maskb: sy.dma_start(out=maskb[:], in_=self.inp["mask_b"][:]), inc="qd12", inc_by=16)
            P.wait("tensor", "qd12", nmb)
            P.mark("ca_kpos", "qd12", nmb)

            pend = []
            ctx_done = {}
            def phase2(unit):
                hh, qt, ridx = unit
                hb = (hh % 2) * 64
                nk = qt + 1
                pbuf = pb_[ridx]
                P.wait("tensor", "vc", exp_done[(hh, qt)])
                P.waitk("vector", "ca_ptb")
                ntr = None
                for g0 in range(0, nk, 8):
                    gend = min(g0 + 8, nk)
                    P.waitk("tensor", "pbt")
                    for kc in range(g0, gend):
                        ntr = P.op("tensor", lambda t, kc=kc, g0=g0, pbuf=pbuf: t.transpose(
                            g["pb_t"][:, (kc - g0) * 128:(kc - g0) * 128 + 128],
                            pbuf[:, kc * 128:kc * 128 + 128], ident[:]),
                            inc="tc" if kc == gend - 1 else None)
                    W = (gend - g0) * 128
                    P.wait("vector", "tc", ntr)
                    npt = P.op("vector", lambda v, g0=g0, W=W: v.tensor_copy(
                        ptb[:, g0 * 128: g0 * 128 + W], g["pb_t"][:, 0:W]), inc="vc")
                    P.war["pbt"] = {"vc": npt}
                npt_last = P.cnt["vc"]
                P.war[f"ca_p2_{ridx}"] = {"tc": ntr}
                P.wait("tensor", "vc", npt_last)
                for kc in range(nk):
                    kcg = kc
                    npv = P.op("tensor", lambda t, kcg=kcg, hh=hh, qt=qt, kc=kc: t.matmul(
                        g["pb"][2][0:64, (qt % 4) * 128:(qt % 4) * 128 + 128],
                        Vx[:, kcg * (H * 64) + hh * 64: kcg * (H * 64) + hh * 64 + 64],
                        ptb[:, kc * 128:kc * 128 + 128],
                        start=(kc == 0), stop=(kc == nk - 1)),
                        inc="tc" if kc == nk - 1 else None)
                P.war["ca_ptb"] = {"tc": npv}
                ctx_done[unit] = npv
                if qt % 4 == 3:
                    hc = hh // 2
                    P.wait("scalar", "tc", npv)
                    P.waitk("scalar", "ca_ctxw")
                    nctx = P.op("scalar", lambda s, hb=hb, hc=hc: s.activation(
                        out=ctx[hb:hb + 64, hc * TT:(hc + 1) * TT],
                        in0=0 if False else g["pb"][2][0:64, 0:TT], in_=g["pb"][2][0:64, 0:TT],
                        func=AF.Identity, bias=0.0, scale=1.0) if False else s.activation(
                        out=ctx[hb:hb + 64, hc * TT:(hc + 1) * TT],
                        in_=g["pb"][2][0:64, 0:TT],
                        func=AF.Identity, bias=0.0, scale=1.0), inc="sc")
                    P.war["pb2"] = {"sc": nctx}
                    ctx_done[unit] = ("sc", nctx)

            exp_done = {}
            for tq in range(NTQ):
                P.waitk("gpsimd", "ca_woL")
                nwo0 = P.op("gpsimd", lambda gg, l=l: gg.dma_start(
                    out=wo[:], in_=self.inp[f"bWoT{l}"][0]),
                    inc="qg7", inc_by=16)
                wo_load = {0: ("qg7", nwo0)}
                if tq == 0:
                    t12_ready = t12_ready0
                qr_all = t12_ready
                for hh in range(H):
                    hb = (hh % 2) * 64
                    hc = hh // 2
                    for qq in range(4):
                        qt = tq * 4 + qq
                        uidx = hh * 4 + qq
                        ridx = uidx % 2
                        unit = (hh, qt, ridx)
                        nk5 = qt // 4 + 1
                        par = uidx % 2
                        so = 12 * ridx
                        banks = (g["pb"][0], g["pb"][1]) if par == 0 else (g["pb"][4], g["pb"][5])
                        bnames = ("pb0", "pb1") if par == 0 else ("pb4", "pb5")
                        ev = []
                        for kc5 in range(nk5):
                            W = min(512, qt * 128 + 128 - kc5 * 512)
                            bank = banks[kc5 % 2]
                            bname = bnames[kc5 % 2]
                            P.waitk("tensor", bname)
                            P.wait("tensor", "gc", qr_all)
                            last_scores = P.cnt["tc"] + 1
                            P.op("tensor", lambda t, hb=hb, hc=hc, qt=qt, kc5=kc5, W=W, bank=bank: t.matmul(
                                bank[:, 0:W],
                                Q[hb:hb + 64, hc * S + qt * 128: hc * S + qt * 128 + 128],
                                K[hb:hb + 64, hc * S + kc5 * 512: hc * S + kc5 * 512 + W],
                                start=True, stop=False, tile_position=(hb, 0)))
                            P.op("tensor", lambda t, hb=hb, hc=hc, qq=qq, kc5=kc5, W=W, bank=bank: t.matmul(
                                bank[:, 0:W],
                                T1b[hb:hb + 64, hc * TT + qq * 128: hc * TT + qq * 128 + 128],
                                kpos[hb:hb + 64, kc5 * 512: kc5 * 512 + W],
                                start=False, stop=False, tile_position=(hb, 0)))
                            P.op("tensor", lambda t, hb=hb, hc=hc, qq=qq, kc5=kc5, W=W, bank=bank, nk5=nk5: t.matmul(
                                bank[:, 0:W],
                                T2b[hb:hb + 64, hc * TT + qq * 128: hc * TT + qq * 128 + 128],
                                kpos2[hb:hb + 64, kc5 * 512: kc5 * 512 + W],
                                start=False, stop=(kc5 != nk5 - 1), tile_position=(hb, 0)),
                                inc="tc" if kc5 != nk5 - 1 else None)
                            if kc5 == nk5 - 1:
                                P.op("tensor", lambda t, qq=qq, W=W, bank=bank: t.matmul(
                                    bank[:, 0:W], ident[:],
                                    maskb[:, qq * 512: qq * 512 + W],
                                    start=False, stop=True), inc="tc")
                            nmm = P.cnt["tc"]
                            P.wait("vector", "tc", nmm)
                            if kc5 == 0:
                                P.waitk("vector", "ca_ss0")
                            nev = P.op("vector", lambda v, kc5=kc5, W=W, bank=bank, ridx=ridx, so=so: v.tensor_scalar(
                                out=s_s[ridx][:, kc5 * 512: kc5 * 512 + W], in0=bank[:, 0:W],
                                scalar1=0.0, scalar2=None, op0=ALU.bypass, op1=ALU.max,
                                accum_out=stats[:, so + kc5:so + kc5 + 1]), inc="vc")
                            P.war[bname] = {"vc": nev}
                            ev.append(nev)
                        P.wait("vector", "vc", ev[-1])
                        nbias = P.op("vector", lambda v, nk5=nk5, so=so: v.tensor_scalar(
                            out=stats[:, so + 8:so + 8 + nk5], in0=stats[:, so:so + nk5],
                            scalar1=-1.0, scalar2=None,
                            op0=ALU.mult, op1=ALU.min, accum_out=stats[:, so + 7:so + 8]), inc="vc")
                        if len(pend) >= 2:
                            phase2(pend.pop(0))
                        pend.append(unit)
                        Wtot = qt * 128 + 128
                        P.wait("scalar", "vc", nbias)
                        P.waitk("scalar", f"ca_p2_{ridx}")
                        nexp = P.op("scalar", lambda s, Wtot=Wtot, ridx=ridx, so=so: s.activation(
                            out=pb_[ridx][:, 0:Wtot], in_=s_s[ridx][:, 0:Wtot], func=AF.Exp,
                            bias=stats[:, so + 7:so + 8], scale=1.0,
                            accum_out=stats[:, so + 6:so + 7]), inc="sc")
                        P.war["ca_ss0"] = {"sc": nexp}
                        P.wait("vector", "sc", nexp)
                        nrs = P.op("vector", lambda v, so=so: v.reciprocal(
                            stats[:, so + 5:so + 6], stats[:, so + 6:so + 7]), inc="vc")
                        P.wait("vector", "vc", nrs)
                        npn = P.op("vector", lambda v, Wtot=Wtot, ridx=ridx, so=so: v.tensor_scalar_mul(
                            pb_[ridx][:, 0:Wtot], pb_[ridx][:, 0:Wtot], stats[:, so + 5:so + 6]), inc="vc")
                        exp_done[(hh, qt)] = npn
                while pend:
                    phase2(pend.pop(0))
                t0 = tb + tq * TT
                ctx_all = P.cnt["sc"]
                if tq + 1 < NTQ:
                    P.mark("ca_T12", "tc", last_scores)
                    t12_ready = self._c_t12(P, g, tq + 1, 0, Q, que, t12)
                for hf in range(2):
                    if hf == 0:
                        sname, nwo = wo_load[0]
                        P.wait("tensor", sname, nwo)
                    else:
                        P.wait("gpsimd", "tc", n_half0)
                        nwo = P.op("gpsimd", lambda gg, l=l: gg.dma_start(
                            out=wo[:], in_=self.inp[f"bWoT{l}"][1]),
                            inc="qg8", inc_by=16)
                        P.wait("tensor", "qg8", nwo)
                    for oc4 in range(4):
                        oc = hf * 4 + oc4
                        bank = g["pb"][oc % 2]
                        P.waitk("tensor", f"pb{oc % 2}")
                        P.wait("tensor", "sc", ctx_all)
                        for ic in range(NCH):
                            P.op("tensor", lambda t, ic=ic, oc4=oc4, bank=bank: t.matmul(
                                bank[:, 0:TT], wo[:, ic * TT + oc4 * 128: ic * TT + oc4 * 128 + 128],
                                ctx[:, ic * TT:(ic + 1) * TT],
                                start=(ic == 0), stop=(ic == NCH - 1)),
                                inc="tc" if ic == NCH - 1 else None)
                        nmm = P.cnt["tc"]
                        rr = 0
                        P.waitk("sync", "ca_hr0")
                        nhl = P.op("sync", lambda sy, oc=oc, t0=t0: sy.dma_start(
                            out=hres[0][:, 0:TT], in_=s_h[oc, :, t0:t0 + TT]), inc="qd2", inc_by=16)
                        P.wait("vector", "tc", nmm)
                        P.wait("vector", "qd2", nhl)
                        P.waitk("vector", "ca_ho0")
                        nho = P.op("vector", lambda v, oc=oc, rr=rr, bank=bank: v.scalar_tensor_tensor(
                            out=hout[0][:, 0:TT], in0=bank[:, 0:TT], scalar=boc[:, oc:oc + 1],
                            in1=hres[rr][:, 0:TT], op0=ALU.add, op1=ALU.add), inc="vc")
                        P.war[f"pb{oc % 2}"] = {"vc": nho}
                        P.war["ca_hr0"] = {"vc": nho}
                        P.wait("sync", "vc", nho)
                        nst = P.op("sync", lambda sy, oc=oc, t0=t0: sy.dma_start(
                            out=s_h[oc, :, t0:t0 + TT], in_=hout[0][:, 0:TT]), inc="qd4", inc_by=16)
                        P.war["ca_ho0"] = {"qd4": nst}
                    if hf == 0:
                        n_half0 = P.cnt["tc"]
                    else:
                        P.war["ca_woL"] = {"tc": P.cnt["tc"]}
                P.mark("ca_ctxw", "tc", P.cnt["tc"])
                ctx_done = {}
            for sname in ("qd4",):
                P.wait("sync", sname, P.cnt[sname])
                P.wait("gpsimd", sname, P.cnt[sname])
            P.barrier()



    def _c_ffn(self, P, g, l, b, tb, b1c, b2c, l2w, l2b, s_h, x_src, x_dst):
        TT = 512
        NTQ = S // TT
        with contextlib.ExitStack() as fk:
            fn = [self.sbuf(fk, f"cf_fn{i}", [128, NCH * TT], BF16) for i in range(3)]
            mid = self.sbuf(fk, "cf_mid", [128, NCH_H * TT], BF16)
            w1sl = [self.sbuf(fk, f"cf_w1{i}", [128, NCH * TT], BF16) for i in range(2)]
            w2sl = [self.sbuf(fk, f"cf_w2{i}", [128, NCH_H * 128], BF16) for i in range(2)]
            hres2 = [self.sbuf(fk, f"cf_hr{i}", [128, TT], F32) for i in range(2)]
            hout2 = self.sbuf(fk, "cf_ho", [128, TT], F32)
            x2r = self.sbuf(fk, "cf_x2", [128, TT], F32)
            w1load = {}
            w2load = {}
            def w1slab(i):
                buf = i % 2
                P.waitk("gpsimd", f"cf_w1{buf}")
                n = P.op("gpsimd", lambda gg, i=i, buf=buf, l=l: gg.dma_start(
                    out=w1sl[buf][:], in_=self.inp[f"bW1T{l}"][i]),
                    inc=f"qg{3 + buf}", inc_by=16)
                w1load[i] = (f"qg{3 + buf}", n)
            def w2slab(i):
                buf = i % 2
                P.waitk("gpsimd", f"cf_w2{buf}")
                n = P.op("gpsimd", lambda gg, i=i, buf=buf, l=l: gg.dma_start(
                    out=w2sl[buf][:], in_=self.inp[f"bW2T{l}"][i]),
                    inc=f"qg{5 + buf}", inc_by=16)
                w2load[i] = (f"qg{5 + buf}", n)
            w1slab(0); w1slab(1)
            w2slab(0); w2slab(1)
            ln_done_f = {}
            def emit_ln_f(tile2):
                slot = tile2 % 3
                P.waitk("vector", f"cf_fn{slot}")
                fsl = [fn[slot][:, c * TT:(c + 1) * TT] for c in range(NCH)]
                outs = self.emit_ln_tile(P, g, s_h, tb + tile2 * TT, TT, fsl, 1e-6, gc=l2w, bc=l2b)
                ln_done_f[tile2] = outs[-1]
            for tile in range(NTQ):
                t0 = tb + tile * TT
                r = tile % 3
                if tile == 0:
                    emit_ln_f(0)
                fn_ready = ln_done_f[tile]
                for i8 in range(8):
                    sname, nld = w1load[i8 % 8 if tile == 0 else i8]
                    buf = i8 % 2
                    for oc4 in range(4):
                        oc = i8 * 4 + oc4
                        bank = g["pb"][oc4 % 2]
                        P.waitk("tensor", f"pb{oc4 % 2}")
                        P.wait("tensor", sname, nld)
                        P.wait("tensor", "vc", fn_ready)
                        for ic in range(NCH):
                            P.op("tensor", lambda t, ic=ic, oc4=oc4, buf=buf, r=r, bank=bank: t.matmul(
                                bank[:, 0:TT],
                                w1sl[buf][:, ic * TT + oc4 * 128: ic * TT + oc4 * 128 + 128],
                                fn[r][:, ic * TT:(ic + 1) * TT],
                                start=(ic == 0), stop=(ic == NCH - 1)),
                                inc="tc" if ic == NCH - 1 else None)
                        nmm = P.cnt["tc"]
                        P.wait("scalar", "tc", nmm)
                        ne = P.op("scalar", lambda s, oc=oc, bank=bank: s.activation(
                            out=mid[:, oc * TT:(oc + 1) * TT], in_=bank[:, 0:TT],
                            func=AF.Gelu, bias=b1c[:, oc:oc + 1], scale=1.0), inc="sc")
                        P.war[f"pb{oc4 % 2}"] = {"sc": ne}
                    P.mark(f"cf_w1{buf}", "tc", P.cnt["tc"])
                    nxt = i8 + 2
                    if nxt < 8:
                        w1slab(nxt)
                    elif tile < NTQ - 1:
                        w1load[nxt - 8] = None
                if tile < NTQ - 1:
                    w1slab(0); w1slab(1)
                mid_all = P.cnt["sc"]
                P.mark(f"cf_fn{r}", "tc", P.cnt["tc"])
                if tile + 1 < NTQ:
                    emit_ln_f(tile + 1)
                for i8 in range(8):
                    oc = i8
                    sname, nld = w2load[i8]
                    buf = i8 % 2
                    if True:
                        bank = g["pb"][i8 % 2]
                        P.waitk("tensor", f"pb{i8 % 2}")
                        P.wait("tensor", sname, nld)
                        P.wait("tensor", "sc", mid_all)
                        for ic in range(NCH_H):
                            P.op("tensor", lambda t, ic=ic, buf=buf, bank=bank: t.matmul(
                                bank[:, 0:TT],
                                w2sl[buf][:, ic * 128: ic * 128 + 128],
                                mid[:, ic * TT:(ic + 1) * TT],
                                start=(ic == 0), stop=(ic == NCH_H - 1)),
                                inc="tc" if ic == NCH_H - 1 else None)
                        nmm = P.cnt["tc"]
                        rr = oc % 2
                        P.waitk("sync", f"cf_hr{rr}")
                        nhl = P.op("sync", lambda sy, oc=oc, t0=t0, rr=rr: sy.dma_start(
                            out=hres2[rr][:, 0:TT], in_=s_h[oc, :, t0:t0 + TT]), inc=f"qd{6 + rr}", inc_by=16)
                        P.wait("vector", "tc", nmm)
                        P.wait("vector", f"qd{6 + rr}", nhl)
                        P.waitk("vector", "cf_ho")
                        nho = P.op("vector", lambda v, oc=oc, rr=rr, bank=bank: v.scalar_tensor_tensor(
                            out=hout2[:, 0:TT], in0=bank[:, 0:TT], scalar=b2c[:, oc:oc + 1],
                            in1=hres2[rr][:, 0:TT], op0=ALU.add, op1=ALU.add), inc="vc")
                        P.war[f"pb{i8 % 2}"] = {"vc": nho}
                        P.war[f"cf_hr{rr}"] = {"vc": nho}
                        if l == 3:
                            P.waitk("sync", "cf_x2")
                            nx2 = P.op("sync", lambda sy, oc=oc, t0=t0: sy.dma_start(
                                out=x2r[:, 0:TT], in_=x_src[oc, :, t0:t0 + TT]), inc="qd8", inc_by=16)
                            P.wait("vector", "vc", nho)
                            P.wait("vector", "qd8", nx2)
                            nho = P.op("vector", lambda v: v.tensor_tensor(
                                out=hout2[:, 0:TT], in0=hout2[:, 0:TT], in1=x2r[:, 0:TT],
                                op=ALU.add), inc="vc")
                            P.war["cf_x2"] = {"vc": nho}
                        P.war["cf_ho"] = {"vc": nho}
                        dst = x_dst if l == 3 else s_h
                        P.wait("sync", "vc", nho)
                        nst = P.op("sync", lambda sy, oc=oc, t0=t0, dst=dst: sy.dma_start(
                            out=dst[oc, :, t0:t0 + TT], in_=hout2[:, 0:TT]), inc="qd9", inc_by=16)
                        P.mark("cf_ho", "qd9", nst)
                    P.mark(f"cf_w2{buf}", "tc", P.cnt["tc"])
                    nxt = i8 + 2
                    if nxt < 8:
                        w2slab(nxt)
                if tile < NTQ - 1:
                    w2slab(0); w2slab(1)
            for sname in ("qd9",):
                P.wait("sync", sname, P.cnt[sname])
                P.wait("gpsimd", sname, P.cnt[sname])
            P.barrier()


    def emit_stage_c(self, P, g, x_src, x_dst):
        nc = self.nc
        TT = 512
        s_h = self.dram["s_h"]
        NQT = S // 128
        NTQ = S // TT

        with contextlib.ExitStack() as stk:
            ident = self.sbuf(stk, "c_id", [128, 128], BF16)
            pswap = self.sbuf(stk, "c_psw", [128, 128], BF16)
            bqe = self.sbuf(stk, "c_bqe", [128, NCH], F32)
            que = self.sbuf(stk, "c_que", [128, NCH], F32)
            bkc = self.sbuf(stk, "c_bkc", [128, NCH], F32)
            boc = self.sbuf(stk, "c_boc", [128, NCH], F32)
            b1c = self.sbuf(stk, "c_b1c", [128, NCH_H], F32)
            b2c = self.sbuf(stk, "c_b2c", [128, NCH], F32)
            l1w = self.sbuf(stk, "c_l1w", [128, NCH], F32)
            l1b = self.sbuf(stk, "c_l1b", [128, NCH], F32)
            l2w = self.sbuf(stk, "c_l2w", [128, NCH], F32)
            l2b = self.sbuf(stk, "c_l2b", [128, NCH], F32)
            ucol = self.sbuf(stk, "c_uc", [128, NCH], F32)
            bvr = self.sbuf(stk, "c_bvr", [1, D], F32)
            bvr16 = self.sbuf(stk, "c_bvr16", [1, D], BF16)
            P.op("sync", lambda sy: sy.dma_start(out=ident[:], in_=self.inp["ident"][:]), inc="qc", inc_by=16)
            P.op("sync", lambda sy: sy.dma_start(out=pswap[:], in_=self.inp["pswap"][:]), inc="qc", inc_by=16)
            nqc0 = P.cnt["qc"]
            P.wait("tensor", "qc", nqc0)
            P.wait("vector", "qc", nqc0)

            with contextlib.ExitStack() as hk:
                h0t = self.sbuf(hk, "c_h0", [128, NCH * TT], F32)
                for b in range(BPC):
                    for tile in range(NTQ):
                        t0 = b * S + tile * TT
                        haps = [h0t[:, c * TT:(c + 1) * TT] for c in range(NCH)]
                        P.waitk("vector", "c_h0")
                        outs = self.emit_ln_tile(P, g, x_src, t0, TT, haps, 1e-5, fsrc="dram")
                        stc = []
                        for c in range(NCH):
                            P.wait("sync", "vc", outs[c])
                            nst = P.op("sync", lambda sy, c=c, t0=t0: sy.dma_start(
                                out=s_h[c, :, t0:t0 + TT], in_=h0t[:, c * TT:(c + 1) * TT]),
                                inc="qd1", inc_by=16)
                            stc.append(nst)
                        P.war["c_h0"] = {"qd1": stc[-1]}
                P.wait("sync", "qd1", P.cnt["qd1"])
                P.wait("gpsimd", "qd1", P.cnt["qd1"])
                P.barrier()

            for l in range(self.c_layers):
                for n, sb in ((f"bbqc{l}", bqe), (f"buc{l}", ucol), (f"bbkc{l}", bkc),
                              (f"bboc{l}", boc), (f"bb1c{l}", b1c), (f"bb2c{l}", b2c),
                              (f"bln1wc{l}", l1w), (f"bln1bc{l}", l1b),
                              (f"bln2wc{l}", l2w), (f"bln2bc{l}", l2b)):
                    P.waitk("sync", "c_cols")
                    P.op("sync", lambda sy, n=n, sb=sb: sy.dma_start(out=sb[:], in_=self.inp[n][:]),
                         inc="qc", inc_by=16)
                P.waitk("sync", "c_cols")
                P.op("sync", lambda sy, l=l: sy.dma_start(out=bvr[:], in_=self.inp[f"bbvr{l}"][:]),
                     inc="qc", inc_by=16)
                nqc = P.cnt["qc"]
                P.wait("vector", "qc", nqc)
                nv1 = P.op("vector", lambda v: v.tensor_tensor(out=que[:], in0=bqe[:], in1=ucol[:], op=ALU.add), inc="vc")
                P.wait("vector", "vc", nv1)
                nv2 = P.op("vector", lambda v: v.tensor_scalar_mul(que[:], que[:], 0.125), inc="vc")
                nv3 = P.op("vector", lambda v: v.tensor_scalar_mul(bqe[:], bqe[:], 0.125), inc="vc")
                nv4 = P.op("vector", lambda v: v.tensor_copy(bvr16[:], bvr[:]), inc="vc")
                ncols = P.cnt["vc"]
                P.war["c_cols"] = {"vc": ncols}
                P.wait("scalar", "vc", ncols)
                P.wait("tensor", "vc", ncols)

                for b in range(BPC):
                    tb = b * S
                    qk_stack = contextlib.ExitStack()
                    Q = self.sbuf(qk_stack, "c_Q", [128, NCH * S], BF16)
                    K = self.sbuf(qk_stack, "c_K", [128, NCH * S], BF16)
                    Vx = self.sbuf(qk_stack, "c_V", [128, (S // 128) * (H * 64)], BF16)
                    T1b = self.sbuf(qk_stack, "c_T1", [128, NCH * 512], BF16)
                    T2b = self.sbuf(qk_stack, "c_T2", [128, NCH * 512], BF16)
                    qut = self.sbuf(qk_stack, "c_qu", [128, 512], BF16)
                    trg_s = self.sbuf(qk_stack, "c_ts", [128, 512], BF16)
                    trg_c = self.sbuf(qk_stack, "c_tc", [128, 512], BF16)
                    t12 = (T1b, T2b, qut, trg_s, trg_c)
                    ones_all = P.cnt["vc"]
                    q0_ready = self._c_proj(P, g, l, b, tb, Q, K, Vx, ones_all, bqe, que, bkc, bvr16, l1w, l1b, ident)
                    t12_ready0 = self._c_t12(P, g, 0, q0_ready, Q, que, t12)
                    P.barrier()
                    self._c_attn(P, g, l, b, tb, Q, K, Vx, ident, None, pswap, que, boc, s_h, t12, t12_ready0)
                    if self.c_attn_only:
                        P.mark("c_Vones", "tc", P.cnt["tc"])
                        P.mark("c_Vones", "vc", P.cnt["vc"])
                        qk_stack.close()
                        continue
                    P.mark("c_Vones", "tc", P.cnt["tc"])
                    P.mark("c_Vones", "vc", P.cnt["vc"])
                    qk_stack.close()
                    self._c_ffn(P, g, l, b, tb, b1c, b2c, l2w, l2b, s_h, x_src, x_dst)
        return

    def build(self):
        nc = self.nc
        self.declare()
        with contextlib.ExitStack() as stk:
            P = Prog(nc)
            semnames = ["tc", "vc", "sc", "gc", "qc"] + ["b_" + e for e in ENGS]
            semnames += [f"qx{i}" for i in range(NCH)]
            semnames += [f"qw{i}" for i in range(3)]
            semnames += ["qr0", "qr1", "qo0", "qo1", "qf0", "qf1"]
            semnames += [f"qd{i}" for i in range(16)]
            semnames += [f"qg{i}" for i in range(10)]
            for s in semnames:
                P.add_sem(stk, s)
            g = {}
            g["pb"] = [stk.enter_context(nc.psum_tensor(f"pb{i}", [128, 512], F32)) for i in range(7)]
            g["pb_t"] = stk.enter_context(nc.psum_tensor("pbt", [128, 1024], BF16))
            g["pb_sum"] = g["pb"][3]; g["pb_sq"] = g["pb"][4]
            g["pb_rb"] = g["pb"][5]; g["pb_mrb"] = g["pb"][6]
            g["ones_col"] = self.sbuf(stk, "g_ones_col", [128, 1], BF16)
            g["ones_f32"] = self.sbuf(stk, "g_ones_f32", [1, 128], F32)
            g["rows"] = self.sbuf(stk, "g_rows", [1, 2560], F32)
            g["lnt1f"] = self.sbuf(stk, "g_lnt1f", [128, 512], F32)
            g["rb_sb"] = self.sbuf(stk, "g_rbsb", [128, 512], F32)
            g["mrb_sb"] = self.sbuf(stk, "g_mrbsb", [128, 512], F32)
            g["lnt1"] = g["lnt1f"]
            g["ln_xf"] = [self.sbuf(stk, "g_lnxf0", [128, 512], F32)] * 2
            g["lnt2"] = self.sbuf(stk, "g_lnt2", [128, 512], BF16)
            g["ln_x"] = self.sbuf(stk, "g_lnx", [128, NCH * 512], BF16)
            g["ln_sq"] = [self.sbuf(stk, "g_lnsq0", [128, 512], BF16)] * 2

            g["ones_row"] = self.sbuf(stk, "g_ones_row", [1, 128], BF16)
            P.op("sync", lambda sy: sy.dma_start(out=g["ones_col"][:], in_=self.inp["ones_col"][:]), inc="qc", inc_by=16)
            P.op("sync", lambda sy: sy.dma_start(out=g["ones_f32"][:], in_=self.inp["ones_f32"][:]), inc="qc", inc_by=16)
            P.op("sync", lambda sy: sy.dma_start(out=g["ones_row"][:], in_=self.inp["ones_row"][:]), inc="qc", inc_by=16)

            if self.stages == ("B",):
                self.emit_ffn_ternary(P, g, "fB", self.inp["x_in"], self.y)
            elif self.stages == ("A",):
                self.emit_stage_a(P, g, self.inp["x_in"], self.y)
            elif self.stages == ("A", "B"):
                self.emit_stage_a(P, g, self.inp["x_in"], self.dram["s_x1"])
                self.emit_ffn_ternary(P, g, "fB", self.dram["s_x1"], self.y)
            elif self.stages == ("C",):
                self.emit_stage_c(P, g, self.inp["x_in"], self.y)
            elif self.stages == ("A", "B", "C", "D"):
                self.emit_stage_a(P, g, self.inp["x_in"], self.dram["s_x1"])
                self.emit_ffn_ternary(P, g, "fB", self.dram["s_x1"], self.dram["s_x2"])
                self.emit_stage_c(P, g, self.dram["s_x2"], self.dram["s_x3"])
                self.emit_ffn_ternary(P, g, "fD", self.dram["s_x3"], self.y)
            else:
                raise NotImplementedError(self.stages)

            for sname in ("qo0", "qo1", "qd14", "qd15"):
                if P.cnt.get(sname):
                    P.wait("sync", sname, P.cnt[sname])

            block = stk.enter_context(nc.Block())
            for e, meth in (("sync", block.sync), ("tensor", block.tensor),
                            ("vector", block.vector), ("scalar", block.scalar),
                            ("gpsimd", block.gpsimd)):
                def run(E, e=e):
                    for f in P.streams[e]:
                        f(E)
                meth(run)
        self.P = P
        return nc


def host_in_maps(inputs, stages=("A", "B", "C", "D"), x_override=None):
    w = inputs
    base = {}
    for n, a in CONSTS.items():
        base[n] = np.ascontiguousarray(a)
    if "A" in stages:
        base["aqkvT"] = wt_np(w["L0_attn_a_qkv_w"])
        base["aoutT"] = wt_np(w["L0_attn_a_out_w"])
        base["aw1T"] = wt_np(w["L0_attn_a_w1"])
        base["aw2T"] = wt_np(w["L0_attn_a_w2"])
        base["ab1c"] = cols_np(w["L0_attn_a_b1"])
        base["ab2c"] = cols_np(w["L0_attn_a_b2"])
        base["alngc"] = cols_np(w["L0_attn_a_ln_g"])
        base["alnbc"] = cols_np(w["L0_attn_a_ln_b"])
    if "B" in stages or "D" in stages:
        base["fw1T"] = slab_np(w["L0_ffn_w1_w"], 128)
        base["fw2T"] = slab_np(w["L0_ffn_w2_w"], 128)
        base["fh1T"] = slab_np(w["L0_ffn_h1_w"], 128)
        base["fh2T"] = slab_np(w["L0_ffn_h2_w"], 128)
        base["fw3T"] = slab_np(w["L0_ffn_w3_w"], 128)
        base["fs1c"] = cols_np(w["L0_ffn_w1_s"])
        base["fs2c"] = cols_np(w["L0_ffn_w2_s"])
        base["fhs1c"] = cols_np(w["L0_ffn_h1_s"])
        base["fhs2c"] = cols_np(w["L0_ffn_h2_s"])
        base["fs3c"] = cols_np(w["L0_ffn_w3_s"])
    if "C" in stages:
        for l in range(4):
            base[f"bWqT{l}"] = slab_np(w["L0_attn_b_Wq"][l], 512)
            base[f"bWkT{l}"] = slab_np(w["L0_attn_b_Wk"][l], 512)
            base[f"bWvT{l}"] = slab_np(w["L0_attn_b_Wv"][l], 512)
            base[f"bWoT{l}"] = slab_np(w["L0_attn_b_Wo"][l], 512)
            base[f"bbqc{l}"] = cols_np(w["L0_attn_b_bq"][l])
            base[f"bbkc{l}"] = cols_np(w["L0_attn_b_bk"][l])
            base[f"bbvr{l}"] = np.ascontiguousarray(
                np.asarray(w["L0_attn_b_bv"][l], np.float32).reshape(1, D))
            base[f"bboc{l}"] = cols_np(w["L0_attn_b_bo"][l])
            base[f"buc{l}"] = cols_np(np.asarray(w["L0_attn_b_u_bias"][l]).reshape(D))
            base[f"bW1T{l}"] = slab_np(w["L0_attn_b_W1"][l], 512)
            base[f"bW2T{l}"] = slab_np(w["L0_attn_b_W2"][l], 128)
            base[f"bb1c{l}"] = cols_np(w["L0_attn_b_b1"][l])
            base[f"bb2c{l}"] = cols_np(w["L0_attn_b_b2"][l])
            base[f"bln1wc{l}"] = cols_np(w["L0_attn_b_ln1_w"][l])
            base[f"bln1bc{l}"] = cols_np(w["L0_attn_b_ln1_b"][l])
            base[f"bln2wc{l}"] = cols_np(w["L0_attn_b_ln2_w"][l])
            base[f"bln2bc{l}"] = cols_np(w["L0_attn_b_ln2_b"][l])
    x = np.asarray(x_override if x_override is not None else w["L0_attn_a_x"], np.float32)
    maps = []
    for c in range(NCORE):
        m = dict(base)
        xs = x[c * BPC:(c + 1) * BPC]
        m["x_in"] = np.ascontiguousarray(
            xs.reshape(BPC, S, NCH, 128).transpose(2, 3, 0, 1).reshape(NCH, 128, T))
        maps.append(m)
    return maps


def gather_y(results):
    out = np.empty((NCORE * BPC, S, D), np.float32)
    for c, r in enumerate(results):
        y = r["y"]
        out[c * BPC:(c + 1) * BPC] = y.reshape(NCH, 128, BPC, S).transpose(2, 3, 0, 1).reshape(BPC, S, D)
    return out


def kernel(**inputs):
    m = MK(stages=("A", "B", "C", "D"))
    nc = m.build()
    maps = host_in_maps(inputs, stages=("A", "B", "C", "D"))
    res = run_bass_kernel_spmd(nc, maps, core_ids=list(range(NCORE)))
    return gather_y(res.results)


# version 08
# speedup vs baseline: 1.0033x; 1.0033x over previous
import contextlib
import numpy as np
import ml_dtypes
import concourse.bass as bass
import concourse.mybir as mybir
from concourse.bass_utils import run_bass_kernel_spmd

F32 = mybir.dt.float32
BF16 = mybir.dt.bfloat16
AF = mybir.ActivationFunctionType
ALU = mybir.AluOpType
AX = mybir.AxisListType
BF = ml_dtypes.bfloat16

D = 1024
NCH = 8
DH = 4096
NCH_H = 32
H = 16
HD = 64
S = 2048
BPC = 2
T = BPC * S
NCORE = 8

ENGS = ("tensor", "vector", "scalar", "sync", "gpsimd")


def make_consts():
    c = {}
    c["ident"] = np.eye(128, dtype=np.float32).astype(BF)
    P = np.zeros((128, 128), np.float32)
    for blk in range(2):
        o = blk * 64
        for m in range(32):
            P[o + m + 32, o + m] = 1.0
            P[o + m, o + m + 32] = -1.0
    c["pswap"] = P.astype(BF)
    c["ones_f32"] = np.ones((1, 128), np.float32)
    c["ones_col"] = np.ones((128, 1), BF)
    c["ones_row"] = np.ones((1, 128), BF)
    inv = 1.0 / 10000.0 ** (np.arange(0, HD, 2, dtype=np.float64) / HD)
    tpos = np.arange(S, dtype=np.float64)
    ang = tpos[None, :] * inv[(np.arange(128) % 32), None]
    c["trig_s"] = np.sin(ang).astype(np.float32).astype(BF)
    c["trig_c"] = np.cos(ang).astype(np.float32).astype(BF)
    kp = np.zeros((128, S), np.float64)
    kp[0:32] = np.cos(tpos[None, :] * inv[:, None])
    kp[32:64] = np.sin(tpos[None, :] * inv[:, None])
    kp[64:128] = kp[0:64]
    c["kposdup"] = kp.astype(np.float32).astype(BF)
    kp2 = np.zeros((128, S), np.float64)
    kp2[0:32] = -np.sin(tpos[None, :] * inv[:, None])
    kp2[32:64] = np.cos(tpos[None, :] * inv[:, None])
    kp2[64:128] = kp2[0:64]
    c["kposdup2"] = kp2.astype(np.float32).astype(BF)
    mb = np.zeros((128, 4 * 512), np.float32)
    pidx = np.arange(128)[:, None]
    cc = np.arange(512)[None, :]
    for a in range(4):
        mb[:, a * 512:(a + 1) * 512] = np.where(cc <= 128 * a + pidx, 0.0, -1e30)
    c["mask_b"] = mb.astype(BF)
    j = np.arange(384)[None, :]
    rel = (j // 128 - 1) * 128 + (j % 128) - pidx
    c["mask_a"] = np.where(np.abs(rel) <= 64, 0.0, -1e30).astype(np.float32).astype(BF)
    return c


CONSTS = make_consts()


def cols_np(v):
    v = np.asarray(v, np.float32)
    return np.ascontiguousarray(v.reshape(-1, 128).T)


def wt_np(w):
    return np.ascontiguousarray(np.asarray(w, np.float32).T)


def slab_np(w, osz):
    WT = np.asarray(w, np.float32).T
    I, O = WT.shape
    nic = I // 128
    n = O // osz
    return np.ascontiguousarray(WT.reshape(nic, 128, n, osz).transpose(2, 1, 0, 3).reshape(n, 128, nic * osz))


class Prog:
    def __init__(self, nc):
        self.nc = nc
        self.streams = {e: [] for e in ENGS}
        self.cnt = {}
        self.sem = {}
        self.iic = {e: 0 for e in ENGS}
        self.war = {}

    def add_sem(self, stk, name):
        self.sem[name] = stk.enter_context(self.nc.semaphore(name))
        self.cnt[name] = 0

    def op(self, eng, fn, inc=None, inc_by=1):
        self.iic[eng] += 1
        if inc is None:
            self.streams[eng].append(fn)
            return None
        self.cnt[inc] += inc_by
        h = self.sem[inc]
        def wrapped(E, fn=fn, h=h, n=inc_by):
            fn(E).then_inc(h, n)
        self.streams[eng].append(wrapped)
        return self.cnt[inc]

    def wait(self, eng, sem, val):
        if val is None or val <= 0:
            return
        h = self.sem[sem]
        self.streams[eng].append(lambda E, h=h, v=val: E.wait_ge(h, v))

    def waitk(self, eng, key):
        v = self.war.get(key)
        if v:
            for semname, val in v.items():
                self.wait(eng, semname, val)

    def mark(self, key, semname, val):
        d = self.war.setdefault(key, {})
        d[semname] = max(d.get(semname, 0), val)

    def cleark(self, key):
        self.war.pop(key, None)

    def barrier(self, engs=ENGS):
        vals = {}
        for e in engs:
            sname = "b_" + e
            self.cnt[sname] += 1
            vals[e] = self.cnt[sname]
            h = self.sem[sname]
            self.streams[e].append(lambda E, h=h: E.nop().then_inc(h, 1))
        for e in engs:
            for o in engs:
                if o != e:
                    self.wait(e, "b_" + o, vals[o])


class MK:
    def __init__(self, stages=("A", "B", "C", "D"), c_layers=4, c_attn_only=False):
        self.stages = stages
        self.c_layers = c_layers
        self.c_attn_only = c_attn_only
        self.nc = bass.Bass("TRN2", target_bir_lowering=False)
        self.inp = {}
        self.dram = {}

    def declare(self):
        nc = self.nc
        add = lambda n, shp, dt=F32: self.inp.__setitem__(
            n, nc.dram_tensor(n, shp, dt, kind="ExternalInput").ap())
        add("x_in", [NCH, 128, T])
        for n, a in CONSTS.items():
            dt = BF16 if a.dtype == BF else F32
            self.inp[n] = nc.dram_tensor(n, list(a.shape), dt, kind="ExternalInput").ap()
        if "A" in self.stages:
            add("aqkvT", [D, 3 * D]); add("aoutT", [D, D])
            add("aw1T", [D, DH]); add("aw2T", [DH, D])
            add("ab1c", [128, NCH_H]); add("ab2c", [128, NCH])
            add("alngc", [128, NCH]); add("alnbc", [128, NCH])
        if "B" in self.stages or "D" in self.stages:
            add("fw1T", [DH // 128, 128, NCH * 128]); add("fw2T", [DH // 128, 128, NCH * 128])
            add("fh1T", [DH // 128, 128, NCH_H * 128]); add("fh2T", [DH // 128, 128, NCH_H * 128])
            add("fw3T", [D // 128, 128, NCH_H * 128])
            add("fs1c", [128, NCH_H]); add("fs2c", [128, NCH_H])
            add("fhs1c", [128, NCH_H]); add("fhs2c", [128, NCH_H]); add("fs3c", [128, NCH])
        if "C" in self.stages:
            for l in range(4):
                add(f"bWqT{l}", [2, 128, NCH * 512]); add(f"bWkT{l}", [2, 128, NCH * 512])
                add(f"bWvT{l}", [2, 128, NCH * 512]); add(f"bWoT{l}", [2, 128, NCH * 512])
                add(f"bbqc{l}", [128, NCH]); add(f"bbkc{l}", [128, NCH])
                add(f"bbvr{l}", [1, D]); add(f"bboc{l}", [128, NCH])
                add(f"buc{l}", [128, NCH])
                add(f"bW1T{l}", [DH // 512, 128, NCH * 512]); add(f"bW2T{l}", [D // 128, 128, NCH_H * 128])
                add(f"bb1c{l}", [128, NCH_H]); add(f"bb2c{l}", [128, NCH])
                add(f"bln1wc{l}", [128, NCH]); add(f"bln1bc{l}", [128, NCH])
                add(f"bln2wc{l}", [128, NCH]); add(f"bln2bc{l}", [128, NCH])
        self.y = nc.dram_tensor("y", [NCH, 128, T], F32, kind="ExternalOutput").ap()
        names = ["s_mid", "s_x1"]
        if "A" in self.stages:
            names += ["s_na"]
        if "C" in self.stages:
            names += ["s_x2", "s_x3", "s_h"]
        if "D" in self.stages and "s_x2" not in names:
            names += ["s_x2", "s_x3"]
        for n in names:
            self.dram[n] = nc.dram_tensor(n, [NCH, 128, T], F32, kind="Internal").ap()
        if "A" in self.stages:
            for n in ("qa", "ka"):
                self.dram[n] = nc.dram_tensor(n, [NCH, 128, T], BF16, kind="Internal").ap()
            self.dram["va"] = nc.dram_tensor("va", [T // 128, 128, D], BF16, kind="Internal").ap()

    def sbuf(self, stk, name, shape, dtype):
        self._uid = getattr(self, "_uid", 0) + 1
        return stk.enter_context(self.nc.sbuf_tensor(f"{name}_u{self._uid}", shape, dtype))

    def emit_ln_tile(self, P, g, src, t0, TT, xn, eps, gc=None, bc=None, xkeep=None,
                     ready=None, fsrc=None):
        lx = g["ln_x"]; sq = g["ln_sq"]
        rows = g["rows"]
        if xkeep is not None:
            xsl = xkeep
        else:
            xsl = [lx[:, c * 512:c * 512 + TT] for c in range(NCH)]
        P.waitk("tensor", "pb_stats")
        n_sum = n_sq = 0
        if xkeep is None:
            P.waitk("gpsimd", "ln_x")
            lx2 = lx.rearrange("p (c t) -> p c t", t=512)[:, :, 0:TT]
            nq = P.op("gpsimd", lambda gg, lx2=lx2: gg.dma_start(
                out=lx2, in_=src[:, :, t0:t0 + TT].rearrange("c p t -> p c t")),
                inc="qx0", inc_by=16)
            P.wait("scalar", "qx0", nq)
            P.wait("tensor", "qx0", nq)
        for c in range(NCH):
            buf = c % 2
            if False:
                pass
            elif ready is not None and ready[c] is not None:
                P.wait("scalar", ready[c][0], ready[c][1])
                P.wait("tensor", ready[c][0], ready[c][1])
            P.waitk("scalar", "ln_sq0")
            nsq = P.op("scalar", lambda s, c=c, buf=buf: s.activation(
                out=sq[buf][:, 0:TT], in_=xsl[c], func=AF.Square, bias=0.0, scale=1.0), inc="sc")
            nmm_s = P.op("tensor", lambda t, c=c: t.matmul(
                g["pb_sum"][0:1, 0:TT], g["ones_col"][:], xsl[c],
                start=(c == 0), stop=(c == NCH - 1)), inc="tc")
            P.wait("tensor", "sc", nsq)
            nmm_q = P.op("tensor", lambda t, c=c, buf=buf: t.matmul(
                g["pb_sq"][0:1, 0:TT], g["ones_col"][:], sq[buf][:, 0:TT],
                start=(c == 0), stop=(c == NCH - 1)), inc="tc")
            P.war["ln_sq0"] = {"tc": nmm_q}
            if c == NCH - 1:
                n_sum, n_sq, n_sq_sc = nmm_s, nmm_q, nsq
                if xkeep is None:
                    P.mark("ln_x", "tc", nmm_s)
                    P.mark("ln_x", "sc", nsq)
        mean = rows[0:1, 0:TT]; m2 = rows[0:1, 512:512 + TT]
        var = rows[0:1, 512:512 + TT]
        rstd = rows[0:1, 1024:1024 + TT]
        mr = rows[0:1, 1536:1536 + TT]; msq = rows[0:1, 2048:2048 + TT]
        lnv = rows[0:1, 2048:2048 + TT]
        P.waitk("vector", "ln_rows")
        P.wait("vector", "tc", n_sum)
        n1 = P.op("vector", lambda v: v.tensor_scalar_mul(mean, g["pb_sum"][0:1, 0:TT], 1.0 / D), inc="vc")
        P.wait("vector", "tc", n_sq)
        n2 = P.op("vector", lambda v: v.tensor_scalar_mul(m2, g["pb_sq"][0:1, 0:TT], 1.0 / D), inc="vc")
        P.war["pb_stats"] = {"vc": n2}
        P.wait("vector", "vc", n1)
        n3 = P.op("vector", lambda v: v.tensor_tensor(out=msq, in0=mean, in1=mean, op=ALU.mult), inc="vc")
        P.wait("vector", "vc", max(n2, n3))
        n4 = P.op("vector", lambda v: v.scalar_tensor_tensor(
            out=var, in0=m2, scalar=float(eps), in1=msq, op0=ALU.add, op1=ALU.subtract), inc="vc")
        P.wait("scalar", "vc", n4)
        n5 = P.op("scalar", lambda s: s.activation(out=lnv, in_=var, func=AF.Ln, bias=0.0, scale=1.0), inc="sc")
        P.wait("scalar", "sc", n5)
        n6 = P.op("scalar", lambda s: s.activation(out=rstd, in_=lnv, func=AF.Exp, bias=0.0, scale=-0.5), inc="sc")
        P.wait("vector", "sc", n6)
        n7 = P.op("vector", lambda v: v.tensor_tensor(out=mr, in0=mean, in1=rstd, op=ALU.mult), inc="vc")
        P.waitk("tensor", "pb_bcast")
        P.wait("tensor", "sc", n6)
        P.wait("tensor", "vc", n7)
        P.op("tensor", lambda t: t.matmul(g["pb_rb"][:, 0:TT], g["ones_f32"][:], rstd, start=True, stop=True))
        nb = P.op("tensor", lambda t: t.matmul(g["pb_mrb"][:, 0:TT], g["ones_f32"][:], mr, start=True, stop=True), inc="tc")
        P.war["ln_rows"] = {"tc": nb, "sc": n6}
        P.wait("scalar", "tc", nb)
        P.waitk("scalar", "rbsb")
        P.op("scalar", lambda s: s.activation(out=g["rb_sb"][:, 0:TT], in_=g["pb_rb"][:, 0:TT],
                                              func=AF.Identity, bias=0.0, scale=1.0), inc="sc")
        nbe = P.op("scalar", lambda s: s.activation(out=g["mrb_sb"][:, 0:TT], in_=g["pb_mrb"][:, 0:TT],
                                                    func=AF.Identity, bias=0.0, scale=1.0), inc="sc")
        P.war["pb_bcast"] = {"sc": nbe}
        outs = []
        t1 = g["lnt1f"] if fsrc is not None else g["lnt1"]
        t2 = g["lnt2"]
        prev_t1 = prev_t2 = None
        for c in range(NCH):
            if fsrc == "dram":
                P.waitk("sync", "ln_xf0")
                nf = P.op("sync", lambda sy, c=c: sy.dma_start(
                    out=g["ln_xf"][0][:, 0:TT], in_=src[c, :, t0:t0 + TT]),
                    inc="qf0", inc_by=16)
                asrc = g["ln_xf"][0][:, 0:TT]
                P.wait("vector", "qf0", nf)
            elif fsrc is not None:
                asrc, rsem, rval = fsrc[c]
                if rsem is not None:
                    P.wait("vector", rsem, rval)
            else:
                asrc = xsl[c]
            P.wait("vector", "sc", nbe)
            if prev_t1 is not None:
                P.wait("vector", "vc", prev_t1)
            na = P.op("vector", lambda v, c=c, asrc=asrc: v.tensor_tensor(
                out=t1[:, 0:TT], in0=asrc, in1=g["rb_sb"][:, 0:TT], op=ALU.mult), inc="vc")
            if fsrc == "dram":
                P.war["ln_xf0"] = {"vc": na}
            if xkeep is None and fsrc is None:
                P.mark("ln_x", "vc", na)
            P.wait("vector", "vc", na)
            if gc is None:
                nn = P.op("vector", lambda v, c=c: v.scalar_tensor_tensor(
                    out=xn[c], in0=g["mrb_sb"][:, 0:TT], scalar=-1.0, in1=t1[:, 0:TT],
                    op0=ALU.mult, op1=ALU.add), inc="vc")
                prev_t1 = nn
            else:
                if prev_t2 is not None:
                    P.wait("vector", "vc", prev_t2)
                nn0 = P.op("vector", lambda v: v.scalar_tensor_tensor(
                    out=t2[:, 0:TT], in0=g["mrb_sb"][:, 0:TT], scalar=-1.0, in1=t1[:, 0:TT],
                    op0=ALU.mult, op1=ALU.add), inc="vc")
                prev_t1 = nn0
                P.wait("vector", "vc", nn0)
                nn = P.op("vector", lambda v, c=c: v.tensor_scalar(
                    out=xn[c], in0=t2[:, 0:TT], scalar1=gc[:, c:c + 1], scalar2=bc[:, c:c + 1],
                    op0=ALU.mult, op1=ALU.add), inc="vc")
                prev_t2 = nn
            outs.append(nn)
        P.war["rbsb"] = {"vc": outs[-1]}
        if xkeep is not None:
            P.war["ln_keep"] = {"tc": n_sq, "sc": n_sq_sc, "vc": outs[-1]}
        return outs

    def emit_ffn_ternary(self, P, g, tag, x_src, x_dst):
        TT = 512
        PT = 1024
        npass = T // PT
        with contextlib.ExitStack() as stk:
            nb3 = [self.sbuf(stk, f"{tag}_nb{i}", [128, NCH * TT], BF16) for i in range(2)]
            A = self.sbuf(stk, f"{tag}_A", [128, NCH_H * PT], BF16)
            Bb = self.sbuf(stk, f"{tag}_B", [128, NCH_H * PT], BF16)
            t1t = self.sbuf(stk, f"{tag}_t1", [128, TT], BF16)
            t2t = self.sbuf(stk, f"{tag}_t2", [128, TT], BF16)
            xr = [self.sbuf(stk, f"{tag}_xr0", [128, TT], F32)] * 2
            xo = [self.sbuf(stk, f"{tag}_xo0", [128, TT], F32)] * 2
            wsl = [self.sbuf(stk, f"{tag}_w{i}", [128, NCH_H * 128], BF16) for i in range(3)]
            scol = {}
            for n in ("fs1c", "fs2c", "fhs1c", "fhs2c", "fs3c"):
                shp = [128, NCH_H] if n != "fs3c" else [128, NCH]
                scol[n] = self.sbuf(stk, f"{tag}_" + n, shp, F32)
            for n, sb in scol.items():
                P.op("sync", lambda sy, n=n, sb=sb: sy.dma_start(out=sb[:], in_=self.inp[n][:]),
                     inc="qc", inc_by=16)
            q_cols = P.cnt["qc"]
            P.wait("scalar", "qc", q_cols)
            P.wait("vector", "qc", q_cols)
            P.wait("tensor", "qc", q_cols)

            SL = ([("fw1T", NCH, oc) for oc in range(NCH_H) for _ in (0,)] and [])
            SL = []
            for oc in range(NCH_H):
                SL.append(("fw1T", NCH, oc))
                SL.append(("fw2T", NCH, oc))
            SL += [("fh1T", NCH_H, oc) for oc in range(NCH_H)]
            SL += [("fh2T", NCH_H, oc) for oc in range(NCH_H)]
            SL += [("fw3T", NCH_H, oc) for oc in range(NCH)]
            NSL = len(SL)
            NTOT = npass * NSL
            slab_load = {}
            pf = {"i": 0}

            def emit_slab_load(o):
                mat, nic, oc = SL[o % NSL]
                buf = o % 3
                P.waitk("gpsimd", f"{tag}_wsl{buf}")
                n = P.op("gpsimd", lambda gg, mat=mat, nic=nic, oc=oc, buf=buf: gg.dma_start(
                    out=wsl[buf][:, 0:nic * 128],
                    in_=self.inp[mat][oc]),
                    inc=f"qw{buf}", inc_by=16)
                slab_load[o] = (f"qw{buf}", n)

            def pump(limit):
                while pf["i"] < min(limit, NTOT):
                    emit_slab_load(pf["i"])
                    pf["i"] += 1

            pump(3)
            prev_no = [None, None]
            prev_store = [None, None]
            ln_done = {}

            def emit_ln_half(ht):
                slot = ht % 2
                P.waitk("vector", f"{tag}_nb{slot}")
                xnl = [nb3[slot][:, c * TT:(c + 1) * TT] for c in range(NCH)]
                outs = self.emit_ln_tile(P, g, x_src, ht * TT, TT, xnl, 1e-5)
                ln_done[ht] = outs[-1]

            for ps in range(npass):
                t0 = ps * PT
                base = ps * NSL
                if ps == 0:
                    emit_ln_half(0)
                    emit_ln_half(1)
                xn_counts = [[ln_done[2 * ps]], [ln_done[2 * ps + 1]]]

                a_ready = []
                sidx = 0
                prev_e1 = prev_e2 = None
                for oc in range(NCH_H):
                    o1, o2 = base + sidx, base + sidx + 1
                    b1, b2 = o1 % 3, o2 % 3
                    for half in range(2):
                        toff = half * TT
                        P.wait("tensor", slab_load[o1][0], slab_load[o1][1])
                        P.wait("tensor", slab_load[o2][0], slab_load[o2][1])
                        P.wait("tensor", "vc", ln_done[2 * ps + half])
                        P.waitk("tensor", "pb0")
                        slot = (2 * ps + half) % 2
                        for ic in range(NCH):
                            P.op("tensor", lambda t, ic=ic, b1=b1, slot=slot: t.matmul(
                                g["pb"][0][:, 0:TT],
                                wsl[b1].rearrange("p (ic o) -> p ic o", o=128)[:, ic, :],
                                nb3[slot][:, ic * TT:(ic + 1) * TT],
                                start=(ic == 0), stop=(ic == NCH - 1)),
                                inc="tc" if ic == NCH - 1 else None)
                        nmm1 = P.cnt["tc"]
                        P.waitk("tensor", "pb1")
                        for ic in range(NCH):
                            P.op("tensor", lambda t, ic=ic, b2=b2, slot=slot: t.matmul(
                                g["pb"][1][:, 0:TT],
                                wsl[b2].rearrange("p (ic o) -> p ic o", o=128)[:, ic, :],
                                nb3[slot][:, ic * TT:(ic + 1) * TT],
                                start=(ic == 0), stop=(ic == NCH - 1)),
                                inc="tc" if ic == NCH - 1 else None)
                        nmm2 = P.cnt["tc"]
                        P.mark(f"{tag}_nb{slot}", "tc", nmm2)
                        P.wait("scalar", "tc", nmm1)
                        if prev_e1 is not None:
                            P.wait("scalar", "vc", prev_e1)
                        e1 = P.op("scalar", lambda s, oc=oc: s.activation(
                            out=t1t[:, 0:TT], in_=g["pb"][0][:, 0:TT], func=AF.Silu,
                            bias=0.0, scale=scol["fs1c"][:, oc:oc + 1]), inc="sc")
                        P.war["pb0"] = {"sc": e1}
                        P.wait("scalar", "tc", nmm2)
                        e2 = P.op("scalar", lambda s, oc=oc: s.activation(
                            out=t2t[:, 0:TT], in_=g["pb"][1][:, 0:TT], func=AF.Identity,
                            bias=0.0, scale=scol["fs2c"][:, oc:oc + 1]), inc="sc")
                        P.war["pb1"] = {"sc": e2}
                        P.wait("vector", "sc", e2)
                        nA = P.op("vector", lambda v, oc=oc, toff=toff: v.tensor_tensor(
                            out=A[:, oc * PT + toff: oc * PT + toff + TT],
                            in0=t1t[:, 0:TT], in1=t2t[:, 0:TT], op=ALU.mult), inc="vc")
                        prev_e1 = prev_e2 = nA
                        a_ready.append(nA)
                    P.mark(f"{tag}_wsl{b1}", "tc", nmm1)
                    P.mark(f"{tag}_wsl{b2}", "tc", P.cnt["tc"])
                    sidx += 2
                    pump(base + sidx + 3)
                a_all = max(a_ready)

                def h_layer(src_buf, dst_buf, scol_name, ready_sem, ready_val, sidx0):
                    sidx = sidx0
                    done = []
                    for oc in range(NCH_H):
                        o = base + sidx
                        bw = o % 3
                        for half in range(2):
                            toff = half * TT
                            bname = f"pb{half}"
                            P.wait("tensor", slab_load[o][0], slab_load[o][1])
                            P.wait("tensor", ready_sem, ready_val)
                            P.waitk("tensor", bname)
                            for ic in range(NCH_H):
                                P.op("tensor", lambda t, ic=ic, bw=bw, toff=toff, half=half: t.matmul(
                                    g["pb"][half][:, 0:TT],
                                    wsl[bw].rearrange("p (ic o) -> p ic o", o=128)[:, ic, :],
                                    src_buf[:, ic * PT + toff: ic * PT + toff + TT],
                                    start=(ic == 0), stop=(ic == NCH_H - 1)),
                                    inc="tc" if ic == NCH_H - 1 else None)
                            nmm = P.cnt["tc"]
                            P.wait("scalar", "tc", nmm)
                            ne = P.op("scalar", lambda s, oc=oc, toff=toff, half=half: s.activation(
                                out=dst_buf[:, oc * PT + toff: oc * PT + toff + TT],
                                in_=g["pb"][half][:, 0:TT], func=AF.Gelu,
                                bias=0.0, scale=scol[scol_name][:, oc:oc + 1]), inc="sc")
                            P.war[bname] = {"sc": ne}
                            done.append(ne)
                        P.mark(f"{tag}_wsl{bw}", "tc", P.cnt["tc"])
                        sidx += 1
                        pump(base + sidx + 3)
                    return max(done), sidx

                b_all, sidx = h_layer(A, Bb, "fhs1c", "vc", a_all, sidx)
                a2_all, sidx = h_layer(Bb, A, "fhs2c", "sc", b_all, sidx)

                if ps + 1 < npass:
                    emit_ln_half(2 * ps + 2)
                    emit_ln_half(2 * ps + 3)

                for oc in range(NCH):
                    o = base + sidx
                    bw = o % 3
                    for half in range(2):
                        toff = half * TT
                        bname = f"pb{half}"
                        if prev_no[0] is not None:
                            P.wait("sync", "vc", prev_no[0])
                        nf = P.op("sync", lambda sy, oc=oc, t0=t0, toff=toff: sy.dma_start(
                            out=xr[0][:, 0:TT], in_=x_src[oc, :, t0 + toff:t0 + toff + TT]),
                            inc="qr0", inc_by=16)
                        P.wait("tensor", slab_load[o][0], slab_load[o][1])
                        P.wait("tensor", "sc", a2_all)
                        P.waitk("tensor", bname)
                        for ic in range(NCH_H):
                            P.op("tensor", lambda t, ic=ic, bw=bw, toff=toff, half=half: t.matmul(
                                g["pb"][half][:, 0:TT],
                                wsl[bw].rearrange("p (ic o) -> p ic o", o=128)[:, ic, :],
                                A[:, ic * PT + toff: ic * PT + toff + TT],
                                start=(ic == 0), stop=(ic == NCH_H - 1)),
                                inc="tc" if ic == NCH_H - 1 else None)
                        nmm = P.cnt["tc"]
                        P.wait("vector", "tc", nmm)
                        P.wait("vector", "qr0", nf)
                        if prev_store[0] is not None:
                            P.wait("vector", "qo0", prev_store[0])
                        no = P.op("vector", lambda v, oc=oc, toff=toff, half=half: v.scalar_tensor_tensor(
                            out=xo[0][:, 0:TT], in0=g["pb"][half][:, 0:TT],
                            scalar=scol["fs3c"][:, oc:oc + 1],
                            in1=xr[0][:, 0:TT], op0=ALU.mult, op1=ALU.add), inc="vc")
                        P.war[bname] = {"vc": no}
                        prev_no[0] = no
                        P.wait("sync", "vc", no)
                        nst = P.op("sync", lambda sy, oc=oc, t0=t0, toff=toff: sy.dma_start(
                            out=x_dst[oc, :, t0 + toff:t0 + toff + TT], in_=xo[0][:, 0:TT]),
                            inc="qo0", inc_by=16)
                        prev_store[0] = nst
                    P.mark(f"{tag}_wsl{bw}", "tc", P.cnt["tc"])
                    sidx += 1
                    pump(base + sidx + 3)
            for sname in ("qo0",):
                P.wait("sync", sname, P.cnt[sname])
                P.wait("gpsimd", sname, P.cnt[sname])
            P.barrier()


    def emit_stage_a(self, P, g, x_src, x_dst):
        nc = self.nc
        TT = 512
        qa, ka, va = self.dram["qa"], self.dram["ka"], self.dram["va"]
        s_na, s_mid = self.dram["s_na"], self.dram["s_mid"]

        with contextlib.ExitStack() as stk:
            qkvw = self.sbuf(stk, "a0_qkvw", [128, NCH * 3 * D], BF16)
            na_f = self.sbuf(stk, "a0_naf", [128, NCH * TT], F32)
            na_b = self.sbuf(stk, "a0_nab", [128, NCH * TT], BF16)
            xn = self.sbuf(stk, "a0_xn", [128, NCH * TT], BF16)
            qet = [self.sbuf(stk, f"a0_qe{i}", [128, TT], BF16) for i in range(2)]
            ket = [self.sbuf(stk, f"a0_ke{i}", [128, TT], BF16) for i in range(2)]
            vti = [self.sbuf(stk, f"a0_vt{i}", [128, 1024], BF16) for i in range(2)]
            gcol = self.sbuf(stk, "a0_gc", [128, NCH], F32)
            bcol = self.sbuf(stk, "a0_bc", [128, NCH], F32)
            for ic in range(NCH):
                P.op("gpsimd", lambda gg, ic=ic: gg.dma_start(
                    out=qkvw[:, ic * 3 * D:(ic + 1) * 3 * D],
                    in_=self.inp["aqkvT"][ic * 128:(ic + 1) * 128, :]), inc="qg0", inc_by=16)
            P.op("sync", lambda sy: sy.dma_start(out=gcol[:], in_=self.inp["alngc"][:]), inc="qc", inc_by=16)
            P.op("sync", lambda sy: sy.dma_start(out=bcol[:], in_=self.inp["alnbc"][:]), inc="qc", inc_by=16)
            P.wait("tensor", "qg0", P.cnt["qg0"])
            P.wait("tensor", "qc", P.cnt["qc"])
            P.wait("vector", "qc", P.cnt["qc"])
            P.wait("vector", "qg0", P.cnt["qg0"])
            P.wait("scalar", "qc", P.cnt["qc"])

            for b in range(BPC):
                for tile in range(S // TT):
                    t0 = b * S + tile * TT
                    naps = [na_f[:, c * TT:(c + 1) * TT] for c in range(NCH)]
                    P.waitk("vector", "a0_naf")
                    outs0 = self.emit_ln_tile(P, g, x_src, t0, TT, naps, 1e-5, fsrc="dram")
                    stc = []
                    P.waitk("vector", "a0_nab")
                    casts = []
                    for c in range(NCH):
                        P.wait("sync", "vc", outs0[c])
                        nst = P.op("sync", lambda sy, c=c, t0=t0: sy.dma_start(
                            out=s_na[c, :, t0:t0 + TT], in_=na_f[:, c * TT:(c + 1) * TT]),
                            inc="qd1", inc_by=16)
                        stc.append(nst)
                        P.wait("vector", "vc", outs0[c])
                        ncst = P.op("vector", lambda v, c=c: v.tensor_copy(
                            na_b[:, c * TT:(c + 1) * TT], na_f[:, c * TT:(c + 1) * TT]), inc="vc")
                        casts.append(ncst)
                    P.war["a0_naf"] = {"qd1": stc[-1], "vc": casts[-1]}
                    P.waitk("vector", "a0_xn")
                    nb_sl = [na_b[:, c * TT:(c + 1) * TT] for c in range(NCH)]
                    xn_sl = [xn[:, c * TT:(c + 1) * TT] for c in range(NCH)]
                    outs1 = self.emit_ln_tile(P, g, None, t0, TT, xn_sl, 1e-5,
                                              gc=gcol, bc=bcol, xkeep=nb_sl,
                                              ready=[("vc", casts[c]) for c in range(NCH)])
                    P.war["a0_nab"] = dict(P.war["ln_keep"])
                    xn_ready = outs1[-1]
                    for which, dst, scale, ring, qbase in (
                            (0, qa, 0.125, qet, 2), (1, ka, 1.0, ket, 4)):
                        for oc in range(NCH):
                            og = which * D + oc * 128
                            bank = g["pb"][oc % 2]
                            qsem = f"qd{qbase + oc % 2}"
                            P.waitk("tensor", f"pb{oc % 2}")
                            P.wait("tensor", "vc", xn_ready)
                            for ic in range(NCH):
                                P.op("tensor", lambda t, ic=ic, og=og, bank=bank: t.matmul(
                                    bank[:, 0:TT], qkvw[:, ic * 3 * D + og: ic * 3 * D + og + 128],
                                    xn[:, ic * TT:(ic + 1) * TT],
                                    start=(ic == 0), stop=(ic == NCH - 1)),
                                    inc="tc" if ic == NCH - 1 else None)
                            nmm = P.cnt["tc"]
                            rb = ring[oc % 2]
                            P.wait("scalar", "tc", nmm)
                            P.waitk("scalar", f"a0r_{qbase}{oc % 2}")
                            ne = P.op("scalar", lambda s, rb=rb, bank=bank, scale=scale: s.activation(
                                out=rb[:, 0:TT], in_=bank[:, 0:TT], func=AF.Identity,
                                bias=0.0, scale=scale), inc="sc")
                            P.war[f"pb{oc % 2}"] = {"sc": ne}
                            P.wait("sync", "sc", ne)
                            nst = P.op("sync", lambda sy, rb=rb, oc=oc, t0=t0, dstd=dst: sy.dma_start(
                                out=dstd[oc, :, t0:t0 + TT], in_=rb[:, 0:TT]), inc=qsem, inc_by=16)
                            P.war[f"a0r_{qbase}{oc % 2}"] = {qsem: nst}
                    for ts in range(TT // 128):
                        vb = vti[ts % 2]
                        P.waitk("scalar", f"a0_vt{ts % 2}")
                        ev = []
                        for oh in range(2):
                            bank = g["pb"][oh]
                            P.waitk("tensor", f"pb{oh}")
                            P.wait("tensor", "vc", xn_ready)
                            for ic in range(NCH):
                                P.op("tensor", lambda t, ic=ic, ts=ts, oh=oh, bank=bank: t.matmul(
                                    bank[:, 0:TT],
                                    xn[:, ic * TT + ts * 128: ic * TT + ts * 128 + 128],
                                    qkvw[:, ic * 3 * D + 2 * D + oh * TT: ic * 3 * D + 2 * D + oh * TT + TT],
                                    start=(ic == 0), stop=(ic == NCH - 1)),
                                    inc="tc" if ic == NCH - 1 else None)
                            nmm = P.cnt["tc"]
                            P.wait("scalar", "tc", nmm)
                            ne = P.op("scalar", lambda s, vb=vb, oh=oh, bank=bank: s.activation(
                                out=vb[:, oh * TT:(oh + 1) * TT], in_=bank[:, 0:TT],
                                func=AF.Identity, bias=0.0, scale=1.0), inc="sc")
                            P.war[f"pb{oh}"] = {"sc": ne}
                            ev.append(ne)
                        P.wait("sync", "sc", ev[-1])
                        nst = P.op("sync", lambda sy, vb=vb, t0=t0, ts=ts: sy.dma_start(
                            out=va[(t0 + ts * 128) // 128, :, :], in_=vb[:]), inc=f"qd{6 + ts % 2}", inc_by=16)
                        P.war[f"a0_vt{ts % 2}"] = {f"qd{6 + ts % 2}": nst}
            for sname in ("qd1", "qd2", "qd3", "qd4", "qd5", "qd6", "qd7"):
                P.wait("sync", sname, P.cnt[sname])
                P.wait("gpsimd", sname, P.cnt[sname])
            P.barrier()

        with contextlib.ExitStack() as stk:
            outw = self.sbuf(stk, "a1_outw", [128, NCH * D], BF16)
            qbuf = [self.sbuf(stk, f"a1_q{i}", [128, NCH * TT], BF16) for i in range(2)]
            kbuf = [self.sbuf(stk, f"a1_k{i}", [128, NCH * 768], BF16) for i in range(2)]
            vbuf = [self.sbuf(stk, f"a1_v{i}", [128, 6 * 1024], BF16) for i in range(2)]
            nabuf = [self.sbuf(stk, f"a1_na{i}", [128, NCH * TT], F32) for i in range(2)]
            ctx = self.sbuf(stk, "a1_ctx", [128, NCH * TT], BF16)
            ident = self.sbuf(stk, "a1_id", [128, 128], BF16)
            maska = self.sbuf(stk, "a1_mask", [128, 384], BF16)
            s_s = self.sbuf(stk, "a1_ss", [128, 384], F32)
            pbuf = self.sbuf(stk, "a1_p", [128, 384], BF16)
            ptb = self.sbuf(stk, "a1_pt", [128, 384], BF16)
            stats = self.sbuf(stk, "a1_st", [128, 4], F32)
            outt = [self.sbuf(stk, f"a1_o{i}", [128, TT], F32) for i in range(2)]
            for ic in range(NCH):
                P.op("gpsimd", lambda gg, ic=ic: gg.dma_start(
                    out=outw[:, ic * D:(ic + 1) * D],
                    in_=self.inp["aoutT"][ic * 128:(ic + 1) * 128, :]), inc="qg1", inc_by=16)
            P.op("sync", lambda sy: sy.dma_start(out=ident[:], in_=self.inp["ident"][:]), inc="qc", inc_by=16)
            P.op("sync", lambda sy: sy.dma_start(out=maska[:], in_=self.inp["mask_a"][:]), inc="qc", inc_by=16)
            P.wait("tensor", "qg1", P.cnt["qg1"])
            P.wait("tensor", "qc", P.cnt["qc"])

            NTILE = S // TT
            for b in range(BPC):
                for tile in range(NTILE):
                    t0 = b * S + tile * TT
                    r = (b * NTILE + tile) % 2
                    klo = max(b * S, t0 - 128)
                    khi = min(b * S + S, t0 + TT + 128)
                    kbase = t0 - 128
                    P.waitk("gpsimd", f"a1_q{r}")
                    nq = P.op("gpsimd", lambda gg, r=r, t0=t0: gg.dma_start(
                        out=qbuf[r].rearrange("p (c t) -> p c t", t=TT),
                        in_=qa[:, :, t0:t0 + TT].rearrange("c p t -> p c t")), inc=f"qg{3 + r}", inc_by=16)
                    P.waitk("gpsimd", f"a1_k{r}")
                    nk = P.op("gpsimd", lambda gg, r=r, klo=klo, khi=khi, kbase=kbase: gg.dma_start(
                        out=kbuf[r].rearrange("p (c t) -> p c t", t=768)[:, :, klo - kbase:khi - kbase],
                        in_=ka[:, :, klo:khi].rearrange("c p t -> p c t")), inc=f"qg{5 + r}", inc_by=16)
                    P.waitk("gpsimd", f"a1_v{r}")
                    nv = P.op("gpsimd", lambda gg, r=r, klo=klo, khi=khi, kbase=kbase: gg.dma_start(
                        out=vbuf[r].rearrange("p (c d) -> p c d", d=1024)[:, (klo - kbase) // 128:(khi - kbase) // 128, :],
                        in_=va[klo // 128:khi // 128, :, :].rearrange("c p d -> p c d")), inc=f"qg{7 + r}", inc_by=16)
                    P.waitk("sync", f"a1_na{r}")
                    nna = P.op("sync", lambda sy, r=r, t0=t0: sy.dma_start(
                        out=nabuf[r].rearrange("p (c t) -> p c t", t=TT),
                        in_=s_na[:, :, t0:t0 + TT].rearrange("c p t -> p c t")), inc=f"qd{12 + r}", inc_by=16)
                    P.wait("tensor", f"qg{3 + r}", nq)
                    P.wait("tensor", f"qg{5 + r}", nk)
                    P.wait("tensor", f"qg{7 + r}", nv)
                    last_q = last_k = last_v = 0
                    ctx_done = []
                    for hh in range(H):
                        hb = (hh % 2) * 64
                        hc = hh // 2
                        for j in range(4):
                            lo = 1 if (tile == 0 and j == 0) else 0
                            hi = 1 if (tile == NTILE - 1 and j == 3) else 2
                            W = (hi - lo + 1) * 128
                            bank = g["pb"][j % 2]
                            P.waitk("tensor", f"pb{j % 2}")
                            for dk in range(lo, hi + 1):
                                kcol = hc * 768 + (j + dk) * 128
                                P.op("tensor", lambda t, hb=hb, hc=hc, j=j, dk=dk, lo=lo, r=r, kcol=kcol, bank=bank: t.matmul(
                                    bank[:, 0:0] if False else bank[:, (dk - lo) * 128:(dk - lo) * 128 + 128],
                                    qbuf[r][hb:hb + 64, hc * TT + j * 128: hc * TT + j * 128 + 128],
                                    kbuf[r][hb:hb + 64, kcol:kcol + 128],
                                    start=(dk == lo), stop=False, tile_position=(hb, 0)))
                            nmm = P.op("tensor", lambda t, lo=lo, hi=hi, W=W, bank=bank: t.matmul(
                                bank[:, 0:W], ident[:], maska[:, lo * 128:lo * 128 + W],
                                start=False, stop=True), inc="tc")
                            last_q = nmm
                            P.wait("vector", "tc", nmm)
                            P.waitk("vector", "a1_ss")
                            nev = P.op("vector", lambda v, W=W, bank=bank: v.tensor_scalar(
                                out=s_s[:, 0:W], in0=bank[:, 0:W], scalar1=0.0, scalar2=None,
                                op0=ALU.bypass, op1=ALU.max, accum_out=stats[:, 0:1]), inc="vc")
                            P.war[f"pb{j % 2}"] = {"vc": nev}
                            P.wait("vector", "vc", nev)
                            nbias = P.op("vector", lambda v: v.tensor_scalar(
                                out=stats[:, 2:3], in0=stats[:, 0:1], scalar1=-1.0, scalar2=None,
                                op0=ALU.mult, op1=ALU.min, accum_out=stats[:, 1:2]), inc="vc")
                            P.wait("scalar", "vc", nbias)
                            P.waitk("scalar", "a1_p")
                            nexp = P.op("scalar", lambda s, W=W: s.activation(
                                out=pbuf[:, 0:W], in_=s_s[:, 0:W], func=AF.Exp,
                                bias=stats[:, 1:2], scale=1.0, accum_out=stats[:, 3:4]), inc="sc")
                            P.war["a1_ss"] = {"sc": nexp}
                            P.wait("vector", "sc", nexp)
                            nrs = P.op("vector", lambda v: v.reciprocal(stats[:, 2:3], stats[:, 3:4]), inc="vc")
                            P.wait("vector", "vc", nrs)
                            npn = P.op("vector", lambda v, W=W: v.tensor_scalar_mul(
                                pbuf[:, 0:W], pbuf[:, 0:W], stats[:, 2:3]), inc="vc")
                            P.wait("tensor", "vc", npn)
                            P.waitk("tensor", "pbt")
                            for rel in range(W // 128):
                                ntr = P.op("tensor", lambda t, rel=rel: t.transpose(
                                    g["pb_t"][:, rel * 128:rel * 128 + 128],
                                    pbuf[:, rel * 128:rel * 128 + 128], ident[:]),
                                    inc="tc" if rel == W // 128 - 1 else None)
                            P.war["a1_p"] = {"tc": ntr}
                            P.wait("vector", "tc", ntr)
                            P.waitk("vector", "a1_ptb")
                            npt = P.op("vector", lambda v, W=W: v.tensor_copy(ptb[:, 0:W], g["pb_t"][:, 0:W]), inc="vc")
                            P.war["pbt"] = {"vc": npt}
                            P.wait("tensor", "vc", npt)
                            P.waitk("tensor", "pb2")
                            for rel in range(W // 128):
                                vcol = (j + lo + rel) * 1024 + hh * 64
                                npv = P.op("tensor", lambda t, rel=rel, vcol=vcol, r=r, W=W: t.matmul(
                                    g["pb"][2][0:64, 0:128],
                                    vbuf[r][:, vcol:vcol + 64],
                                    ptb[:, rel * 128:rel * 128 + 128],
                                    start=(rel == 0), stop=(rel == W // 128 - 1)),
                                    inc="tc" if rel == W // 128 - 1 else None)
                            P.war["a1_ptb"] = {"tc": npv}
                            last_v = npv
                            P.wait("vector", "tc", npv)
                            nctx = P.op("vector", lambda v, hb=hb, hc=hc, j=j: v.tensor_copy(
                                ctx[hb:hb + 64, hc * TT + j * 128: hc * TT + j * 128 + 128],
                                g["pb"][2][0:64, 0:128]), inc="vc")
                            P.war["pb2"] = {"vc": nctx}
                            ctx_done.append(nctx)
                    ctx_all = max(ctx_done)
                    for oc in range(NCH):
                        bank = g["pb"][oc % 2]
                        P.waitk("tensor", f"pb{oc % 2}")
                        P.wait("tensor", "vc", ctx_all)
                        for ic in range(NCH):
                            P.op("tensor", lambda t, ic=ic, oc=oc, bank=bank: t.matmul(
                                bank[:, 0:TT], outw[:, ic * D + oc * 128: ic * D + oc * 128 + 128],
                                ctx[:, ic * TT:(ic + 1) * TT],
                                start=(ic == 0), stop=(ic == NCH - 1)),
                                inc="tc" if ic == NCH - 1 else None)
                        nmm = P.cnt["tc"]
                        ot = outt[oc % 2]
                        P.wait("vector", "tc", nmm)
                        P.wait("vector", f"qd{12 + r}", nna)
                        P.waitk("vector", f"a1_o{oc % 2}")
                        no = P.op("vector", lambda v, oc=oc, r=r, ot=ot, bank=bank: v.tensor_tensor(
                            out=ot[:, 0:TT], in0=bank[:, 0:TT],
                            in1=nabuf[r][:, oc * TT:(oc + 1) * TT], op=ALU.add), inc="vc")
                        P.war[f"pb{oc % 2}"] = {"vc": no}
                        P.mark(f"a1_na{r}", "vc", no)
                        P.wait("sync", "vc", no)
                        nst = P.op("sync", lambda sy, oc=oc, t0=t0, ot=ot: sy.dma_start(
                            out=s_mid[oc, :, t0:t0 + TT], in_=ot[:, 0:TT]), inc=f"qd{14 + oc % 2}", inc_by=16)
                        P.war[f"a1_o{oc % 2}"] = {f"qd{14 + oc % 2}": nst}
                    P.mark(f"a1_q{r}", "tc", last_q)
                    P.mark(f"a1_k{r}", "tc", last_q)
                    P.mark(f"a1_v{r}", "tc", last_v)
            for sname in ("qd14", "qd15"):
                P.wait("sync", sname, P.cnt[sname])
                P.wait("gpsimd", sname, P.cnt[sname])
            P.barrier()

        with contextlib.ExitStack() as stk:
            w1 = self.sbuf(stk, "a2_w1", [128, NCH * DH], BF16)
            w2 = self.sbuf(stk, "a2_w2", [128, NCH_H * D], BF16)
            b1c = self.sbuf(stk, "a2_b1", [128, NCH_H], F32)
            b2c = self.sbuf(stk, "a2_b2", [128, NCH], F32)
            midb = [self.sbuf(stk, "a2_mb0", [128, NCH * TT], BF16)]
            midf = [self.sbuf(stk, "a2_mf0", [128, TT], F32)]
            x0f = [self.sbuf(stk, "a2_x00", [128, TT], F32)]
            Ab = self.sbuf(stk, "a2_A", [128, NCH_H * TT], BF16)
            outb = [self.sbuf(stk, "a2_o0", [128, TT], F32)]
            for ic in range(NCH):
                P.op("gpsimd", lambda gg, ic=ic: gg.dma_start(
                    out=w1[:, ic * DH:(ic + 1) * DH],
                    in_=self.inp["aw1T"][ic * 128:(ic + 1) * 128, :]), inc="qg2", inc_by=16)
            for ic in range(NCH_H):
                P.op("gpsimd", lambda gg, ic=ic: gg.dma_start(
                    out=w2[:, ic * D:(ic + 1) * D],
                    in_=self.inp["aw2T"][ic * 128:(ic + 1) * 128, :]), inc="qg2", inc_by=16)
            P.op("sync", lambda sy: sy.dma_start(out=b1c[:], in_=self.inp["ab1c"][:]), inc="qc", inc_by=16)
            P.op("sync", lambda sy: sy.dma_start(out=b2c[:], in_=self.inp["ab2c"][:]), inc="qc", inc_by=16)
            P.wait("tensor", "qg2", P.cnt["qg2"])
            P.wait("scalar", "qc", P.cnt["qc"])
            P.wait("vector", "qc", P.cnt["qc"])

            for b in range(BPC):
                for tile in range(S // TT):
                    t0 = b * S + tile * TT
                    r = (b * (S // TT) + tile) % 2
                    P.waitk("gpsimd", "a2_mb0")
                    nmb = P.op("gpsimd", lambda gg, t0=t0: gg.dma_start(
                        out=midb[0].rearrange("p (c t) -> p c t", t=TT),
                        in_=s_mid[:, :, t0:t0 + TT].rearrange("c p t -> p c t")), inc="qg3", inc_by=16)
                    P.wait("tensor", "qg3", nmb)
                    gel = []
                    for oc in range(NCH_H):
                        bank = g["pb"][oc % 2]
                        P.waitk("tensor", f"pb{oc % 2}")
                        for ic in range(NCH):
                            P.op("tensor", lambda t, ic=ic, oc=oc, r=r, bank=bank: t.matmul(
                                bank[:, 0:TT], w1[:, ic * DH + oc * 128: ic * DH + oc * 128 + 128],
                                midb[0][:, ic * TT:(ic + 1) * TT],
                                start=(ic == 0), stop=(ic == NCH - 1)),
                                inc="tc" if ic == NCH - 1 else None)
                        nmm = P.cnt["tc"]
                        P.wait("scalar", "tc", nmm)
                        ne = P.op("scalar", lambda s, oc=oc, bank=bank: s.activation(
                            out=Ab[:, oc * TT:(oc + 1) * TT], in_=bank[:, 0:TT], func=AF.Gelu,
                            bias=b1c[:, oc:oc + 1], scale=1.0), inc="sc")
                        P.war[f"pb{oc % 2}"] = {"sc": ne}
                        gel.append(ne)
                    P.mark("a2_mb0", "tc", P.cnt["tc"])
                    a_all = max(gel)
                    for oc in range(NCH):
                        bank = g["pb"][oc % 2]
                        P.waitk("tensor", f"pb{oc % 2}")
                        P.wait("tensor", "sc", a_all)
                        P.waitk("sync", "a2_mf0")
                        nmf = P.op("sync", lambda sy, oc=oc, t0=t0: sy.dma_start(
                            out=midf[0][:, 0:TT], in_=s_mid[oc, :, t0:t0 + TT]), inc="qd8", inc_by=16)
                        P.waitk("sync", "a2_x00")
                        nx0 = P.op("sync", lambda sy, oc=oc, t0=t0: sy.dma_start(
                            out=x0f[0][:, 0:TT], in_=x_src[oc, :, t0:t0 + TT]), inc="qd10", inc_by=16)
                        for ic in range(NCH_H):
                            P.op("tensor", lambda t, ic=ic, oc=oc, bank=bank: t.matmul(
                                bank[:, 0:TT], w2[:, ic * D + oc * 128: ic * D + oc * 128 + 128],
                                Ab[:, ic * TT:(ic + 1) * TT],
                                start=(ic == 0), stop=(ic == NCH_H - 1)),
                                inc="tc" if ic == NCH_H - 1 else None)
                        nmm = P.cnt["tc"]
                        P.wait("vector", "tc", nmm)
                        P.wait("vector", "qd8", nmf)
                        P.wait("vector", "qd10", nx0)
                        P.waitk("vector", "a2_tmp")
                        ob = outb[0]
                        P.waitk("vector", "a2_ob0")
                        nt = P.op("vector", lambda v, oc=oc, bank=bank, ob=ob: v.scalar_tensor_tensor(
                            out=ob[:, 0:TT], in0=bank[:, 0:TT], scalar=b2c[:, oc:oc + 1],
                            in1=midf[0][:, 0:TT], op0=ALU.add, op1=ALU.add), inc="vc")
                        P.war[f"pb{oc % 2}"] = {"vc": nt}
                        P.mark("a2_mf0", "vc", nt)
                        P.wait("vector", "vc", nt)
                        no = P.op("vector", lambda v, ob=ob: v.tensor_tensor(
                            out=ob[:, 0:TT], in0=ob[:, 0:TT], in1=x0f[0][:, 0:TT], op=ALU.add), inc="vc")
                        P.mark("a2_x00", "vc", no)
                        P.wait("sync", "vc", no)
                        nst = P.op("sync", lambda sy, oc=oc, t0=t0, ob=ob: sy.dma_start(
                            out=x_dst[oc, :, t0:t0 + TT], in_=ob[:, 0:TT]), inc="qd14", inc_by=16)
                        P.war["a2_ob0"] = {"qd14": nst}
            for sname in ("qd14", "qd15"):
                P.wait("sync", sname, P.cnt[sname])
                P.wait("gpsimd", sname, P.cnt[sname])
            P.barrier()


    def _c_proj(self, P, g, l, b, tb, Q, K, Vx, ones_all, bqe, que, bkc, bvr16, l1w, l1b, ident):
        TT = 512
        NTQ = S // TT
        s_h = self.dram["s_h"]
        with contextlib.ExitStack() as pk:
            xn = [self.sbuf(pk, "cp_xn0", [128, NCH * TT], BF16)] * 2
            wsl = [self.sbuf(pk, f"cp_w{i}", [128, NCH * TT], BF16) for i in range(6)]
            wload = {}
            WMATS = [(f"bWqT{l}", 0), (f"bWqT{l}", 1), (f"bWkT{l}", 0), (f"bWkT{l}", 1),
                     (f"bWvT{l}", 0), (f"bWvT{l}", 1)]
            for i in range(6):
                mat, hf = WMATS[i]
                P.waitk("gpsimd", f"cp_w{i}")
                n = P.op("gpsimd", lambda gg, mat=mat, hf=hf, i=i: gg.dma_start(
                    out=wsl[i][:], in_=self.inp[mat][hf]),
                    inc=f"qg{i}", inc_by=16)
                wload[i] = (f"qg{i}", n)
            q0_ready = None
            for tile in range(NTQ):
                t0 = tb + tile * TT
                r = 0
                P.waitk("vector", "cp_xn0")
                xsl = [xn[r][:, c * TT:(c + 1) * TT] for c in range(NCH)]
                outs = self.emit_ln_tile(P, g, s_h, t0, TT, xsl, 1e-6, gc=l1w, bc=l1b)
                xn_ready = outs[-1]
                for wi, (dstbuf, biascol, scale) in enumerate((
                        (Q, bqe, 0.125), (K, bkc, 1.0))):
                    for hf in range(2):
                        si = wi * 2 + hf
                        sname, nld = wload[si]
                        buf = si
                        for oc4 in range(4):
                            oc = hf * 4 + oc4
                            bank = g["pb"][oc4 % 2]
                            P.waitk("tensor", f"pb{oc4 % 2}")
                            P.wait("tensor", sname, nld)
                            P.wait("tensor", "vc", xn_ready)
                            for ic in range(NCH):
                                P.op("tensor", lambda t, ic=ic, oc4=oc4, buf=buf, r=r, bank=bank: t.matmul(
                                    bank[:, 0:TT],
                                    wsl[buf][:, ic * TT + oc4 * 128: ic * TT + oc4 * 128 + 128],
                                    xn[r][:, ic * TT:(ic + 1) * TT],
                                    start=(ic == 0), stop=(ic == NCH - 1)),
                                    inc="tc" if ic == NCH - 1 else None)
                            nmm = P.cnt["tc"]
                            P.wait("scalar", "tc", nmm)
                            P.waitk("scalar", "qkevac")
                            ne = P.op("scalar", lambda s, dstbuf=dstbuf, oc=oc, tile=tile, bank=bank,
                                      biascol=biascol, scale=scale: s.activation(
                                out=dstbuf[:, oc * S + tile * TT: oc * S + tile * TT + TT],
                                in_=bank[:, 0:TT], func=AF.Identity,
                                bias=biascol[:, oc:oc + 1], scale=scale), inc="sc")
                            P.war[f"pb{oc4 % 2}"] = {"sc": ne}
                for hf in range(2):
                    si = 4 + hf
                    sname, nld = wload[si]
                    buf = si
                    for ts in range(TT // 128):
                        kc = (tile * TT + ts * 128) // 128
                        bank = g["pb"][ts % 2]
                        P.waitk("tensor", f"pb{ts % 2}")
                        P.wait("tensor", sname, nld)
                        P.wait("tensor", "vc", xn_ready)
                        P.wait("tensor", "vc", ones_all)
                        for ic in range(NCH):
                            P.op("tensor", lambda t, ic=ic, ts=ts, buf=buf, r=r, bank=bank: t.matmul(
                                bank[:, 0:TT],
                                xn[r][:, ic * TT + ts * 128: ic * TT + ts * 128 + 128],
                                wsl[buf][:, ic * TT:(ic + 1) * TT],
                                start=(ic == 0), stop=False),
                                inc=None)
                        nmm = P.op("tensor", lambda t, hf=hf, bank=bank: t.matmul(
                            bank[:, 0:TT], g["ones_row"][:],
                            bvr16[0:1, hf * TT:(hf + 1) * TT],
                            start=False, stop=True), inc="tc")
                        P.wait("scalar", "tc", nmm)
                        P.waitk("scalar", "qkevac")
                        for hh2 in range(8):
                            hh = hf * 8 + hh2
                            ne = P.op("scalar", lambda s, kc=kc, hh=hh, hh2=hh2, bank=bank: s.activation(
                                out=Vx[:, kc * (H * 64) + hh * 64: kc * (H * 64) + hh * 64 + 64],
                                in_=bank[:, hh2 * 64: hh2 * 64 + 64], func=AF.Identity,
                                bias=0.0, scale=1.0), inc="sc")
                        P.war[f"pb{ts % 2}"] = {"sc": ne}
                P.mark("cp_xn0", "tc", P.cnt["tc"])
                if tile == 0:
                    q0_ready = P.cnt["sc"]
            for i in range(6):
                P.mark(f"cp_w{i}", "tc", P.cnt["tc"])
            P.war["qkevac"] = {}
        return q0_ready
        P.barrier()



    def _c_t12(self, P, g, tq2, war_sc, Q, que, t12):
        TT = 512
        T1b, T2b, qut, trg_s, trg_c = t12
        st_loc = tq2 * TT
        P.waitk("sync", "ca_ts")
        nts = P.op("sync", lambda sy, st_loc=st_loc, trg_s=trg_s: sy.dma_start(
            out=trg_s[:], in_=self.inp["trig_s"][:, st_loc:st_loc + TT]), inc="qd10", inc_by=16)
        ntc2 = P.op("sync", lambda sy, st_loc=st_loc, trg_c=trg_c: sy.dma_start(
            out=trg_c[:], in_=self.inp["trig_c"][:, st_loc:st_loc + TT]), inc="qd11", inc_by=16)
        P.wait("gpsimd", "qd10", nts)
        P.wait("gpsimd", "qd11", ntc2)
        if war_sc:
            P.wait("gpsimd", "sc", war_sc)
        for c in range(NCH):
            P.waitk("gpsimd", "ca_T12")
            P.waitk("gpsimd", "ca_qu")
            nqu = P.op("gpsimd", lambda gg, c=c, tq2=tq2, Q=Q, qut=qut, que=que: gg.tensor_scalar(
                out=qut[:, 0:TT], in0=Q[:, c * S + tq2 * TT: c * S + tq2 * TT + TT],
                scalar1=que[:, c:c + 1], scalar2=None,
                op0=ALU.add, op1=ALU.bypass), inc="gc")
            P.wait("gpsimd", "gc", nqu)
            n1 = P.op("gpsimd", lambda gg, c=c, T1b=T1b, qut=qut, trg_s=trg_s: gg.tensor_tensor(
                out=T1b[:, c * TT:(c + 1) * TT], in0=qut[:, 0:TT],
                in1=trg_s[:, 0:TT], op=ALU.mult), inc="gc")
            n2 = P.op("gpsimd", lambda gg, c=c, T2b=T2b, qut=qut, trg_c=trg_c: gg.tensor_tensor(
                out=T2b[:, c * TT:(c + 1) * TT], in0=qut[:, 0:TT],
                in1=trg_c[:, 0:TT], op=ALU.mult), inc="gc")
            P.war["ca_qu"] = {"gc": n2}
        P.war["ca_ts"] = {"gc": P.cnt["gc"]}
        return P.cnt["gc"]

    def _c_attn(self, P, g, l, b, tb, Q, K, Vx, ident, maskb, pswap, que, boc, s_h, t12, t12_ready0):
        T1b, T2b, qut, trg_s, trg_c = t12
        TT = 512
        NTQ = S // TT
        NQT = S // 128
        with contextlib.ExitStack() as ak:
            kpos = self.sbuf(ak, "ca_kp", [128, S], BF16)
            kpos2 = self.sbuf(ak, "ca_kp2", [128, S], BF16)
            maskb = self.sbuf(ak, "ca_mb", [128, 4 * 512], BF16)
            wo = self.sbuf(ak, "ca_wo", [128, NCH * TT], BF16)
            s_s = [self.sbuf(ak, "ca_ss0", [128, S], F32)] * 2
            pb_ = [self.sbuf(ak, f"ca_p{i}", [128, S], BF16) for i in range(2)]
            ptb = self.sbuf(ak, "ca_pt", [128, S], BF16)
            stats = self.sbuf(ak, "ca_st", [128, 24], F32)
            ctx = self.sbuf(ak, "ca_ctx", [128, NCH * TT], BF16)
            hres = [self.sbuf(ak, "ca_hr0", [128, TT], F32)] * 2
            hout = [self.sbuf(ak, "ca_ho0", [128, TT], F32)] * 2
            P.waitk("sync", "ca_kpos")
            nkp = P.op("sync", lambda sy: sy.dma_start(out=kpos[:], in_=self.inp["kposdup"][:]), inc="qd12", inc_by=16)
            nkp2 = P.op("sync", lambda sy: sy.dma_start(out=kpos2[:], in_=self.inp["kposdup2"][:]), inc="qd12", inc_by=16)
            nmb = P.op("sync", lambda sy, maskb=maskb: sy.dma_start(out=maskb[:], in_=self.inp["mask_b"][:]), inc="qd12", inc_by=16)
            P.wait("tensor", "qd12", nmb)
            P.mark("ca_kpos", "qd12", nmb)

            pend = []
            pending_op = []
            ctx_done = {}
            def phase2(unit):
                hh, qt, ridx = unit
                hb = (hh % 2) * 64
                nk = qt + 1
                pbuf = pb_[ridx]
                P.wait("tensor", "vc", exp_done[(hh, qt)])
                P.waitk("vector", "ca_ptb")
                ntr = None
                for g0 in range(0, nk, 8):
                    gend = min(g0 + 8, nk)
                    P.waitk("tensor", "pbt")
                    for kc in range(g0, gend):
                        ntr = P.op("tensor", lambda t, kc=kc, g0=g0, pbuf=pbuf: t.transpose(
                            g["pb_t"][:, (kc - g0) * 128:(kc - g0) * 128 + 128],
                            pbuf[:, kc * 128:kc * 128 + 128], ident[:]),
                            inc="tc" if kc == gend - 1 else None)
                    W = (gend - g0) * 128
                    P.wait("vector", "tc", ntr)
                    npt = P.op("vector", lambda v, g0=g0, W=W: v.tensor_copy(
                        ptb[:, g0 * 128: g0 * 128 + W], g["pb_t"][:, 0:W]), inc="vc")
                    P.war["pbt"] = {"vc": npt}
                npt_last = P.cnt["vc"]
                P.war[f"ca_p2_{ridx}"] = {"tc": ntr}
                P.wait("tensor", "vc", npt_last)
                for kc in range(nk):
                    kcg = kc
                    npv = P.op("tensor", lambda t, kcg=kcg, hh=hh, qt=qt, kc=kc: t.matmul(
                        g["pb"][2][0:64, (qt % 4) * 128:(qt % 4) * 128 + 128],
                        Vx[:, kcg * (H * 64) + hh * 64: kcg * (H * 64) + hh * 64 + 64],
                        ptb[:, kc * 128:kc * 128 + 128],
                        start=(kc == 0), stop=(kc == nk - 1)),
                        inc="tc" if kc == nk - 1 else None)
                P.war["ca_ptb"] = {"tc": npv}
                ctx_done[unit] = npv
                if qt % 4 == 3:
                    hc = hh // 2
                    P.wait("scalar", "tc", npv)
                    P.waitk("scalar", "ca_ctxw")
                    nctx = P.op("scalar", lambda s, hb=hb, hc=hc: s.activation(
                        out=ctx[hb:hb + 64, hc * TT:(hc + 1) * TT],
                        in0=0 if False else g["pb"][2][0:64, 0:TT], in_=g["pb"][2][0:64, 0:TT],
                        func=AF.Identity, bias=0.0, scale=1.0) if False else s.activation(
                        out=ctx[hb:hb + 64, hc * TT:(hc + 1) * TT],
                        in_=g["pb"][2][0:64, 0:TT],
                        func=AF.Identity, bias=0.0, scale=1.0), inc="sc")
                    P.war["pb2"] = {"sc": nctx}
                    ctx_done[unit] = ("sc", nctx)

            exp_done = {}
            for tq in range(NTQ):
                if tq == 0:
                    t12_ready = t12_ready0
                qr_all = t12_ready
                for hh in range(H):
                    hb = (hh % 2) * 64
                    hc = hh // 2
                    for qq in range(4):
                        qt = tq * 4 + qq
                        uidx = hh * 4 + qq
                        if uidx == 2 and pending_op:
                            pending_op.pop(0)()
                        ridx = uidx % 2
                        unit = (hh, qt, ridx)
                        nk5 = qt // 4 + 1
                        par = uidx % 2
                        so = 12 * ridx
                        banks = (g["pb"][0], g["pb"][1]) if par == 0 else (g["pb"][4], g["pb"][5])
                        bnames = ("pb0", "pb1") if par == 0 else ("pb4", "pb5")
                        ev = []
                        for kc5 in range(nk5):
                            W = min(512, qt * 128 + 128 - kc5 * 512)
                            bank = banks[kc5 % 2]
                            bname = bnames[kc5 % 2]
                            P.waitk("tensor", bname)
                            P.wait("tensor", "gc", qr_all)
                            last_scores = P.cnt["tc"] + 1
                            P.op("tensor", lambda t, hb=hb, hc=hc, qt=qt, kc5=kc5, W=W, bank=bank: t.matmul(
                                bank[:, 0:W],
                                Q[hb:hb + 64, hc * S + qt * 128: hc * S + qt * 128 + 128],
                                K[hb:hb + 64, hc * S + kc5 * 512: hc * S + kc5 * 512 + W],
                                start=True, stop=False, tile_position=(hb, 0)))
                            P.op("tensor", lambda t, hb=hb, hc=hc, qq=qq, kc5=kc5, W=W, bank=bank: t.matmul(
                                bank[:, 0:W],
                                T1b[hb:hb + 64, hc * TT + qq * 128: hc * TT + qq * 128 + 128],
                                kpos[hb:hb + 64, kc5 * 512: kc5 * 512 + W],
                                start=False, stop=False, tile_position=(hb, 0)))
                            P.op("tensor", lambda t, hb=hb, hc=hc, qq=qq, kc5=kc5, W=W, bank=bank, nk5=nk5: t.matmul(
                                bank[:, 0:W],
                                T2b[hb:hb + 64, hc * TT + qq * 128: hc * TT + qq * 128 + 128],
                                kpos2[hb:hb + 64, kc5 * 512: kc5 * 512 + W],
                                start=False, stop=(kc5 != nk5 - 1), tile_position=(hb, 0)),
                                inc="tc" if kc5 != nk5 - 1 else None)
                            if kc5 == nk5 - 1:
                                P.op("tensor", lambda t, qq=qq, W=W, bank=bank: t.matmul(
                                    bank[:, 0:W], ident[:],
                                    maskb[:, qq * 512: qq * 512 + W],
                                    start=False, stop=True), inc="tc")
                            nmm = P.cnt["tc"]
                            P.wait("vector", "tc", nmm)
                            if kc5 == 0:
                                P.waitk("vector", "ca_ss0")
                            nev = P.op("vector", lambda v, kc5=kc5, W=W, bank=bank, ridx=ridx, so=so: v.tensor_scalar(
                                out=s_s[ridx][:, kc5 * 512: kc5 * 512 + W], in0=bank[:, 0:W],
                                scalar1=0.0, scalar2=None, op0=ALU.bypass, op1=ALU.max,
                                accum_out=stats[:, so + kc5:so + kc5 + 1]), inc="vc")
                            P.war[bname] = {"vc": nev}
                            ev.append(nev)
                        P.wait("vector", "vc", ev[-1])
                        nbias = P.op("vector", lambda v, nk5=nk5, so=so: v.tensor_scalar(
                            out=stats[:, so + 8:so + 8 + nk5], in0=stats[:, so:so + nk5],
                            scalar1=-1.0, scalar2=None,
                            op0=ALU.mult, op1=ALU.min, accum_out=stats[:, so + 7:so + 8]), inc="vc")
                        if len(pend) >= 2:
                            phase2(pend.pop(0))
                        pend.append(unit)
                        Wtot = qt * 128 + 128
                        P.wait("scalar", "vc", nbias)
                        P.waitk("scalar", f"ca_p2_{ridx}")
                        nexp = P.op("scalar", lambda s, Wtot=Wtot, ridx=ridx, so=so: s.activation(
                            out=pb_[ridx][:, 0:Wtot], in_=s_s[ridx][:, 0:Wtot], func=AF.Exp,
                            bias=stats[:, so + 7:so + 8], scale=1.0,
                            accum_out=stats[:, so + 6:so + 7]), inc="sc")
                        P.war["ca_ss0"] = {"sc": nexp}
                        P.wait("vector", "sc", nexp)
                        nrs = P.op("vector", lambda v, so=so: v.reciprocal(
                            stats[:, so + 5:so + 6], stats[:, so + 6:so + 7]), inc="vc")
                        P.wait("vector", "vc", nrs)
                        npn = P.op("vector", lambda v, Wtot=Wtot, ridx=ridx, so=so: v.tensor_scalar_mul(
                            pb_[ridx][:, 0:Wtot], pb_[ridx][:, 0:Wtot], stats[:, so + 5:so + 6]), inc="vc")
                        exp_done[(hh, qt)] = npn
                while pend:
                    phase2(pend.pop(0))
                t0 = tb + tq * TT
                ctx_all = P.cnt["sc"]
                if tq + 1 < NTQ:
                    P.mark("ca_T12", "tc", last_scores)
                    t12_ready = self._c_t12(P, g, tq + 1, 0, Q, que, t12)

                def emit_outproj(t0=t0, ctx_all=ctx_all, l=l):
                    P.waitk("gpsimd", "ca_woL")
                    nwo0 = P.op("gpsimd", lambda gg, l=l: gg.dma_start(
                        out=wo[:], in_=self.inp[f"bWoT{l}"][0]),
                        inc="qg7", inc_by=16)
                    wo_load = {0: ("qg7", nwo0)}
                    for hf in range(2):
                        if hf == 0:
                            sname, nwo = wo_load[0]
                            P.wait("tensor", sname, nwo)
                        else:
                            P.wait("gpsimd", "tc", n_half0)
                            nwo = P.op("gpsimd", lambda gg, l=l: gg.dma_start(
                                out=wo[:], in_=self.inp[f"bWoT{l}"][1]),
                                inc="qg8", inc_by=16)
                            P.wait("tensor", "qg8", nwo)
                        for oc4 in range(4):
                            oc = hf * 4 + oc4
                            bank = g["pb"][oc % 2]
                            P.waitk("tensor", f"pb{oc % 2}")
                            P.wait("tensor", "sc", ctx_all)
                            for ic in range(NCH):
                                P.op("tensor", lambda t, ic=ic, oc4=oc4, bank=bank: t.matmul(
                                    bank[:, 0:TT], wo[:, ic * TT + oc4 * 128: ic * TT + oc4 * 128 + 128],
                                    ctx[:, ic * TT:(ic + 1) * TT],
                                    start=(ic == 0), stop=(ic == NCH - 1)),
                                    inc="tc" if ic == NCH - 1 else None)
                            nmm = P.cnt["tc"]
                            rr = 0
                            P.waitk("sync", "ca_hr0")
                            nhl = P.op("sync", lambda sy, oc=oc, t0=t0: sy.dma_start(
                                out=hres[0][:, 0:TT], in_=s_h[oc, :, t0:t0 + TT]), inc="qd2", inc_by=16)
                            P.wait("vector", "tc", nmm)
                            P.wait("vector", "qd2", nhl)
                            P.waitk("vector", "ca_ho0")
                            nho = P.op("vector", lambda v, oc=oc, rr=rr, bank=bank: v.scalar_tensor_tensor(
                                out=hout[0][:, 0:TT], in0=bank[:, 0:TT], scalar=boc[:, oc:oc + 1],
                                in1=hres[rr][:, 0:TT], op0=ALU.add, op1=ALU.add), inc="vc")
                            P.war[f"pb{oc % 2}"] = {"vc": nho}
                            P.war["ca_hr0"] = {"vc": nho}
                            P.wait("sync", "vc", nho)
                            nst = P.op("sync", lambda sy, oc=oc, t0=t0: sy.dma_start(
                                out=s_h[oc, :, t0:t0 + TT], in_=hout[0][:, 0:TT]), inc="qd4", inc_by=16)
                            P.war["ca_ho0"] = {"qd4": nst}
                        if hf == 0:
                            n_half0 = P.cnt["tc"]
                        else:
                            P.war["ca_woL"] = {"tc": P.cnt["tc"]}

                    P.mark("ca_ctxw", "tc", P.cnt["tc"])

                pending_op.append(emit_outproj)
                ctx_done = {}
            while pending_op:
                pending_op.pop(0)()
            for sname in ("qd4",):
                P.wait("sync", sname, P.cnt[sname])
                P.wait("gpsimd", sname, P.cnt[sname])
            P.barrier()



    def _c_ffn(self, P, g, l, b, tb, b1c, b2c, l2w, l2b, s_h, x_src, x_dst):
        TT = 512
        NTQ = S // TT
        with contextlib.ExitStack() as fk:
            fn = [self.sbuf(fk, f"cf_fn{i}", [128, NCH * TT], BF16) for i in range(3)]
            mid = self.sbuf(fk, "cf_mid", [128, NCH_H * TT], BF16)
            w1sl = [self.sbuf(fk, f"cf_w1{i}", [128, NCH * TT], BF16) for i in range(2)]
            w2sl = [self.sbuf(fk, f"cf_w2{i}", [128, NCH_H * 128], BF16) for i in range(2)]
            hres2 = [self.sbuf(fk, f"cf_hr{i}", [128, TT], F32) for i in range(2)]
            hout2 = self.sbuf(fk, "cf_ho", [128, TT], F32)
            x2r = self.sbuf(fk, "cf_x2", [128, TT], F32)
            w1load = {}
            w2load = {}
            def w1slab(i):
                buf = i % 2
                P.waitk("gpsimd", f"cf_w1{buf}")
                n = P.op("gpsimd", lambda gg, i=i, buf=buf, l=l: gg.dma_start(
                    out=w1sl[buf][:], in_=self.inp[f"bW1T{l}"][i]),
                    inc=f"qg{3 + buf}", inc_by=16)
                w1load[i] = (f"qg{3 + buf}", n)
            def w2slab(i):
                buf = i % 2
                P.waitk("gpsimd", f"cf_w2{buf}")
                n = P.op("gpsimd", lambda gg, i=i, buf=buf, l=l: gg.dma_start(
                    out=w2sl[buf][:], in_=self.inp[f"bW2T{l}"][i]),
                    inc=f"qg{5 + buf}", inc_by=16)
                w2load[i] = (f"qg{5 + buf}", n)
            w1slab(0); w1slab(1)
            w2slab(0); w2slab(1)
            ln_done_f = {}
            def emit_ln_f(tile2):
                slot = tile2 % 3
                P.waitk("vector", f"cf_fn{slot}")
                fsl = [fn[slot][:, c * TT:(c + 1) * TT] for c in range(NCH)]
                outs = self.emit_ln_tile(P, g, s_h, tb + tile2 * TT, TT, fsl, 1e-6, gc=l2w, bc=l2b)
                ln_done_f[tile2] = outs[-1]
            for tile in range(NTQ):
                t0 = tb + tile * TT
                r = tile % 3
                if tile == 0:
                    emit_ln_f(0)
                fn_ready = ln_done_f[tile]
                for i8 in range(8):
                    sname, nld = w1load[i8 % 8 if tile == 0 else i8]
                    buf = i8 % 2
                    for oc4 in range(4):
                        oc = i8 * 4 + oc4
                        bank = g["pb"][oc4 % 2]
                        P.waitk("tensor", f"pb{oc4 % 2}")
                        P.wait("tensor", sname, nld)
                        P.wait("tensor", "vc", fn_ready)
                        for ic in range(NCH):
                            P.op("tensor", lambda t, ic=ic, oc4=oc4, buf=buf, r=r, bank=bank: t.matmul(
                                bank[:, 0:TT],
                                w1sl[buf][:, ic * TT + oc4 * 128: ic * TT + oc4 * 128 + 128],
                                fn[r][:, ic * TT:(ic + 1) * TT],
                                start=(ic == 0), stop=(ic == NCH - 1)),
                                inc="tc" if ic == NCH - 1 else None)
                        nmm = P.cnt["tc"]
                        P.wait("scalar", "tc", nmm)
                        ne = P.op("scalar", lambda s, oc=oc, bank=bank: s.activation(
                            out=mid[:, oc * TT:(oc + 1) * TT], in_=bank[:, 0:TT],
                            func=AF.Gelu, bias=b1c[:, oc:oc + 1], scale=1.0), inc="sc")
                        P.war[f"pb{oc4 % 2}"] = {"sc": ne}
                    P.mark(f"cf_w1{buf}", "tc", P.cnt["tc"])
                    nxt = i8 + 2
                    if nxt < 8:
                        w1slab(nxt)
                    elif tile < NTQ - 1:
                        w1load[nxt - 8] = None
                if tile < NTQ - 1:
                    w1slab(0); w1slab(1)
                mid_all = P.cnt["sc"]
                P.mark(f"cf_fn{r}", "tc", P.cnt["tc"])
                if tile + 1 < NTQ:
                    emit_ln_f(tile + 1)
                for i8 in range(8):
                    oc = i8
                    sname, nld = w2load[i8]
                    buf = i8 % 2
                    if True:
                        bank = g["pb"][i8 % 2]
                        P.waitk("tensor", f"pb{i8 % 2}")
                        P.wait("tensor", sname, nld)
                        P.wait("tensor", "sc", mid_all)
                        for ic in range(NCH_H):
                            P.op("tensor", lambda t, ic=ic, buf=buf, bank=bank: t.matmul(
                                bank[:, 0:TT],
                                w2sl[buf][:, ic * 128: ic * 128 + 128],
                                mid[:, ic * TT:(ic + 1) * TT],
                                start=(ic == 0), stop=(ic == NCH_H - 1)),
                                inc="tc" if ic == NCH_H - 1 else None)
                        nmm = P.cnt["tc"]
                        rr = oc % 2
                        P.waitk("sync", f"cf_hr{rr}")
                        nhl = P.op("sync", lambda sy, oc=oc, t0=t0, rr=rr: sy.dma_start(
                            out=hres2[rr][:, 0:TT], in_=s_h[oc, :, t0:t0 + TT]), inc=f"qd{6 + rr}", inc_by=16)
                        P.wait("vector", "tc", nmm)
                        P.wait("vector", f"qd{6 + rr}", nhl)
                        P.waitk("vector", "cf_ho")
                        nho = P.op("vector", lambda v, oc=oc, rr=rr, bank=bank: v.scalar_tensor_tensor(
                            out=hout2[:, 0:TT], in0=bank[:, 0:TT], scalar=b2c[:, oc:oc + 1],
                            in1=hres2[rr][:, 0:TT], op0=ALU.add, op1=ALU.add), inc="vc")
                        P.war[f"pb{i8 % 2}"] = {"vc": nho}
                        P.war[f"cf_hr{rr}"] = {"vc": nho}
                        if l == 3:
                            P.waitk("sync", "cf_x2")
                            nx2 = P.op("sync", lambda sy, oc=oc, t0=t0: sy.dma_start(
                                out=x2r[:, 0:TT], in_=x_src[oc, :, t0:t0 + TT]), inc="qd8", inc_by=16)
                            P.wait("vector", "vc", nho)
                            P.wait("vector", "qd8", nx2)
                            nho = P.op("vector", lambda v: v.tensor_tensor(
                                out=hout2[:, 0:TT], in0=hout2[:, 0:TT], in1=x2r[:, 0:TT],
                                op=ALU.add), inc="vc")
                            P.war["cf_x2"] = {"vc": nho}
                        P.war["cf_ho"] = {"vc": nho}
                        dst = x_dst if l == 3 else s_h
                        P.wait("sync", "vc", nho)
                        nst = P.op("sync", lambda sy, oc=oc, t0=t0, dst=dst: sy.dma_start(
                            out=dst[oc, :, t0:t0 + TT], in_=hout2[:, 0:TT]), inc="qd9", inc_by=16)
                        P.mark("cf_ho", "qd9", nst)
                    P.mark(f"cf_w2{buf}", "tc", P.cnt["tc"])
                    nxt = i8 + 2
                    if nxt < 8:
                        w2slab(nxt)
                if tile < NTQ - 1:
                    w2slab(0); w2slab(1)
            for sname in ("qd9",):
                P.wait("sync", sname, P.cnt[sname])
                P.wait("gpsimd", sname, P.cnt[sname])
            P.barrier()


    def emit_stage_c(self, P, g, x_src, x_dst):
        nc = self.nc
        TT = 512
        s_h = self.dram["s_h"]
        NQT = S // 128
        NTQ = S // TT

        with contextlib.ExitStack() as stk:
            ident = self.sbuf(stk, "c_id", [128, 128], BF16)
            pswap = self.sbuf(stk, "c_psw", [128, 128], BF16)
            bqe = self.sbuf(stk, "c_bqe", [128, NCH], F32)
            que = self.sbuf(stk, "c_que", [128, NCH], F32)
            bkc = self.sbuf(stk, "c_bkc", [128, NCH], F32)
            boc = self.sbuf(stk, "c_boc", [128, NCH], F32)
            b1c = self.sbuf(stk, "c_b1c", [128, NCH_H], F32)
            b2c = self.sbuf(stk, "c_b2c", [128, NCH], F32)
            l1w = self.sbuf(stk, "c_l1w", [128, NCH], F32)
            l1b = self.sbuf(stk, "c_l1b", [128, NCH], F32)
            l2w = self.sbuf(stk, "c_l2w", [128, NCH], F32)
            l2b = self.sbuf(stk, "c_l2b", [128, NCH], F32)
            ucol = self.sbuf(stk, "c_uc", [128, NCH], F32)
            bvr = self.sbuf(stk, "c_bvr", [1, D], F32)
            bvr16 = self.sbuf(stk, "c_bvr16", [1, D], BF16)
            P.op("sync", lambda sy: sy.dma_start(out=ident[:], in_=self.inp["ident"][:]), inc="qc", inc_by=16)
            P.op("sync", lambda sy: sy.dma_start(out=pswap[:], in_=self.inp["pswap"][:]), inc="qc", inc_by=16)
            nqc0 = P.cnt["qc"]
            P.wait("tensor", "qc", nqc0)
            P.wait("vector", "qc", nqc0)

            with contextlib.ExitStack() as hk:
                h0t = self.sbuf(hk, "c_h0", [128, NCH * TT], F32)
                for b in range(BPC):
                    for tile in range(NTQ):
                        t0 = b * S + tile * TT
                        haps = [h0t[:, c * TT:(c + 1) * TT] for c in range(NCH)]
                        P.waitk("vector", "c_h0")
                        outs = self.emit_ln_tile(P, g, x_src, t0, TT, haps, 1e-5, fsrc="dram")
                        stc = []
                        for c in range(NCH):
                            P.wait("sync", "vc", outs[c])
                            nst = P.op("sync", lambda sy, c=c, t0=t0: sy.dma_start(
                                out=s_h[c, :, t0:t0 + TT], in_=h0t[:, c * TT:(c + 1) * TT]),
                                inc="qd1", inc_by=16)
                            stc.append(nst)
                        P.war["c_h0"] = {"qd1": stc[-1]}
                P.wait("sync", "qd1", P.cnt["qd1"])
                P.wait("gpsimd", "qd1", P.cnt["qd1"])
                P.barrier()

            for l in range(self.c_layers):
                for n, sb in ((f"bbqc{l}", bqe), (f"buc{l}", ucol), (f"bbkc{l}", bkc),
                              (f"bboc{l}", boc), (f"bb1c{l}", b1c), (f"bb2c{l}", b2c),
                              (f"bln1wc{l}", l1w), (f"bln1bc{l}", l1b),
                              (f"bln2wc{l}", l2w), (f"bln2bc{l}", l2b)):
                    P.waitk("sync", "c_cols")
                    P.op("sync", lambda sy, n=n, sb=sb: sy.dma_start(out=sb[:], in_=self.inp[n][:]),
                         inc="qc", inc_by=16)
                P.waitk("sync", "c_cols")
                P.op("sync", lambda sy, l=l: sy.dma_start(out=bvr[:], in_=self.inp[f"bbvr{l}"][:]),
                     inc="qc", inc_by=16)
                nqc = P.cnt["qc"]
                P.wait("vector", "qc", nqc)
                nv1 = P.op("vector", lambda v: v.tensor_tensor(out=que[:], in0=bqe[:], in1=ucol[:], op=ALU.add), inc="vc")
                P.wait("vector", "vc", nv1)
                nv2 = P.op("vector", lambda v: v.tensor_scalar_mul(que[:], que[:], 0.125), inc="vc")
                nv3 = P.op("vector", lambda v: v.tensor_scalar_mul(bqe[:], bqe[:], 0.125), inc="vc")
                nv4 = P.op("vector", lambda v: v.tensor_copy(bvr16[:], bvr[:]), inc="vc")
                ncols = P.cnt["vc"]
                P.war["c_cols"] = {"vc": ncols}
                P.wait("scalar", "vc", ncols)
                P.wait("tensor", "vc", ncols)

                for b in range(BPC):
                    tb = b * S
                    qk_stack = contextlib.ExitStack()
                    Q = self.sbuf(qk_stack, "c_Q", [128, NCH * S], BF16)
                    K = self.sbuf(qk_stack, "c_K", [128, NCH * S], BF16)
                    Vx = self.sbuf(qk_stack, "c_V", [128, (S // 128) * (H * 64)], BF16)
                    T1b = self.sbuf(qk_stack, "c_T1", [128, NCH * 512], BF16)
                    T2b = self.sbuf(qk_stack, "c_T2", [128, NCH * 512], BF16)
                    qut = self.sbuf(qk_stack, "c_qu", [128, 512], BF16)
                    trg_s = self.sbuf(qk_stack, "c_ts", [128, 512], BF16)
                    trg_c = self.sbuf(qk_stack, "c_tc", [128, 512], BF16)
                    t12 = (T1b, T2b, qut, trg_s, trg_c)
                    ones_all = P.cnt["vc"]
                    q0_ready = self._c_proj(P, g, l, b, tb, Q, K, Vx, ones_all, bqe, que, bkc, bvr16, l1w, l1b, ident)
                    t12_ready0 = self._c_t12(P, g, 0, q0_ready, Q, que, t12)
                    P.barrier()
                    self._c_attn(P, g, l, b, tb, Q, K, Vx, ident, None, pswap, que, boc, s_h, t12, t12_ready0)
                    if self.c_attn_only:
                        P.mark("c_Vones", "tc", P.cnt["tc"])
                        P.mark("c_Vones", "vc", P.cnt["vc"])
                        qk_stack.close()
                        continue
                    P.mark("c_Vones", "tc", P.cnt["tc"])
                    P.mark("c_Vones", "vc", P.cnt["vc"])
                    qk_stack.close()
                    self._c_ffn(P, g, l, b, tb, b1c, b2c, l2w, l2b, s_h, x_src, x_dst)
        return

    def build(self):
        nc = self.nc
        self.declare()
        with contextlib.ExitStack() as stk:
            P = Prog(nc)
            semnames = ["tc", "vc", "sc", "gc", "qc"] + ["b_" + e for e in ENGS]
            semnames += [f"qx{i}" for i in range(NCH)]
            semnames += [f"qw{i}" for i in range(3)]
            semnames += ["qr0", "qr1", "qo0", "qo1", "qf0", "qf1"]
            semnames += [f"qd{i}" for i in range(16)]
            semnames += [f"qg{i}" for i in range(10)]
            for s in semnames:
                P.add_sem(stk, s)
            g = {}
            g["pb"] = [stk.enter_context(nc.psum_tensor(f"pb{i}", [128, 512], F32)) for i in range(7)]
            g["pb_t"] = stk.enter_context(nc.psum_tensor("pbt", [128, 1024], BF16))
            g["pb_sum"] = g["pb"][3]; g["pb_sq"] = g["pb"][4]
            g["pb_rb"] = g["pb"][5]; g["pb_mrb"] = g["pb"][6]
            g["ones_col"] = self.sbuf(stk, "g_ones_col", [128, 1], BF16)
            g["ones_f32"] = self.sbuf(stk, "g_ones_f32", [1, 128], F32)
            g["rows"] = self.sbuf(stk, "g_rows", [1, 2560], F32)
            g["lnt1f"] = self.sbuf(stk, "g_lnt1f", [128, 512], F32)
            g["rb_sb"] = self.sbuf(stk, "g_rbsb", [128, 512], F32)
            g["mrb_sb"] = self.sbuf(stk, "g_mrbsb", [128, 512], F32)
            g["lnt1"] = g["lnt1f"]
            g["ln_xf"] = [self.sbuf(stk, "g_lnxf0", [128, 512], F32)] * 2
            g["lnt2"] = self.sbuf(stk, "g_lnt2", [128, 512], BF16)
            g["ln_x"] = self.sbuf(stk, "g_lnx", [128, NCH * 512], BF16)
            g["ln_sq"] = [self.sbuf(stk, "g_lnsq0", [128, 512], BF16)] * 2

            g["ones_row"] = self.sbuf(stk, "g_ones_row", [1, 128], BF16)
            P.op("sync", lambda sy: sy.dma_start(out=g["ones_col"][:], in_=self.inp["ones_col"][:]), inc="qc", inc_by=16)
            P.op("sync", lambda sy: sy.dma_start(out=g["ones_f32"][:], in_=self.inp["ones_f32"][:]), inc="qc", inc_by=16)
            P.op("sync", lambda sy: sy.dma_start(out=g["ones_row"][:], in_=self.inp["ones_row"][:]), inc="qc", inc_by=16)

            if self.stages == ("B",):
                self.emit_ffn_ternary(P, g, "fB", self.inp["x_in"], self.y)
            elif self.stages == ("A",):
                self.emit_stage_a(P, g, self.inp["x_in"], self.y)
            elif self.stages == ("A", "B"):
                self.emit_stage_a(P, g, self.inp["x_in"], self.dram["s_x1"])
                self.emit_ffn_ternary(P, g, "fB", self.dram["s_x1"], self.y)
            elif self.stages == ("C",):
                self.emit_stage_c(P, g, self.inp["x_in"], self.y)
            elif self.stages == ("A", "B", "C", "D"):
                self.emit_stage_a(P, g, self.inp["x_in"], self.dram["s_x1"])
                self.emit_ffn_ternary(P, g, "fB", self.dram["s_x1"], self.dram["s_x2"])
                self.emit_stage_c(P, g, self.dram["s_x2"], self.dram["s_x3"])
                self.emit_ffn_ternary(P, g, "fD", self.dram["s_x3"], self.y)
            else:
                raise NotImplementedError(self.stages)

            for sname in ("qo0", "qo1", "qd14", "qd15"):
                if P.cnt.get(sname):
                    P.wait("sync", sname, P.cnt[sname])

            block = stk.enter_context(nc.Block())
            for e, meth in (("sync", block.sync), ("tensor", block.tensor),
                            ("vector", block.vector), ("scalar", block.scalar),
                            ("gpsimd", block.gpsimd)):
                def run(E, e=e):
                    for f in P.streams[e]:
                        f(E)
                meth(run)
        self.P = P
        return nc


def host_in_maps(inputs, stages=("A", "B", "C", "D"), x_override=None):
    w = inputs
    base = {}
    for n, a in CONSTS.items():
        base[n] = np.ascontiguousarray(a)
    if "A" in stages:
        base["aqkvT"] = wt_np(w["L0_attn_a_qkv_w"])
        base["aoutT"] = wt_np(w["L0_attn_a_out_w"])
        base["aw1T"] = wt_np(w["L0_attn_a_w1"])
        base["aw2T"] = wt_np(w["L0_attn_a_w2"])
        base["ab1c"] = cols_np(w["L0_attn_a_b1"])
        base["ab2c"] = cols_np(w["L0_attn_a_b2"])
        base["alngc"] = cols_np(w["L0_attn_a_ln_g"])
        base["alnbc"] = cols_np(w["L0_attn_a_ln_b"])
    if "B" in stages or "D" in stages:
        base["fw1T"] = slab_np(w["L0_ffn_w1_w"], 128)
        base["fw2T"] = slab_np(w["L0_ffn_w2_w"], 128)
        base["fh1T"] = slab_np(w["L0_ffn_h1_w"], 128)
        base["fh2T"] = slab_np(w["L0_ffn_h2_w"], 128)
        base["fw3T"] = slab_np(w["L0_ffn_w3_w"], 128)
        base["fs1c"] = cols_np(w["L0_ffn_w1_s"])
        base["fs2c"] = cols_np(w["L0_ffn_w2_s"])
        base["fhs1c"] = cols_np(w["L0_ffn_h1_s"])
        base["fhs2c"] = cols_np(w["L0_ffn_h2_s"])
        base["fs3c"] = cols_np(w["L0_ffn_w3_s"])
    if "C" in stages:
        for l in range(4):
            base[f"bWqT{l}"] = slab_np(w["L0_attn_b_Wq"][l], 512)
            base[f"bWkT{l}"] = slab_np(w["L0_attn_b_Wk"][l], 512)
            base[f"bWvT{l}"] = slab_np(w["L0_attn_b_Wv"][l], 512)
            base[f"bWoT{l}"] = slab_np(w["L0_attn_b_Wo"][l], 512)
            base[f"bbqc{l}"] = cols_np(w["L0_attn_b_bq"][l])
            base[f"bbkc{l}"] = cols_np(w["L0_attn_b_bk"][l])
            base[f"bbvr{l}"] = np.ascontiguousarray(
                np.asarray(w["L0_attn_b_bv"][l], np.float32).reshape(1, D))
            base[f"bboc{l}"] = cols_np(w["L0_attn_b_bo"][l])
            base[f"buc{l}"] = cols_np(np.asarray(w["L0_attn_b_u_bias"][l]).reshape(D))
            base[f"bW1T{l}"] = slab_np(w["L0_attn_b_W1"][l], 512)
            base[f"bW2T{l}"] = slab_np(w["L0_attn_b_W2"][l], 128)
            base[f"bb1c{l}"] = cols_np(w["L0_attn_b_b1"][l])
            base[f"bb2c{l}"] = cols_np(w["L0_attn_b_b2"][l])
            base[f"bln1wc{l}"] = cols_np(w["L0_attn_b_ln1_w"][l])
            base[f"bln1bc{l}"] = cols_np(w["L0_attn_b_ln1_b"][l])
            base[f"bln2wc{l}"] = cols_np(w["L0_attn_b_ln2_w"][l])
            base[f"bln2bc{l}"] = cols_np(w["L0_attn_b_ln2_b"][l])
    x = np.asarray(x_override if x_override is not None else w["L0_attn_a_x"], np.float32)
    maps = []
    for c in range(NCORE):
        m = dict(base)
        xs = x[c * BPC:(c + 1) * BPC]
        m["x_in"] = np.ascontiguousarray(
            xs.reshape(BPC, S, NCH, 128).transpose(2, 3, 0, 1).reshape(NCH, 128, T))
        maps.append(m)
    return maps


def gather_y(results):
    out = np.empty((NCORE * BPC, S, D), np.float32)
    for c, r in enumerate(results):
        y = r["y"]
        out[c * BPC:(c + 1) * BPC] = y.reshape(NCH, 128, BPC, S).transpose(2, 3, 0, 1).reshape(BPC, S, D)
    return out


def kernel(**inputs):
    m = MK(stages=("A", "B", "C", "D"))
    nc = m.build()
    maps = host_in_maps(inputs, stages=("A", "B", "C", "D"))
    res = run_bass_kernel_spmd(nc, maps, core_ids=list(range(NCORE)))
    return gather_y(res.results)


# version 09 (final)
# speedup vs baseline: 1.0386x; 1.0352x over previous
import contextlib
import numpy as np
import ml_dtypes
import concourse.bass as bass
import concourse.mybir as mybir
from concourse.bass_utils import run_bass_kernel_spmd

F32 = mybir.dt.float32
BF16 = mybir.dt.bfloat16
AF = mybir.ActivationFunctionType
ALU = mybir.AluOpType
AX = mybir.AxisListType
BF = ml_dtypes.bfloat16

D = 1024
NCH = 8
DH = 4096
NCH_H = 32
H = 16
HD = 64
S = 2048
BPC = 2
T = BPC * S
NCORE = 8

ENGS = ("tensor", "vector", "scalar", "sync", "gpsimd")


def make_consts():
    c = {}
    c["ident"] = np.eye(128, dtype=np.float32).astype(BF)
    P = np.zeros((128, 128), np.float32)
    for blk in range(2):
        o = blk * 64
        for m in range(32):
            P[o + m + 32, o + m] = 1.0
            P[o + m, o + m + 32] = -1.0
    c["pswap"] = P.astype(BF)
    c["ones_f32"] = np.ones((1, 128), np.float32)
    c["ones_col"] = np.ones((128, 1), BF)
    c["ones_row"] = np.ones((1, 128), BF)
    inv = 1.0 / 10000.0 ** (np.arange(0, HD, 2, dtype=np.float64) / HD)
    tpos = np.arange(S, dtype=np.float64)
    ang = tpos[None, :] * inv[(np.arange(128) % 32), None]
    c["trig_s"] = np.sin(ang).astype(np.float32).astype(BF)
    c["trig_c"] = np.cos(ang).astype(np.float32).astype(BF)
    kp = np.zeros((128, S), np.float64)
    kp[0:32] = np.cos(tpos[None, :] * inv[:, None])
    kp[32:64] = np.sin(tpos[None, :] * inv[:, None])
    kp[64:128] = kp[0:64]
    c["kposdup"] = kp.astype(np.float32).astype(BF)
    kp2 = np.zeros((128, S), np.float64)
    kp2[0:32] = -np.sin(tpos[None, :] * inv[:, None])
    kp2[32:64] = np.cos(tpos[None, :] * inv[:, None])
    kp2[64:128] = kp2[0:64]
    c["kposdup2"] = kp2.astype(np.float32).astype(BF)
    mb = np.zeros((128, 4 * 512), np.float32)
    pidx = np.arange(128)[:, None]
    cc = np.arange(512)[None, :]
    for a in range(4):
        mb[:, a * 512:(a + 1) * 512] = np.where(cc <= 128 * a + pidx, 0.0, -1e30)
    c["mask_b"] = mb.astype(BF)
    j = np.arange(384)[None, :]
    rel = (j // 128 - 1) * 128 + (j % 128) - pidx
    c["mask_a"] = np.where(np.abs(rel) <= 64, 0.0, -1e30).astype(np.float32).astype(BF)
    return c


CONSTS = make_consts()


def cols_np(v):
    v = np.asarray(v, np.float32)
    return np.ascontiguousarray(v.reshape(-1, 128).T)


def wt_np(w):
    return np.ascontiguousarray(np.asarray(w, np.float32).T)


def slab_np(w, osz):
    WT = np.asarray(w, np.float32).T
    I, O = WT.shape
    nic = I // 128
    n = O // osz
    return np.ascontiguousarray(WT.reshape(nic, 128, n, osz).transpose(2, 1, 0, 3).reshape(n, 128, nic * osz))


class Prog:
    def __init__(self, nc):
        self.nc = nc
        self.streams = {e: [] for e in ENGS}
        self.cnt = {}
        self.sem = {}
        self.iic = {e: 0 for e in ENGS}
        self.war = {}

    def add_sem(self, stk, name):
        self.sem[name] = stk.enter_context(self.nc.semaphore(name))
        self.cnt[name] = 0

    def op(self, eng, fn, inc=None, inc_by=1):
        self.iic[eng] += 1
        if inc is None:
            self.streams[eng].append(fn)
            return None
        self.cnt[inc] += inc_by
        h = self.sem[inc]
        def wrapped(E, fn=fn, h=h, n=inc_by):
            fn(E).then_inc(h, n)
        self.streams[eng].append(wrapped)
        return self.cnt[inc]

    def wait(self, eng, sem, val):
        if val is None or val <= 0:
            return
        h = self.sem[sem]
        self.streams[eng].append(lambda E, h=h, v=val: E.wait_ge(h, v))

    def waitk(self, eng, key):
        v = self.war.get(key)
        if v:
            for semname, val in v.items():
                self.wait(eng, semname, val)

    def mark(self, key, semname, val):
        d = self.war.setdefault(key, {})
        d[semname] = max(d.get(semname, 0), val)

    def cleark(self, key):
        self.war.pop(key, None)

    def barrier(self, engs=ENGS):
        vals = {}
        for e in engs:
            sname = "b_" + e
            self.cnt[sname] += 1
            vals[e] = self.cnt[sname]
            h = self.sem[sname]
            self.streams[e].append(lambda E, h=h: E.nop().then_inc(h, 1))
        for e in engs:
            for o in engs:
                if o != e:
                    self.wait(e, "b_" + o, vals[o])


class MK:
    def __init__(self, stages=("A", "B", "C", "D"), c_layers=4, c_attn_only=False):
        self.stages = stages
        self.c_layers = c_layers
        self.c_attn_only = c_attn_only
        self.nc = bass.Bass("TRN2", target_bir_lowering=False)
        self.inp = {}
        self.dram = {}

    def declare(self):
        nc = self.nc
        add = lambda n, shp, dt=F32: self.inp.__setitem__(
            n, nc.dram_tensor(n, shp, dt, kind="ExternalInput").ap())
        add("x_in", [NCH, 128, T])
        for n, a in CONSTS.items():
            dt = BF16 if a.dtype == BF else F32
            self.inp[n] = nc.dram_tensor(n, list(a.shape), dt, kind="ExternalInput").ap()
        if "A" in self.stages:
            add("aqkvT", [D, 3 * D]); add("aoutT", [D, D])
            add("aw1T", [D, DH]); add("aw2T", [DH, D])
            add("ab1c", [128, NCH_H]); add("ab2c", [128, NCH])
            add("alngc", [128, NCH]); add("alnbc", [128, NCH])
        if "B" in self.stages or "D" in self.stages:
            add("fw1T", [DH // 128, 128, NCH * 128]); add("fw2T", [DH // 128, 128, NCH * 128])
            add("fh1T", [DH // 128, 128, NCH_H * 128]); add("fh2T", [DH // 128, 128, NCH_H * 128])
            add("fw3T", [D // 128, 128, NCH_H * 128])
            add("fs1c", [128, NCH_H]); add("fs2c", [128, NCH_H])
            add("fhs1c", [128, NCH_H]); add("fhs2c", [128, NCH_H]); add("fs3c", [128, NCH])
        if "C" in self.stages:
            for l in range(4):
                add(f"bWqT{l}", [2, 128, NCH * 512]); add(f"bWkT{l}", [2, 128, NCH * 512])
                add(f"bWvT{l}", [2, 128, NCH * 512]); add(f"bWoT{l}", [2, 128, NCH * 512])
                add(f"bbqc{l}", [128, NCH]); add(f"bbkc{l}", [128, NCH])
                add(f"bbvr{l}", [1, D]); add(f"bboc{l}", [128, NCH])
                add(f"buc{l}", [128, NCH])
                add(f"bW1T{l}", [DH // 512, 128, NCH * 512]); add(f"bW2T{l}", [D // 128, 128, NCH_H * 128])
                add(f"bb1c{l}", [128, NCH_H]); add(f"bb2c{l}", [128, NCH])
                add(f"bln1wc{l}", [128, NCH]); add(f"bln1bc{l}", [128, NCH])
                add(f"bln2wc{l}", [128, NCH]); add(f"bln2bc{l}", [128, NCH])
        self.y = nc.dram_tensor("y", [NCH, 128, T], F32, kind="ExternalOutput").ap()
        names = ["s_mid", "s_x1"]
        if "A" in self.stages:
            names += ["s_na"]
        if "C" in self.stages:
            names += ["s_x2", "s_x3", "s_h"]
        if "D" in self.stages and "s_x2" not in names:
            names += ["s_x2", "s_x3"]
        for n in names:
            self.dram[n] = nc.dram_tensor(n, [NCH, 128, T], F32, kind="Internal").ap()
        if "A" in self.stages:
            for n in ("qa", "ka"):
                self.dram[n] = nc.dram_tensor(n, [NCH, 128, T], BF16, kind="Internal").ap()
            self.dram["va"] = nc.dram_tensor("va", [T // 128, 128, D], BF16, kind="Internal").ap()

    def sbuf(self, stk, name, shape, dtype):
        self._uid = getattr(self, "_uid", 0) + 1
        return stk.enter_context(self.nc.sbuf_tensor(f"{name}_u{self._uid}", shape, dtype))

    def emit_ln_tile(self, P, g, src, t0, TT, xn, eps, gc=None, bc=None, xkeep=None,
                     ready=None, fsrc=None):
        lx = g["ln_x"]; sq = g["ln_sq"]
        rows = g["rows"]
        if xkeep is not None:
            xsl = xkeep
        else:
            xsl = [lx[:, c * 512:c * 512 + TT] for c in range(NCH)]
        P.waitk("tensor", "pb_stats")
        n_sum = n_sq = 0
        if xkeep is None:
            P.waitk("gpsimd", "ln_x")
            lx2 = lx.rearrange("p (c t) -> p c t", t=512)[:, :, 0:TT]
            nq = P.op("gpsimd", lambda gg, lx2=lx2: gg.dma_start(
                out=lx2, in_=src[:, :, t0:t0 + TT].rearrange("c p t -> p c t")),
                inc="qx0", inc_by=16)
            P.wait("scalar", "qx0", nq)
            P.wait("tensor", "qx0", nq)
        for c in range(NCH):
            buf = c % 2
            if False:
                pass
            elif ready is not None and ready[c] is not None:
                P.wait("scalar", ready[c][0], ready[c][1])
                P.wait("tensor", ready[c][0], ready[c][1])
            P.waitk("scalar", "ln_sq0")
            nsq = P.op("scalar", lambda s, c=c, buf=buf: s.activation(
                out=sq[buf][:, 0:TT], in_=xsl[c], func=AF.Square, bias=0.0, scale=1.0), inc="sc")
            nmm_s = P.op("tensor", lambda t, c=c: t.matmul(
                g["pb_sum"][0:1, 0:TT], g["ones_col"][:], xsl[c],
                start=(c == 0), stop=(c == NCH - 1)), inc="tc")
            P.wait("tensor", "sc", nsq)
            nmm_q = P.op("tensor", lambda t, c=c, buf=buf: t.matmul(
                g["pb_sq"][0:1, 0:TT], g["ones_col"][:], sq[buf][:, 0:TT],
                start=(c == 0), stop=(c == NCH - 1)), inc="tc")
            P.war["ln_sq0"] = {"tc": nmm_q}
            if c == NCH - 1:
                n_sum, n_sq, n_sq_sc = nmm_s, nmm_q, nsq
                if xkeep is None:
                    P.mark("ln_x", "tc", nmm_s)
                    P.mark("ln_x", "sc", nsq)
        mean = rows[0:1, 0:TT]; m2 = rows[0:1, 512:512 + TT]
        var = rows[0:1, 512:512 + TT]
        rstd = rows[0:1, 1024:1024 + TT]
        mr = rows[0:1, 1536:1536 + TT]; msq = rows[0:1, 2048:2048 + TT]
        lnv = rows[0:1, 2048:2048 + TT]
        P.waitk("vector", "ln_rows")
        P.wait("vector", "tc", n_sum)
        n1 = P.op("vector", lambda v: v.tensor_scalar_mul(mean, g["pb_sum"][0:1, 0:TT], 1.0 / D), inc="vc")
        P.wait("vector", "tc", n_sq)
        n2 = P.op("vector", lambda v: v.tensor_scalar_mul(m2, g["pb_sq"][0:1, 0:TT], 1.0 / D), inc="vc")
        P.war["pb_stats"] = {"vc": n2}
        P.wait("vector", "vc", n1)
        n3 = P.op("vector", lambda v: v.tensor_tensor(out=msq, in0=mean, in1=mean, op=ALU.mult), inc="vc")
        P.wait("vector", "vc", max(n2, n3))
        n4 = P.op("vector", lambda v: v.scalar_tensor_tensor(
            out=var, in0=m2, scalar=float(eps), in1=msq, op0=ALU.add, op1=ALU.subtract), inc="vc")
        P.wait("scalar", "vc", n4)
        n5 = P.op("scalar", lambda s: s.activation(out=lnv, in_=var, func=AF.Ln, bias=0.0, scale=1.0), inc="sc")
        P.wait("scalar", "sc", n5)
        n6 = P.op("scalar", lambda s: s.activation(out=rstd, in_=lnv, func=AF.Exp, bias=0.0, scale=-0.5), inc="sc")
        P.wait("vector", "sc", n6)
        n7 = P.op("vector", lambda v: v.tensor_tensor(out=mr, in0=mean, in1=rstd, op=ALU.mult), inc="vc")
        P.waitk("tensor", "pb_bcast")
        P.wait("tensor", "sc", n6)
        P.wait("tensor", "vc", n7)
        P.op("tensor", lambda t: t.matmul(g["pb_rb"][:, 0:TT], g["ones_f32"][:], rstd, start=True, stop=True))
        nb = P.op("tensor", lambda t: t.matmul(g["pb_mrb"][:, 0:TT], g["ones_f32"][:], mr, start=True, stop=True), inc="tc")
        P.war["ln_rows"] = {"tc": nb, "sc": n6}
        P.wait("scalar", "tc", nb)
        P.waitk("scalar", "rbsb")
        P.op("scalar", lambda s: s.activation(out=g["rb_sb"][:, 0:TT], in_=g["pb_rb"][:, 0:TT],
                                              func=AF.Identity, bias=0.0, scale=1.0), inc="sc")
        nbe = P.op("scalar", lambda s: s.activation(out=g["mrb_sb"][:, 0:TT], in_=g["pb_mrb"][:, 0:TT],
                                                    func=AF.Identity, bias=0.0, scale=1.0), inc="sc")
        P.war["pb_bcast"] = {"sc": nbe}
        outs = []
        t1 = g["lnt1f"] if fsrc is not None else g["lnt1"]
        t2 = g["lnt2"]
        prev_t1 = prev_t2 = None
        for c in range(NCH):
            if fsrc == "dram":
                P.waitk("sync", "ln_xf0")
                nf = P.op("sync", lambda sy, c=c: sy.dma_start(
                    out=g["ln_xf"][0][:, 0:TT], in_=src[c, :, t0:t0 + TT]),
                    inc="qf0", inc_by=16)
                asrc = g["ln_xf"][0][:, 0:TT]
                P.wait("vector", "qf0", nf)
            elif fsrc is not None:
                asrc, rsem, rval = fsrc[c]
                if rsem is not None:
                    P.wait("vector", rsem, rval)
            else:
                asrc = xsl[c]
            P.wait("vector", "sc", nbe)
            if prev_t1 is not None:
                P.wait("vector", "vc", prev_t1)
            na = P.op("vector", lambda v, c=c, asrc=asrc: v.tensor_tensor(
                out=t1[:, 0:TT], in0=asrc, in1=g["rb_sb"][:, 0:TT], op=ALU.mult), inc="vc")
            if fsrc == "dram":
                P.war["ln_xf0"] = {"vc": na}
            if xkeep is None and fsrc is None:
                P.mark("ln_x", "vc", na)
            P.wait("vector", "vc", na)
            if gc is None:
                nn = P.op("vector", lambda v, c=c: v.scalar_tensor_tensor(
                    out=xn[c], in0=g["mrb_sb"][:, 0:TT], scalar=-1.0, in1=t1[:, 0:TT],
                    op0=ALU.mult, op1=ALU.add), inc="vc")
                prev_t1 = nn
            else:
                if prev_t2 is not None:
                    P.wait("vector", "vc", prev_t2)
                nn0 = P.op("vector", lambda v: v.scalar_tensor_tensor(
                    out=t2[:, 0:TT], in0=g["mrb_sb"][:, 0:TT], scalar=-1.0, in1=t1[:, 0:TT],
                    op0=ALU.mult, op1=ALU.add), inc="vc")
                prev_t1 = nn0
                P.wait("vector", "vc", nn0)
                nn = P.op("vector", lambda v, c=c: v.tensor_scalar(
                    out=xn[c], in0=t2[:, 0:TT], scalar1=gc[:, c:c + 1], scalar2=bc[:, c:c + 1],
                    op0=ALU.mult, op1=ALU.add), inc="vc")
                prev_t2 = nn
            outs.append(nn)
        P.war["rbsb"] = {"vc": outs[-1]}
        if xkeep is not None:
            P.war["ln_keep"] = {"tc": n_sq, "sc": n_sq_sc, "vc": outs[-1]}
        return outs

    def emit_ffn_ternary(self, P, g, tag, x_src, x_dst):
        TT = 512
        PT = 1024
        npass = T // PT
        with contextlib.ExitStack() as stk:
            nb3 = [self.sbuf(stk, f"{tag}_nb{i}", [128, NCH * TT], BF16) for i in range(2)]
            A = self.sbuf(stk, f"{tag}_A", [128, NCH_H * PT], BF16)
            Bb = self.sbuf(stk, f"{tag}_B", [128, NCH_H * PT], BF16)
            t1t = self.sbuf(stk, f"{tag}_t1", [128, TT], BF16)
            t2t = self.sbuf(stk, f"{tag}_t2", [128, TT], BF16)
            xr = [self.sbuf(stk, f"{tag}_xr0", [128, TT], F32)] * 2
            xo = [self.sbuf(stk, f"{tag}_xo0", [128, TT], F32)] * 2
            wsl = [self.sbuf(stk, f"{tag}_w{i}", [128, NCH_H * 128], BF16) for i in range(3)]
            scol = {}
            for n in ("fs1c", "fs2c", "fhs1c", "fhs2c", "fs3c"):
                shp = [128, NCH_H] if n != "fs3c" else [128, NCH]
                scol[n] = self.sbuf(stk, f"{tag}_" + n, shp, F32)
            for n, sb in scol.items():
                P.op("sync", lambda sy, n=n, sb=sb: sy.dma_start(out=sb[:], in_=self.inp[n][:]),
                     inc="qc", inc_by=16)
            q_cols = P.cnt["qc"]
            P.wait("scalar", "qc", q_cols)
            P.wait("vector", "qc", q_cols)
            P.wait("tensor", "qc", q_cols)

            SL = ([("fw1T", NCH, oc) for oc in range(NCH_H) for _ in (0,)] and [])
            SL = []
            for oc in range(NCH_H):
                SL.append(("fw1T", NCH, oc))
                SL.append(("fw2T", NCH, oc))
            SL += [("fh1T", NCH_H, oc) for oc in range(NCH_H)]
            SL += [("fh2T", NCH_H, oc) for oc in range(NCH_H)]
            SL += [("fw3T", NCH_H, oc) for oc in range(NCH)]
            NSL = len(SL)
            NTOT = npass * NSL
            slab_load = {}
            pf = {"i": 0}

            def emit_slab_load(o):
                mat, nic, oc = SL[o % NSL]
                buf = o % 3
                P.waitk("gpsimd", f"{tag}_wsl{buf}")
                n = P.op("gpsimd", lambda gg, mat=mat, nic=nic, oc=oc, buf=buf: gg.dma_start(
                    out=wsl[buf][:, 0:nic * 128],
                    in_=self.inp[mat][oc]),
                    inc=f"qw{buf}", inc_by=16)
                slab_load[o] = (f"qw{buf}", n)

            def pump(limit):
                while pf["i"] < min(limit, NTOT):
                    emit_slab_load(pf["i"])
                    pf["i"] += 1

            pump(3)
            prev_no = [None, None]
            prev_store = [None, None]
            ln_done = {}

            def emit_ln_half(ht):
                slot = ht % 2
                P.waitk("vector", f"{tag}_nb{slot}")
                xnl = [nb3[slot][:, c * TT:(c + 1) * TT] for c in range(NCH)]
                outs = self.emit_ln_tile(P, g, x_src, ht * TT, TT, xnl, 1e-5)
                ln_done[ht] = outs[-1]

            for ps in range(npass):
                t0 = ps * PT
                base = ps * NSL
                if ps == 0:
                    emit_ln_half(0)
                    emit_ln_half(1)
                xn_counts = [[ln_done[2 * ps]], [ln_done[2 * ps + 1]]]

                a_ready = []
                sidx = 0
                prev_e1 = prev_e2 = None
                for oc in range(NCH_H):
                    o1, o2 = base + sidx, base + sidx + 1
                    b1, b2 = o1 % 3, o2 % 3
                    for half in range(2):
                        toff = half * TT
                        P.wait("tensor", slab_load[o1][0], slab_load[o1][1])
                        P.wait("tensor", slab_load[o2][0], slab_load[o2][1])
                        P.wait("tensor", "vc", ln_done[2 * ps + half])
                        P.waitk("tensor", "pb0")
                        slot = (2 * ps + half) % 2
                        for ic in range(NCH):
                            P.op("tensor", lambda t, ic=ic, b1=b1, slot=slot: t.matmul(
                                g["pb"][0][:, 0:TT],
                                wsl[b1].rearrange("p (ic o) -> p ic o", o=128)[:, ic, :],
                                nb3[slot][:, ic * TT:(ic + 1) * TT],
                                start=(ic == 0), stop=(ic == NCH - 1)),
                                inc="tc" if ic == NCH - 1 else None)
                        nmm1 = P.cnt["tc"]
                        P.waitk("tensor", "pb1")
                        for ic in range(NCH):
                            P.op("tensor", lambda t, ic=ic, b2=b2, slot=slot: t.matmul(
                                g["pb"][1][:, 0:TT],
                                wsl[b2].rearrange("p (ic o) -> p ic o", o=128)[:, ic, :],
                                nb3[slot][:, ic * TT:(ic + 1) * TT],
                                start=(ic == 0), stop=(ic == NCH - 1)),
                                inc="tc" if ic == NCH - 1 else None)
                        nmm2 = P.cnt["tc"]
                        P.mark(f"{tag}_nb{slot}", "tc", nmm2)
                        P.wait("scalar", "tc", nmm1)
                        if prev_e1 is not None:
                            P.wait("scalar", "vc", prev_e1)
                        e1 = P.op("scalar", lambda s, oc=oc: s.activation(
                            out=t1t[:, 0:TT], in_=g["pb"][0][:, 0:TT], func=AF.Silu,
                            bias=0.0, scale=scol["fs1c"][:, oc:oc + 1]), inc="sc")
                        P.war["pb0"] = {"sc": e1}
                        P.wait("scalar", "tc", nmm2)
                        e2 = P.op("scalar", lambda s, oc=oc: s.activation(
                            out=t2t[:, 0:TT], in_=g["pb"][1][:, 0:TT], func=AF.Identity,
                            bias=0.0, scale=scol["fs2c"][:, oc:oc + 1]), inc="sc")
                        P.war["pb1"] = {"sc": e2}
                        P.wait("vector", "sc", e2)
                        nA = P.op("vector", lambda v, oc=oc, toff=toff: v.tensor_tensor(
                            out=A[:, oc * PT + toff: oc * PT + toff + TT],
                            in0=t1t[:, 0:TT], in1=t2t[:, 0:TT], op=ALU.mult), inc="vc")
                        prev_e1 = prev_e2 = nA
                        a_ready.append(nA)
                    P.mark(f"{tag}_wsl{b1}", "tc", nmm1)
                    P.mark(f"{tag}_wsl{b2}", "tc", P.cnt["tc"])
                    sidx += 2
                    pump(base + sidx + 3)
                a_all = max(a_ready)

                def h_layer(src_buf, dst_buf, scol_name, ready_sem, ready_val, sidx0):
                    sidx = sidx0
                    done = []
                    for oc in range(NCH_H):
                        o = base + sidx
                        bw = o % 3
                        for half in range(2):
                            toff = half * TT
                            bname = f"pb{half}"
                            P.wait("tensor", slab_load[o][0], slab_load[o][1])
                            P.wait("tensor", ready_sem, ready_val)
                            P.waitk("tensor", bname)
                            for ic in range(NCH_H):
                                P.op("tensor", lambda t, ic=ic, bw=bw, toff=toff, half=half: t.matmul(
                                    g["pb"][half][:, 0:TT],
                                    wsl[bw].rearrange("p (ic o) -> p ic o", o=128)[:, ic, :],
                                    src_buf[:, ic * PT + toff: ic * PT + toff + TT],
                                    start=(ic == 0), stop=(ic == NCH_H - 1)),
                                    inc="tc" if ic == NCH_H - 1 else None)
                            nmm = P.cnt["tc"]
                            P.wait("scalar", "tc", nmm)
                            ne = P.op("scalar", lambda s, oc=oc, toff=toff, half=half: s.activation(
                                out=dst_buf[:, oc * PT + toff: oc * PT + toff + TT],
                                in_=g["pb"][half][:, 0:TT], func=AF.Gelu,
                                bias=0.0, scale=scol[scol_name][:, oc:oc + 1]), inc="sc")
                            P.war[bname] = {"sc": ne}
                            done.append(ne)
                        P.mark(f"{tag}_wsl{bw}", "tc", P.cnt["tc"])
                        sidx += 1
                        pump(base + sidx + 3)
                    return max(done), sidx

                b_all, sidx = h_layer(A, Bb, "fhs1c", "vc", a_all, sidx)
                a2_all, sidx = h_layer(Bb, A, "fhs2c", "sc", b_all, sidx)

                if ps + 1 < npass:
                    emit_ln_half(2 * ps + 2)
                    emit_ln_half(2 * ps + 3)

                for oc in range(NCH):
                    o = base + sidx
                    bw = o % 3
                    for half in range(2):
                        toff = half * TT
                        bname = f"pb{half}"
                        if prev_no[0] is not None:
                            P.wait("sync", "vc", prev_no[0])
                        nf = P.op("sync", lambda sy, oc=oc, t0=t0, toff=toff: sy.dma_start(
                            out=xr[0][:, 0:TT], in_=x_src[oc, :, t0 + toff:t0 + toff + TT]),
                            inc="qr0", inc_by=16)
                        P.wait("tensor", slab_load[o][0], slab_load[o][1])
                        P.wait("tensor", "sc", a2_all)
                        P.waitk("tensor", bname)
                        for ic in range(NCH_H):
                            P.op("tensor", lambda t, ic=ic, bw=bw, toff=toff, half=half: t.matmul(
                                g["pb"][half][:, 0:TT],
                                wsl[bw].rearrange("p (ic o) -> p ic o", o=128)[:, ic, :],
                                A[:, ic * PT + toff: ic * PT + toff + TT],
                                start=(ic == 0), stop=(ic == NCH_H - 1)),
                                inc="tc" if ic == NCH_H - 1 else None)
                        nmm = P.cnt["tc"]
                        P.wait("vector", "tc", nmm)
                        P.wait("vector", "qr0", nf)
                        if prev_store[0] is not None:
                            P.wait("vector", "qo0", prev_store[0])
                        no = P.op("vector", lambda v, oc=oc, toff=toff, half=half: v.scalar_tensor_tensor(
                            out=xo[0][:, 0:TT], in0=g["pb"][half][:, 0:TT],
                            scalar=scol["fs3c"][:, oc:oc + 1],
                            in1=xr[0][:, 0:TT], op0=ALU.mult, op1=ALU.add), inc="vc")
                        P.war[bname] = {"vc": no}
                        prev_no[0] = no
                        P.wait("sync", "vc", no)
                        nst = P.op("sync", lambda sy, oc=oc, t0=t0, toff=toff: sy.dma_start(
                            out=x_dst[oc, :, t0 + toff:t0 + toff + TT], in_=xo[0][:, 0:TT]),
                            inc="qo0", inc_by=16)
                        prev_store[0] = nst
                    P.mark(f"{tag}_wsl{bw}", "tc", P.cnt["tc"])
                    sidx += 1
                    pump(base + sidx + 3)
            for sname in ("qo0",):
                P.wait("sync", sname, P.cnt[sname])
                P.wait("gpsimd", sname, P.cnt[sname])
            P.barrier()


    def emit_stage_a(self, P, g, x_src, x_dst):
        nc = self.nc
        TT = 512
        qa, ka, va = self.dram["qa"], self.dram["ka"], self.dram["va"]
        s_na, s_mid = self.dram["s_na"], self.dram["s_mid"]

        with contextlib.ExitStack() as stk:
            qkvw = self.sbuf(stk, "a0_qkvw", [128, NCH * 3 * D], BF16)
            na_f = self.sbuf(stk, "a0_naf", [128, NCH * TT], F32)
            na_b = self.sbuf(stk, "a0_nab", [128, NCH * TT], BF16)
            xn = self.sbuf(stk, "a0_xn", [128, NCH * TT], BF16)
            qet = [self.sbuf(stk, f"a0_qe{i}", [128, TT], BF16) for i in range(2)]
            ket = [self.sbuf(stk, f"a0_ke{i}", [128, TT], BF16) for i in range(2)]
            vti = [self.sbuf(stk, f"a0_vt{i}", [128, 1024], BF16) for i in range(2)]
            gcol = self.sbuf(stk, "a0_gc", [128, NCH], F32)
            bcol = self.sbuf(stk, "a0_bc", [128, NCH], F32)
            for ic in range(NCH):
                P.op("gpsimd", lambda gg, ic=ic: gg.dma_start(
                    out=qkvw[:, ic * 3 * D:(ic + 1) * 3 * D],
                    in_=self.inp["aqkvT"][ic * 128:(ic + 1) * 128, :]), inc="qg0", inc_by=16)
            P.op("sync", lambda sy: sy.dma_start(out=gcol[:], in_=self.inp["alngc"][:]), inc="qc", inc_by=16)
            P.op("sync", lambda sy: sy.dma_start(out=bcol[:], in_=self.inp["alnbc"][:]), inc="qc", inc_by=16)
            P.wait("tensor", "qg0", P.cnt["qg0"])
            P.wait("tensor", "qc", P.cnt["qc"])
            P.wait("vector", "qc", P.cnt["qc"])
            P.wait("vector", "qg0", P.cnt["qg0"])
            P.wait("scalar", "qc", P.cnt["qc"])

            for b in range(BPC):
                for tile in range(S // TT):
                    t0 = b * S + tile * TT
                    naps = [na_f[:, c * TT:(c + 1) * TT] for c in range(NCH)]
                    P.waitk("vector", "a0_naf")
                    outs0 = self.emit_ln_tile(P, g, x_src, t0, TT, naps, 1e-5, fsrc="dram")
                    stc = []
                    P.waitk("vector", "a0_nab")
                    casts = []
                    for c in range(NCH):
                        P.wait("sync", "vc", outs0[c])
                        nst = P.op("sync", lambda sy, c=c, t0=t0: sy.dma_start(
                            out=s_na[c, :, t0:t0 + TT], in_=na_f[:, c * TT:(c + 1) * TT]),
                            inc="qd1", inc_by=16)
                        stc.append(nst)
                        P.wait("vector", "vc", outs0[c])
                        ncst = P.op("vector", lambda v, c=c: v.tensor_copy(
                            na_b[:, c * TT:(c + 1) * TT], na_f[:, c * TT:(c + 1) * TT]), inc="vc")
                        casts.append(ncst)
                    P.war["a0_naf"] = {"qd1": stc[-1], "vc": casts[-1]}
                    P.waitk("vector", "a0_xn")
                    nb_sl = [na_b[:, c * TT:(c + 1) * TT] for c in range(NCH)]
                    xn_sl = [xn[:, c * TT:(c + 1) * TT] for c in range(NCH)]
                    outs1 = self.emit_ln_tile(P, g, None, t0, TT, xn_sl, 1e-5,
                                              gc=gcol, bc=bcol, xkeep=nb_sl,
                                              ready=[("vc", casts[c]) for c in range(NCH)])
                    P.war["a0_nab"] = dict(P.war["ln_keep"])
                    xn_ready = outs1[-1]
                    for which, dst, scale, ring, qbase in (
                            (0, qa, 0.125, qet, 2), (1, ka, 1.0, ket, 4)):
                        for oc in range(NCH):
                            og = which * D + oc * 128
                            bank = g["pb"][oc % 2]
                            qsem = f"qd{qbase + oc % 2}"
                            P.waitk("tensor", f"pb{oc % 2}")
                            P.wait("tensor", "vc", xn_ready)
                            for ic in range(NCH):
                                P.op("tensor", lambda t, ic=ic, og=og, bank=bank: t.matmul(
                                    bank[:, 0:TT], qkvw[:, ic * 3 * D + og: ic * 3 * D + og + 128],
                                    xn[:, ic * TT:(ic + 1) * TT],
                                    start=(ic == 0), stop=(ic == NCH - 1)),
                                    inc="tc" if ic == NCH - 1 else None)
                            nmm = P.cnt["tc"]
                            rb = ring[oc % 2]
                            P.wait("scalar", "tc", nmm)
                            P.waitk("scalar", f"a0r_{qbase}{oc % 2}")
                            ne = P.op("scalar", lambda s, rb=rb, bank=bank, scale=scale: s.activation(
                                out=rb[:, 0:TT], in_=bank[:, 0:TT], func=AF.Identity,
                                bias=0.0, scale=scale), inc="sc")
                            P.war[f"pb{oc % 2}"] = {"sc": ne}
                            P.wait("sync", "sc", ne)
                            nst = P.op("sync", lambda sy, rb=rb, oc=oc, t0=t0, dstd=dst: sy.dma_start(
                                out=dstd[oc, :, t0:t0 + TT], in_=rb[:, 0:TT]), inc=qsem, inc_by=16)
                            P.war[f"a0r_{qbase}{oc % 2}"] = {qsem: nst}
                    for ts in range(TT // 128):
                        vb = vti[ts % 2]
                        P.waitk("scalar", f"a0_vt{ts % 2}")
                        ev = []
                        for oh in range(2):
                            bank = g["pb"][oh]
                            P.waitk("tensor", f"pb{oh}")
                            P.wait("tensor", "vc", xn_ready)
                            for ic in range(NCH):
                                P.op("tensor", lambda t, ic=ic, ts=ts, oh=oh, bank=bank: t.matmul(
                                    bank[:, 0:TT],
                                    xn[:, ic * TT + ts * 128: ic * TT + ts * 128 + 128],
                                    qkvw[:, ic * 3 * D + 2 * D + oh * TT: ic * 3 * D + 2 * D + oh * TT + TT],
                                    start=(ic == 0), stop=(ic == NCH - 1)),
                                    inc="tc" if ic == NCH - 1 else None)
                            nmm = P.cnt["tc"]
                            P.wait("scalar", "tc", nmm)
                            ne = P.op("scalar", lambda s, vb=vb, oh=oh, bank=bank: s.activation(
                                out=vb[:, oh * TT:(oh + 1) * TT], in_=bank[:, 0:TT],
                                func=AF.Identity, bias=0.0, scale=1.0), inc="sc")
                            P.war[f"pb{oh}"] = {"sc": ne}
                            ev.append(ne)
                        P.wait("sync", "sc", ev[-1])
                        nst = P.op("sync", lambda sy, vb=vb, t0=t0, ts=ts: sy.dma_start(
                            out=va[(t0 + ts * 128) // 128, :, :], in_=vb[:]), inc=f"qd{6 + ts % 2}", inc_by=16)
                        P.war[f"a0_vt{ts % 2}"] = {f"qd{6 + ts % 2}": nst}
            for sname in ("qd1", "qd2", "qd3", "qd4", "qd5", "qd6", "qd7"):
                P.wait("sync", sname, P.cnt[sname])
                P.wait("gpsimd", sname, P.cnt[sname])
            P.barrier()

        with contextlib.ExitStack() as stk:
            outw = self.sbuf(stk, "a1_outw", [128, NCH * D], BF16)
            qbuf = [self.sbuf(stk, f"a1_q{i}", [128, NCH * TT], BF16) for i in range(2)]
            kbuf = [self.sbuf(stk, f"a1_k{i}", [128, NCH * 768], BF16) for i in range(2)]
            vbuf = [self.sbuf(stk, f"a1_v{i}", [128, 6 * 1024], BF16) for i in range(2)]
            nabuf = [self.sbuf(stk, f"a1_na{i}", [128, NCH * TT], F32) for i in range(2)]
            ctx = self.sbuf(stk, "a1_ctx", [128, NCH * TT], BF16)
            ident = self.sbuf(stk, "a1_id", [128, 128], BF16)
            maska = self.sbuf(stk, "a1_mask", [128, 384], BF16)
            s_s = [self.sbuf(stk, f"a1_ss{i}", [128, 384], F32) for i in range(2)]
            pbuf = [self.sbuf(stk, f"a1_p{i}", [128, 384], BF16) for i in range(2)]
            ptb = self.sbuf(stk, "a1_pt", [128, 384], BF16)
            stats = self.sbuf(stk, "a1_st", [128, 4], F32)
            outt = [self.sbuf(stk, f"a1_o{i}", [128, TT], F32) for i in range(2)]
            for ic in range(NCH):
                P.op("gpsimd", lambda gg, ic=ic: gg.dma_start(
                    out=outw[:, ic * D:(ic + 1) * D],
                    in_=self.inp["aoutT"][ic * 128:(ic + 1) * 128, :]), inc="qg1", inc_by=16)
            P.op("sync", lambda sy: sy.dma_start(out=ident[:], in_=self.inp["ident"][:]), inc="qc", inc_by=16)
            P.op("sync", lambda sy: sy.dma_start(out=maska[:], in_=self.inp["mask_a"][:]), inc="qc", inc_by=16)
            P.wait("tensor", "qg1", P.cnt["qg1"])
            P.wait("tensor", "qc", P.cnt["qc"])

            NTILE = S // TT
            for b in range(BPC):
                for tile in range(NTILE):
                    t0 = b * S + tile * TT
                    r = (b * NTILE + tile) % 2
                    klo = max(b * S, t0 - 128)
                    khi = min(b * S + S, t0 + TT + 128)
                    kbase = t0 - 128
                    P.waitk("gpsimd", f"a1_q{r}")
                    nq = P.op("gpsimd", lambda gg, r=r, t0=t0: gg.dma_start(
                        out=qbuf[r].rearrange("p (c t) -> p c t", t=TT),
                        in_=qa[:, :, t0:t0 + TT].rearrange("c p t -> p c t")), inc=f"qg{3 + r}", inc_by=16)
                    P.waitk("gpsimd", f"a1_k{r}")
                    nk = P.op("gpsimd", lambda gg, r=r, klo=klo, khi=khi, kbase=kbase: gg.dma_start(
                        out=kbuf[r].rearrange("p (c t) -> p c t", t=768)[:, :, klo - kbase:khi - kbase],
                        in_=ka[:, :, klo:khi].rearrange("c p t -> p c t")), inc=f"qg{5 + r}", inc_by=16)
                    P.waitk("gpsimd", f"a1_v{r}")
                    nv = P.op("gpsimd", lambda gg, r=r, klo=klo, khi=khi, kbase=kbase: gg.dma_start(
                        out=vbuf[r].rearrange("p (c d) -> p c d", d=1024)[:, (klo - kbase) // 128:(khi - kbase) // 128, :],
                        in_=va[klo // 128:khi // 128, :, :].rearrange("c p d -> p c d")), inc=f"qg{7 + r}", inc_by=16)
                    P.waitk("sync", f"a1_na{r}")
                    nna = P.op("sync", lambda sy, r=r, t0=t0: sy.dma_start(
                        out=nabuf[r].rearrange("p (c t) -> p c t", t=TT),
                        in_=s_na[:, :, t0:t0 + TT].rearrange("c p t -> p c t")), inc=f"qd{12 + r}", inc_by=16)
                    P.wait("tensor", f"qg{3 + r}", nq)
                    P.wait("tensor", f"qg{5 + r}", nk)
                    P.wait("tensor", f"qg{7 + r}", nv)
                    last_q = last_k = last_v = 0
                    ctx_done = []
                    pend = []

                    def phase2a(u, r=r):
                        hh2, j2, par2, W2, lo2, npn2 = u
                        hb2 = (hh2 % 2) * 64
                        hc2 = hh2 // 2
                        P.wait("tensor", "vc", npn2)
                        P.waitk("tensor", "pbt")
                        for rel in range(W2 // 128):
                            ntr = P.op("tensor", lambda t, rel=rel, par2=par2: t.transpose(
                                g["pb_t"][:, rel * 128:rel * 128 + 128],
                                pbuf[par2][:, rel * 128:rel * 128 + 128], ident[:]),
                                inc="tc" if rel == W2 // 128 - 1 else None)
                        P.war[f"a1_p{par2}"] = {"tc": ntr}
                        P.wait("vector", "tc", ntr)
                        P.waitk("vector", "a1_ptb")
                        npt = P.op("vector", lambda v, W2=W2: v.tensor_copy(
                            ptb[:, 0:W2], g["pb_t"][:, 0:W2]), inc="vc")
                        P.war["pbt"] = {"vc": npt}
                        P.wait("tensor", "vc", npt)
                        P.waitk("tensor", "pb2")
                        for rel in range(W2 // 128):
                            vcol = (j2 + lo2 + rel) * 1024 + hh2 * 64
                            npv = P.op("tensor", lambda t, rel=rel, vcol=vcol, r=r: t.matmul(
                                g["pb"][2][0:64, 0:128],
                                vbuf[r][:, vcol:vcol + 64],
                                ptb[:, rel * 128:rel * 128 + 128],
                                start=(rel == 0), stop=(rel == W2 // 128 - 1)),
                                inc="tc" if rel == W2 // 128 - 1 else None)
                        P.war["a1_ptb"] = {"tc": npv}
                        P.wait("vector", "tc", npv)
                        nctx = P.op("vector", lambda v, hb2=hb2, hc2=hc2, j2=j2: v.tensor_copy(
                            ctx[hb2:hb2 + 64, hc2 * TT + j2 * 128: hc2 * TT + j2 * 128 + 128],
                            g["pb"][2][0:64, 0:128]), inc="vc")
                        P.war["pb2"] = {"vc": nctx}
                        ctx_done.append(nctx)
                        return npv

                    for hh in range(H):
                        hb = (hh % 2) * 64
                        hc = hh // 2
                        for j in range(4):
                            lo = 1 if (tile == 0 and j == 0) else 0
                            hi = 1 if (tile == NTILE - 1 and j == 3) else 2
                            W = (hi - lo + 1) * 128
                            bank = g["pb"][j % 2]
                            P.waitk("tensor", f"pb{j % 2}")
                            for dk in range(lo, hi + 1):
                                kcol = hc * 768 + (j + dk) * 128
                                P.op("tensor", lambda t, hb=hb, hc=hc, j=j, dk=dk, lo=lo, r=r, kcol=kcol, bank=bank: t.matmul(
                                    bank[:, 0:0] if False else bank[:, (dk - lo) * 128:(dk - lo) * 128 + 128],
                                    qbuf[r][hb:hb + 64, hc * TT + j * 128: hc * TT + j * 128 + 128],
                                    kbuf[r][hb:hb + 64, kcol:kcol + 128],
                                    start=(dk == lo), stop=False, tile_position=(hb, 0)))
                            nmm = P.op("tensor", lambda t, lo=lo, hi=hi, W=W, bank=bank: t.matmul(
                                bank[:, 0:W], ident[:], maska[:, lo * 128:lo * 128 + W],
                                start=False, stop=True), inc="tc")
                            last_q = nmm
                            par = j % 2
                            P.wait("vector", "tc", nmm)
                            P.waitk("vector", f"a1_ss{par}")
                            nev = P.op("vector", lambda v, W=W, bank=bank, par=par: v.tensor_scalar(
                                out=s_s[par][:, 0:W], in0=bank[:, 0:W], scalar1=0.0, scalar2=None,
                                op0=ALU.bypass, op1=ALU.max, accum_out=stats[:, 0:1]), inc="vc")
                            P.war[f"pb{j % 2}"] = {"vc": nev}
                            P.wait("vector", "vc", nev)
                            nbias = P.op("vector", lambda v: v.tensor_scalar(
                                out=stats[:, 2:3], in0=stats[:, 0:1], scalar1=-1.0, scalar2=None,
                                op0=ALU.mult, op1=ALU.min, accum_out=stats[:, 1:2]), inc="vc")
                            if pend:
                                last_v = phase2a(pend.pop(0))
                            P.wait("scalar", "vc", nbias)
                            P.waitk("scalar", f"a1_p{par}")
                            nexp = P.op("scalar", lambda s, W=W, par=par: s.activation(
                                out=pbuf[par][:, 0:W], in_=s_s[par][:, 0:W], func=AF.Exp,
                                bias=stats[:, 1:2], scale=1.0, accum_out=stats[:, 3:4]), inc="sc")
                            P.war[f"a1_ss{par}"] = {"sc": nexp}
                            P.wait("vector", "sc", nexp)
                            nrs = P.op("vector", lambda v: v.reciprocal(stats[:, 2:3], stats[:, 3:4]), inc="vc")
                            P.wait("vector", "vc", nrs)
                            npn = P.op("vector", lambda v, W=W, par=par: v.tensor_scalar_mul(
                                pbuf[par][:, 0:W], pbuf[par][:, 0:W], stats[:, 2:3]), inc="vc")
                            pend.append((hh, j, par, W, lo, npn))
                    while pend:
                        last_v = phase2a(pend.pop(0))
                    ctx_all = max(ctx_done)
                    for oc in range(NCH):
                        bank = g["pb"][oc % 2]
                        P.waitk("tensor", f"pb{oc % 2}")
                        P.wait("tensor", "vc", ctx_all)
                        for ic in range(NCH):
                            P.op("tensor", lambda t, ic=ic, oc=oc, bank=bank: t.matmul(
                                bank[:, 0:TT], outw[:, ic * D + oc * 128: ic * D + oc * 128 + 128],
                                ctx[:, ic * TT:(ic + 1) * TT],
                                start=(ic == 0), stop=(ic == NCH - 1)),
                                inc="tc" if ic == NCH - 1 else None)
                        nmm = P.cnt["tc"]
                        ot = outt[oc % 2]
                        P.wait("vector", "tc", nmm)
                        P.wait("vector", f"qd{12 + r}", nna)
                        P.waitk("vector", f"a1_o{oc % 2}")
                        no = P.op("vector", lambda v, oc=oc, r=r, ot=ot, bank=bank: v.tensor_tensor(
                            out=ot[:, 0:TT], in0=bank[:, 0:TT],
                            in1=nabuf[r][:, oc * TT:(oc + 1) * TT], op=ALU.add), inc="vc")
                        P.war[f"pb{oc % 2}"] = {"vc": no}
                        P.mark(f"a1_na{r}", "vc", no)
                        P.wait("sync", "vc", no)
                        nst = P.op("sync", lambda sy, oc=oc, t0=t0, ot=ot: sy.dma_start(
                            out=s_mid[oc, :, t0:t0 + TT], in_=ot[:, 0:TT]), inc=f"qd{14 + oc % 2}", inc_by=16)
                        P.war[f"a1_o{oc % 2}"] = {f"qd{14 + oc % 2}": nst}
                    P.mark(f"a1_q{r}", "tc", last_q)
                    P.mark(f"a1_k{r}", "tc", last_q)
                    P.mark(f"a1_v{r}", "tc", last_v)
            for sname in ("qd14", "qd15"):
                P.wait("sync", sname, P.cnt[sname])
                P.wait("gpsimd", sname, P.cnt[sname])
            P.barrier()

        with contextlib.ExitStack() as stk:
            w1 = self.sbuf(stk, "a2_w1", [128, NCH * DH], BF16)
            w2 = self.sbuf(stk, "a2_w2", [128, NCH_H * D], BF16)
            b1c = self.sbuf(stk, "a2_b1", [128, NCH_H], F32)
            b2c = self.sbuf(stk, "a2_b2", [128, NCH], F32)
            midb = [self.sbuf(stk, "a2_mb0", [128, NCH * TT], BF16)]
            midf = [self.sbuf(stk, "a2_mf0", [128, TT], F32)]
            x0f = [self.sbuf(stk, "a2_x00", [128, TT], F32)]
            Ab = self.sbuf(stk, "a2_A", [128, NCH_H * TT], BF16)
            outb = [self.sbuf(stk, "a2_o0", [128, TT], F32)]
            for ic in range(NCH):
                P.op("gpsimd", lambda gg, ic=ic: gg.dma_start(
                    out=w1[:, ic * DH:(ic + 1) * DH],
                    in_=self.inp["aw1T"][ic * 128:(ic + 1) * 128, :]), inc="qg2", inc_by=16)
            for ic in range(NCH_H):
                P.op("gpsimd", lambda gg, ic=ic: gg.dma_start(
                    out=w2[:, ic * D:(ic + 1) * D],
                    in_=self.inp["aw2T"][ic * 128:(ic + 1) * 128, :]), inc="qg2", inc_by=16)
            P.op("sync", lambda sy: sy.dma_start(out=b1c[:], in_=self.inp["ab1c"][:]), inc="qc", inc_by=16)
            P.op("sync", lambda sy: sy.dma_start(out=b2c[:], in_=self.inp["ab2c"][:]), inc="qc", inc_by=16)
            P.wait("tensor", "qg2", P.cnt["qg2"])
            P.wait("scalar", "qc", P.cnt["qc"])
            P.wait("vector", "qc", P.cnt["qc"])

            for b in range(BPC):
                for tile in range(S // TT):
                    t0 = b * S + tile * TT
                    r = (b * (S // TT) + tile) % 2
                    P.waitk("gpsimd", "a2_mb0")
                    nmb = P.op("gpsimd", lambda gg, t0=t0: gg.dma_start(
                        out=midb[0].rearrange("p (c t) -> p c t", t=TT),
                        in_=s_mid[:, :, t0:t0 + TT].rearrange("c p t -> p c t")), inc="qg3", inc_by=16)
                    P.wait("tensor", "qg3", nmb)
                    gel = []
                    for oc in range(NCH_H):
                        bank = g["pb"][oc % 2]
                        P.waitk("tensor", f"pb{oc % 2}")
                        for ic in range(NCH):
                            P.op("tensor", lambda t, ic=ic, oc=oc, r=r, bank=bank: t.matmul(
                                bank[:, 0:TT], w1[:, ic * DH + oc * 128: ic * DH + oc * 128 + 128],
                                midb[0][:, ic * TT:(ic + 1) * TT],
                                start=(ic == 0), stop=(ic == NCH - 1)),
                                inc="tc" if ic == NCH - 1 else None)
                        nmm = P.cnt["tc"]
                        P.wait("scalar", "tc", nmm)
                        ne = P.op("scalar", lambda s, oc=oc, bank=bank: s.activation(
                            out=Ab[:, oc * TT:(oc + 1) * TT], in_=bank[:, 0:TT], func=AF.Gelu,
                            bias=b1c[:, oc:oc + 1], scale=1.0), inc="sc")
                        P.war[f"pb{oc % 2}"] = {"sc": ne}
                        gel.append(ne)
                    P.mark("a2_mb0", "tc", P.cnt["tc"])
                    a_all = max(gel)
                    for oc in range(NCH):
                        bank = g["pb"][oc % 2]
                        P.waitk("tensor", f"pb{oc % 2}")
                        P.wait("tensor", "sc", a_all)
                        P.waitk("sync", "a2_mf0")
                        nmf = P.op("sync", lambda sy, oc=oc, t0=t0: sy.dma_start(
                            out=midf[0][:, 0:TT], in_=s_mid[oc, :, t0:t0 + TT]), inc="qd8", inc_by=16)
                        P.waitk("sync", "a2_x00")
                        nx0 = P.op("sync", lambda sy, oc=oc, t0=t0: sy.dma_start(
                            out=x0f[0][:, 0:TT], in_=x_src[oc, :, t0:t0 + TT]), inc="qd10", inc_by=16)
                        for ic in range(NCH_H):
                            P.op("tensor", lambda t, ic=ic, oc=oc, bank=bank: t.matmul(
                                bank[:, 0:TT], w2[:, ic * D + oc * 128: ic * D + oc * 128 + 128],
                                Ab[:, ic * TT:(ic + 1) * TT],
                                start=(ic == 0), stop=(ic == NCH_H - 1)),
                                inc="tc" if ic == NCH_H - 1 else None)
                        nmm = P.cnt["tc"]
                        P.wait("vector", "tc", nmm)
                        P.wait("vector", "qd8", nmf)
                        P.wait("vector", "qd10", nx0)
                        P.waitk("vector", "a2_tmp")
                        ob = outb[0]
                        P.waitk("vector", "a2_ob0")
                        nt = P.op("vector", lambda v, oc=oc, bank=bank, ob=ob: v.scalar_tensor_tensor(
                            out=ob[:, 0:TT], in0=bank[:, 0:TT], scalar=b2c[:, oc:oc + 1],
                            in1=midf[0][:, 0:TT], op0=ALU.add, op1=ALU.add), inc="vc")
                        P.war[f"pb{oc % 2}"] = {"vc": nt}
                        P.mark("a2_mf0", "vc", nt)
                        P.wait("vector", "vc", nt)
                        no = P.op("vector", lambda v, ob=ob: v.tensor_tensor(
                            out=ob[:, 0:TT], in0=ob[:, 0:TT], in1=x0f[0][:, 0:TT], op=ALU.add), inc="vc")
                        P.mark("a2_x00", "vc", no)
                        P.wait("sync", "vc", no)
                        nst = P.op("sync", lambda sy, oc=oc, t0=t0, ob=ob: sy.dma_start(
                            out=x_dst[oc, :, t0:t0 + TT], in_=ob[:, 0:TT]), inc="qd14", inc_by=16)
                        P.war["a2_ob0"] = {"qd14": nst}
            for sname in ("qd14", "qd15"):
                P.wait("sync", sname, P.cnt[sname])
                P.wait("gpsimd", sname, P.cnt[sname])
            P.barrier()


    def _c_proj(self, P, g, l, b, tb, Q, K, Vx, ones_all, bqe, que, bkc, bvr16, l1w, l1b, ident):
        TT = 512
        NTQ = S // TT
        s_h = self.dram["s_h"]
        with contextlib.ExitStack() as pk:
            xn = [self.sbuf(pk, "cp_xn0", [128, NCH * TT], BF16)] * 2
            wsl = [self.sbuf(pk, f"cp_w{i}", [128, NCH * TT], BF16) for i in range(6)]
            wload = {}
            WMATS = [(f"bWqT{l}", 0), (f"bWqT{l}", 1), (f"bWkT{l}", 0), (f"bWkT{l}", 1),
                     (f"bWvT{l}", 0), (f"bWvT{l}", 1)]
            for i in range(6):
                mat, hf = WMATS[i]
                P.waitk("gpsimd", f"cp_w{i}")
                n = P.op("gpsimd", lambda gg, mat=mat, hf=hf, i=i: gg.dma_start(
                    out=wsl[i][:], in_=self.inp[mat][hf]),
                    inc=f"qg{i}", inc_by=16)
                wload[i] = (f"qg{i}", n)
            q0_ready = None
            for tile in range(NTQ):
                t0 = tb + tile * TT
                r = 0
                P.waitk("vector", "cp_xn0")
                xsl = [xn[r][:, c * TT:(c + 1) * TT] for c in range(NCH)]
                outs = self.emit_ln_tile(P, g, s_h, t0, TT, xsl, 1e-6, gc=l1w, bc=l1b)
                xn_ready = outs[-1]
                for wi, (dstbuf, biascol, scale) in enumerate((
                        (Q, bqe, 0.125), (K, bkc, 1.0))):
                    for hf in range(2):
                        si = wi * 2 + hf
                        sname, nld = wload[si]
                        buf = si
                        for oc4 in range(4):
                            oc = hf * 4 + oc4
                            bank = g["pb"][oc4 % 2]
                            P.waitk("tensor", f"pb{oc4 % 2}")
                            P.wait("tensor", sname, nld)
                            P.wait("tensor", "vc", xn_ready)
                            for ic in range(NCH):
                                P.op("tensor", lambda t, ic=ic, oc4=oc4, buf=buf, r=r, bank=bank: t.matmul(
                                    bank[:, 0:TT],
                                    wsl[buf][:, ic * TT + oc4 * 128: ic * TT + oc4 * 128 + 128],
                                    xn[r][:, ic * TT:(ic + 1) * TT],
                                    start=(ic == 0), stop=(ic == NCH - 1)),
                                    inc="tc" if ic == NCH - 1 else None)
                            nmm = P.cnt["tc"]
                            P.wait("scalar", "tc", nmm)
                            P.waitk("scalar", "qkevac")
                            ne = P.op("scalar", lambda s, dstbuf=dstbuf, oc=oc, tile=tile, bank=bank,
                                      biascol=biascol, scale=scale: s.activation(
                                out=dstbuf[:, oc * S + tile * TT: oc * S + tile * TT + TT],
                                in_=bank[:, 0:TT], func=AF.Identity,
                                bias=biascol[:, oc:oc + 1], scale=scale), inc="sc")
                            P.war[f"pb{oc4 % 2}"] = {"sc": ne}
                for hf in range(2):
                    si = 4 + hf
                    sname, nld = wload[si]
                    buf = si
                    for ts in range(TT // 128):
                        kc = (tile * TT + ts * 128) // 128
                        bank = g["pb"][ts % 2]
                        P.waitk("tensor", f"pb{ts % 2}")
                        P.wait("tensor", sname, nld)
                        P.wait("tensor", "vc", xn_ready)
                        P.wait("tensor", "vc", ones_all)
                        for ic in range(NCH):
                            P.op("tensor", lambda t, ic=ic, ts=ts, buf=buf, r=r, bank=bank: t.matmul(
                                bank[:, 0:TT],
                                xn[r][:, ic * TT + ts * 128: ic * TT + ts * 128 + 128],
                                wsl[buf][:, ic * TT:(ic + 1) * TT],
                                start=(ic == 0), stop=False),
                                inc=None)
                        nmm = P.op("tensor", lambda t, hf=hf, bank=bank: t.matmul(
                            bank[:, 0:TT], g["ones_row"][:],
                            bvr16[0:1, hf * TT:(hf + 1) * TT],
                            start=False, stop=True), inc="tc")
                        P.wait("scalar", "tc", nmm)
                        P.waitk("scalar", "qkevac")
                        for hh2 in range(8):
                            hh = hf * 8 + hh2
                            ne = P.op("scalar", lambda s, kc=kc, hh=hh, hh2=hh2, bank=bank: s.activation(
                                out=Vx[:, kc * (H * 64) + hh * 64: kc * (H * 64) + hh * 64 + 64],
                                in_=bank[:, hh2 * 64: hh2 * 64 + 64], func=AF.Identity,
                                bias=0.0, scale=1.0), inc="sc")
                        P.war[f"pb{ts % 2}"] = {"sc": ne}
                P.mark("cp_xn0", "tc", P.cnt["tc"])
                if tile == 0:
                    q0_ready = P.cnt["sc"]
            for i in range(6):
                P.mark(f"cp_w{i}", "tc", P.cnt["tc"])
            P.war["qkevac"] = {}
        return q0_ready
        P.barrier()



    def _c_t12(self, P, g, tq2, war_sc, Q, que, t12):
        TT = 512
        T1b, T2b, qut, trg_s, trg_c = t12
        st_loc = tq2 * TT
        P.waitk("sync", "ca_ts")
        nts = P.op("sync", lambda sy, st_loc=st_loc, trg_s=trg_s: sy.dma_start(
            out=trg_s[:], in_=self.inp["trig_s"][:, st_loc:st_loc + TT]), inc="qd10", inc_by=16)
        ntc2 = P.op("sync", lambda sy, st_loc=st_loc, trg_c=trg_c: sy.dma_start(
            out=trg_c[:], in_=self.inp["trig_c"][:, st_loc:st_loc + TT]), inc="qd11", inc_by=16)
        P.wait("gpsimd", "qd10", nts)
        P.wait("gpsimd", "qd11", ntc2)
        if war_sc:
            P.wait("gpsimd", "sc", war_sc)
        ready = []
        for c in range(NCH):
            P.waitk("gpsimd", f"ca_T12_{c}")
            P.waitk("gpsimd", "ca_qu")
            nqu = P.op("gpsimd", lambda gg, c=c, tq2=tq2, Q=Q, qut=qut, que=que: gg.tensor_scalar(
                out=qut[:, 0:TT], in0=Q[:, c * S + tq2 * TT: c * S + tq2 * TT + TT],
                scalar1=que[:, c:c + 1], scalar2=None,
                op0=ALU.add, op1=ALU.bypass), inc="gc")
            P.wait("gpsimd", "gc", nqu)
            n1 = P.op("gpsimd", lambda gg, c=c, T1b=T1b, qut=qut, trg_s=trg_s: gg.tensor_tensor(
                out=T1b[:, c * TT:(c + 1) * TT], in0=qut[:, 0:TT],
                in1=trg_s[:, 0:TT], op=ALU.mult), inc="gc")
            n2 = P.op("gpsimd", lambda gg, c=c, T2b=T2b, qut=qut, trg_c=trg_c: gg.tensor_tensor(
                out=T2b[:, c * TT:(c + 1) * TT], in0=qut[:, 0:TT],
                in1=trg_c[:, 0:TT], op=ALU.mult), inc="gc")
            P.war["ca_qu"] = {"gc": n2}
            ready.append(n2)
        P.war["ca_ts"] = {"gc": P.cnt["gc"]}
        return ready

    def _c_attn(self, P, g, l, b, tb, Q, K, Vx, ident, maskb, pswap, que, boc, s_h, t12, t12_ready0):
        T1b, T2b, qut, trg_s, trg_c = t12
        TT = 512
        NTQ = S // TT
        NQT = S // 128
        with contextlib.ExitStack() as ak:
            kpos = self.sbuf(ak, "ca_kp", [128, S], BF16)
            kpos2 = self.sbuf(ak, "ca_kp2", [128, S], BF16)
            maskb = self.sbuf(ak, "ca_mb", [128, 4 * 512], BF16)
            wo = self.sbuf(ak, "ca_wo", [128, NCH * TT], BF16)
            s_s = [self.sbuf(ak, "ca_ss0", [128, S], F32)] * 2
            pb_ = [self.sbuf(ak, f"ca_p{i}", [128, S], BF16) for i in range(2)]
            ptb = self.sbuf(ak, "ca_pt", [128, S], BF16)
            stats = self.sbuf(ak, "ca_st", [128, 24], F32)
            ctx = self.sbuf(ak, "ca_ctx", [128, NCH * TT], BF16)
            hres = [self.sbuf(ak, "ca_hr0", [128, TT], F32)] * 2
            hout = [self.sbuf(ak, "ca_ho0", [128, TT], F32)] * 2
            P.waitk("sync", "ca_kpos")
            nkp = P.op("sync", lambda sy: sy.dma_start(out=kpos[:], in_=self.inp["kposdup"][:]), inc="qd12", inc_by=16)
            nkp2 = P.op("sync", lambda sy: sy.dma_start(out=kpos2[:], in_=self.inp["kposdup2"][:]), inc="qd12", inc_by=16)
            nmb = P.op("sync", lambda sy, maskb=maskb: sy.dma_start(out=maskb[:], in_=self.inp["mask_b"][:]), inc="qd12", inc_by=16)
            P.wait("tensor", "qd12", nmb)
            P.mark("ca_kpos", "qd12", nmb)

            pend = []
            pending_op = []
            ctx_done = {}
            def phase2(unit):
                hh, qt, ridx = unit
                hb = (hh % 2) * 64
                nk = qt + 1
                pbuf = pb_[ridx]
                P.wait("tensor", "vc", exp_done[(hh, qt)])
                P.waitk("vector", "ca_ptb")
                ntr = None
                for g0 in range(0, nk, 8):
                    gend = min(g0 + 8, nk)
                    P.waitk("tensor", "pbt")
                    for kc in range(g0, gend):
                        ntr = P.op("tensor", lambda t, kc=kc, g0=g0, pbuf=pbuf: t.transpose(
                            g["pb_t"][:, (kc - g0) * 128:(kc - g0) * 128 + 128],
                            pbuf[:, kc * 128:kc * 128 + 128], ident[:]),
                            inc="tc" if kc == gend - 1 else None)
                    W = (gend - g0) * 128
                    P.wait("vector", "tc", ntr)
                    npt = P.op("vector", lambda v, g0=g0, W=W: v.tensor_copy(
                        ptb[:, g0 * 128: g0 * 128 + W], g["pb_t"][:, 0:W]), inc="vc")
                    P.war["pbt"] = {"vc": npt}
                npt_last = P.cnt["vc"]
                P.war[f"ca_p2_{ridx}"] = {"tc": ntr}
                P.wait("tensor", "vc", npt_last)
                for kc in range(nk):
                    kcg = kc
                    npv = P.op("tensor", lambda t, kcg=kcg, hh=hh, qt=qt, kc=kc: t.matmul(
                        g["pb"][2][0:64, (qt % 4) * 128:(qt % 4) * 128 + 128],
                        Vx[:, kcg * (H * 64) + hh * 64: kcg * (H * 64) + hh * 64 + 64],
                        ptb[:, kc * 128:kc * 128 + 128],
                        start=(kc == 0), stop=(kc == nk - 1)),
                        inc="tc" if kc == nk - 1 else None)
                P.war["ca_ptb"] = {"tc": npv}
                ctx_done[unit] = npv
                if qt % 4 == 3:
                    hc = hh // 2
                    P.wait("scalar", "tc", npv)
                    P.waitk("scalar", "ca_ctxw")
                    nctx = P.op("scalar", lambda s, hb=hb, hc=hc: s.activation(
                        out=ctx[hb:hb + 64, hc * TT:(hc + 1) * TT],
                        in0=0 if False else g["pb"][2][0:64, 0:TT], in_=g["pb"][2][0:64, 0:TT],
                        func=AF.Identity, bias=0.0, scale=1.0) if False else s.activation(
                        out=ctx[hb:hb + 64, hc * TT:(hc + 1) * TT],
                        in_=g["pb"][2][0:64, 0:TT],
                        func=AF.Identity, bias=0.0, scale=1.0), inc="sc")
                    P.war["pb2"] = {"sc": nctx}
                    ctx_done[unit] = ("sc", nctx)

            exp_done = {}
            for tq in range(NTQ):
                if tq == 0:
                    t12_ready = t12_ready0
                qr_all = t12_ready
                for hh in range(H):
                    hb = (hh % 2) * 64
                    hc = hh // 2
                    for qq in range(4):
                        qt = tq * 4 + qq
                        uidx = hh * 4 + qq
                        if uidx == 2 and pending_op:
                            pending_op.pop(0)()
                        ridx = uidx % 2
                        unit = (hh, qt, ridx)
                        nk5 = qt // 4 + 1
                        par = uidx % 2
                        so = 12 * ridx
                        banks = (g["pb"][0], g["pb"][1]) if par == 0 else (g["pb"][4], g["pb"][5])
                        bnames = ("pb0", "pb1") if par == 0 else ("pb4", "pb5")
                        ev = []
                        for kc5 in range(nk5):
                            W = min(512, qt * 128 + 128 - kc5 * 512)
                            bank = banks[kc5 % 2]
                            bname = bnames[kc5 % 2]
                            P.waitk("tensor", bname)
                            P.wait("tensor", "gc", qr_all[hc])
                            last_scores = P.cnt["tc"] + 1
                            P.op("tensor", lambda t, hb=hb, hc=hc, qt=qt, kc5=kc5, W=W, bank=bank: t.matmul(
                                bank[:, 0:W],
                                Q[hb:hb + 64, hc * S + qt * 128: hc * S + qt * 128 + 128],
                                K[hb:hb + 64, hc * S + kc5 * 512: hc * S + kc5 * 512 + W],
                                start=True, stop=False, tile_position=(hb, 0)))
                            P.op("tensor", lambda t, hb=hb, hc=hc, qq=qq, kc5=kc5, W=W, bank=bank: t.matmul(
                                bank[:, 0:W],
                                T1b[hb:hb + 64, hc * TT + qq * 128: hc * TT + qq * 128 + 128],
                                kpos[hb:hb + 64, kc5 * 512: kc5 * 512 + W],
                                start=False, stop=False, tile_position=(hb, 0)))
                            P.op("tensor", lambda t, hb=hb, hc=hc, qq=qq, kc5=kc5, W=W, bank=bank, nk5=nk5: t.matmul(
                                bank[:, 0:W],
                                T2b[hb:hb + 64, hc * TT + qq * 128: hc * TT + qq * 128 + 128],
                                kpos2[hb:hb + 64, kc5 * 512: kc5 * 512 + W],
                                start=False, stop=(kc5 != nk5 - 1), tile_position=(hb, 0)),
                                inc="tc" if kc5 != nk5 - 1 else None)
                            if kc5 == nk5 - 1:
                                P.op("tensor", lambda t, qq=qq, W=W, bank=bank: t.matmul(
                                    bank[:, 0:W], ident[:],
                                    maskb[:, qq * 512: qq * 512 + W],
                                    start=False, stop=True), inc="tc")
                            nmm = P.cnt["tc"]
                            if kc5 == nk5 - 1 and hb == 64 and qq == 3:
                                P.mark(f"ca_T12_{hc}", "tc", nmm)
                            P.wait("vector", "tc", nmm)
                            if kc5 == 0:
                                P.waitk("vector", "ca_ss0")
                            nev = P.op("vector", lambda v, kc5=kc5, W=W, bank=bank, ridx=ridx, so=so: v.tensor_scalar(
                                out=s_s[ridx][:, kc5 * 512: kc5 * 512 + W], in0=bank[:, 0:W],
                                scalar1=0.0, scalar2=None, op0=ALU.bypass, op1=ALU.max,
                                accum_out=stats[:, so + kc5:so + kc5 + 1]), inc="vc")
                            P.war[bname] = {"vc": nev}
                            ev.append(nev)
                        P.wait("vector", "vc", ev[-1])
                        nbias = P.op("vector", lambda v, nk5=nk5, so=so: v.tensor_scalar(
                            out=stats[:, so + 8:so + 8 + nk5], in0=stats[:, so:so + nk5],
                            scalar1=-1.0, scalar2=None,
                            op0=ALU.mult, op1=ALU.min, accum_out=stats[:, so + 7:so + 8]), inc="vc")
                        if len(pend) >= 2:
                            phase2(pend.pop(0))
                        pend.append(unit)
                        Wtot = qt * 128 + 128
                        P.wait("scalar", "vc", nbias)
                        P.waitk("scalar", f"ca_p2_{ridx}")
                        nexp = P.op("scalar", lambda s, Wtot=Wtot, ridx=ridx, so=so: s.activation(
                            out=pb_[ridx][:, 0:Wtot], in_=s_s[ridx][:, 0:Wtot], func=AF.Exp,
                            bias=stats[:, so + 7:so + 8], scale=1.0,
                            accum_out=stats[:, so + 6:so + 7]), inc="sc")
                        P.war["ca_ss0"] = {"sc": nexp}
                        P.wait("vector", "sc", nexp)
                        nrs = P.op("vector", lambda v, so=so: v.reciprocal(
                            stats[:, so + 5:so + 6], stats[:, so + 6:so + 7]), inc="vc")
                        P.wait("vector", "vc", nrs)
                        npn = P.op("vector", lambda v, Wtot=Wtot, ridx=ridx, so=so: v.tensor_scalar_mul(
                            pb_[ridx][:, 0:Wtot], pb_[ridx][:, 0:Wtot], stats[:, so + 5:so + 6]), inc="vc")
                        exp_done[(hh, qt)] = npn
                while pend:
                    phase2(pend.pop(0))
                t0 = tb + tq * TT
                ctx_all = P.cnt["sc"]
                if tq + 1 < NTQ:
                    t12_ready = self._c_t12(P, g, tq + 1, 0, Q, que, t12)

                def emit_outproj(t0=t0, ctx_all=ctx_all, l=l):
                    P.waitk("gpsimd", "ca_woL")
                    nwo0 = P.op("gpsimd", lambda gg, l=l: gg.dma_start(
                        out=wo[:], in_=self.inp[f"bWoT{l}"][0]),
                        inc="qg7", inc_by=16)
                    wo_load = {0: ("qg7", nwo0)}
                    for hf in range(2):
                        if hf == 0:
                            sname, nwo = wo_load[0]
                            P.wait("tensor", sname, nwo)
                        else:
                            P.wait("gpsimd", "tc", n_half0)
                            nwo = P.op("gpsimd", lambda gg, l=l: gg.dma_start(
                                out=wo[:], in_=self.inp[f"bWoT{l}"][1]),
                                inc="qg8", inc_by=16)
                            P.wait("tensor", "qg8", nwo)
                        for oc4 in range(4):
                            oc = hf * 4 + oc4
                            bank = g["pb"][oc % 2]
                            P.waitk("tensor", f"pb{oc % 2}")
                            P.wait("tensor", "sc", ctx_all)
                            for ic in range(NCH):
                                P.op("tensor", lambda t, ic=ic, oc4=oc4, bank=bank: t.matmul(
                                    bank[:, 0:TT], wo[:, ic * TT + oc4 * 128: ic * TT + oc4 * 128 + 128],
                                    ctx[:, ic * TT:(ic + 1) * TT],
                                    start=(ic == 0), stop=(ic == NCH - 1)),
                                    inc="tc" if ic == NCH - 1 else None)
                            nmm = P.cnt["tc"]
                            rr = 0
                            P.waitk("sync", "ca_hr0")
                            nhl = P.op("sync", lambda sy, oc=oc, t0=t0: sy.dma_start(
                                out=hres[0][:, 0:TT], in_=s_h[oc, :, t0:t0 + TT]), inc="qd2", inc_by=16)
                            P.wait("vector", "tc", nmm)
                            P.wait("vector", "qd2", nhl)
                            P.waitk("vector", "ca_ho0")
                            nho = P.op("vector", lambda v, oc=oc, rr=rr, bank=bank: v.scalar_tensor_tensor(
                                out=hout[0][:, 0:TT], in0=bank[:, 0:TT], scalar=boc[:, oc:oc + 1],
                                in1=hres[rr][:, 0:TT], op0=ALU.add, op1=ALU.add), inc="vc")
                            P.war[f"pb{oc % 2}"] = {"vc": nho}
                            P.war["ca_hr0"] = {"vc": nho}
                            P.wait("sync", "vc", nho)
                            nst = P.op("sync", lambda sy, oc=oc, t0=t0: sy.dma_start(
                                out=s_h[oc, :, t0:t0 + TT], in_=hout[0][:, 0:TT]), inc="qd4", inc_by=16)
                            P.war["ca_ho0"] = {"qd4": nst}
                        if hf == 0:
                            n_half0 = P.cnt["tc"]
                        else:
                            P.war["ca_woL"] = {"tc": P.cnt["tc"]}

                    P.mark("ca_ctxw", "tc", P.cnt["tc"])

                pending_op.append(emit_outproj)
                ctx_done = {}
            while pending_op:
                pending_op.pop(0)()
            for sname in ("qd4",):
                P.wait("sync", sname, P.cnt[sname])
                P.wait("gpsimd", sname, P.cnt[sname])
            P.barrier()



    def _c_ffn(self, P, g, l, b, tb, b1c, b2c, l2w, l2b, s_h, x_src, x_dst):
        TT = 512
        NTQ = S // TT
        with contextlib.ExitStack() as fk:
            fn = [self.sbuf(fk, f"cf_fn{i}", [128, NCH * TT], BF16) for i in range(3)]
            mid = self.sbuf(fk, "cf_mid", [128, NCH_H * TT], BF16)
            w1sl = [self.sbuf(fk, f"cf_w1{i}", [128, NCH * TT], BF16) for i in range(2)]
            w2sl = [self.sbuf(fk, f"cf_w2{i}", [128, NCH_H * 128], BF16) for i in range(2)]
            hres2 = [self.sbuf(fk, f"cf_hr{i}", [128, TT], F32) for i in range(2)]
            hout2 = self.sbuf(fk, "cf_ho", [128, TT], F32)
            x2r = self.sbuf(fk, "cf_x2", [128, TT], F32)
            w1load = {}
            w2load = {}
            def w1slab(i):
                buf = i % 2
                P.waitk("gpsimd", f"cf_w1{buf}")
                n = P.op("gpsimd", lambda gg, i=i, buf=buf, l=l: gg.dma_start(
                    out=w1sl[buf][:], in_=self.inp[f"bW1T{l}"][i]),
                    inc=f"qg{3 + buf}", inc_by=16)
                w1load[i] = (f"qg{3 + buf}", n)
            def w2slab(i):
                buf = i % 2
                P.waitk("gpsimd", f"cf_w2{buf}")
                n = P.op("gpsimd", lambda gg, i=i, buf=buf, l=l: gg.dma_start(
                    out=w2sl[buf][:], in_=self.inp[f"bW2T{l}"][i]),
                    inc=f"qg{5 + buf}", inc_by=16)
                w2load[i] = (f"qg{5 + buf}", n)
            w1slab(0); w1slab(1)
            w2slab(0); w2slab(1)
            ln_done_f = {}
            def emit_ln_f(tile2):
                slot = tile2 % 3
                P.waitk("vector", f"cf_fn{slot}")
                fsl = [fn[slot][:, c * TT:(c + 1) * TT] for c in range(NCH)]
                outs = self.emit_ln_tile(P, g, s_h, tb + tile2 * TT, TT, fsl, 1e-6, gc=l2w, bc=l2b)
                ln_done_f[tile2] = outs[-1]
            for tile in range(NTQ):
                t0 = tb + tile * TT
                r = tile % 3
                if tile == 0:
                    emit_ln_f(0)
                fn_ready = ln_done_f[tile]
                for i8 in range(8):
                    sname, nld = w1load[i8 % 8 if tile == 0 else i8]
                    buf = i8 % 2
                    for oc4 in range(4):
                        oc = i8 * 4 + oc4
                        bank = g["pb"][oc4 % 2]
                        P.waitk("tensor", f"pb{oc4 % 2}")
                        P.wait("tensor", sname, nld)
                        P.wait("tensor", "vc", fn_ready)
                        for ic in range(NCH):
                            P.op("tensor", lambda t, ic=ic, oc4=oc4, buf=buf, r=r, bank=bank: t.matmul(
                                bank[:, 0:TT],
                                w1sl[buf][:, ic * TT + oc4 * 128: ic * TT + oc4 * 128 + 128],
                                fn[r][:, ic * TT:(ic + 1) * TT],
                                start=(ic == 0), stop=(ic == NCH - 1)),
                                inc="tc" if ic == NCH - 1 else None)
                        nmm = P.cnt["tc"]
                        P.wait("scalar", "tc", nmm)
                        ne = P.op("scalar", lambda s, oc=oc, bank=bank: s.activation(
                            out=mid[:, oc * TT:(oc + 1) * TT], in_=bank[:, 0:TT],
                            func=AF.Gelu, bias=b1c[:, oc:oc + 1], scale=1.0), inc="sc")
                        P.war[f"pb{oc4 % 2}"] = {"sc": ne}
                    P.mark(f"cf_w1{buf}", "tc", P.cnt["tc"])
                    nxt = i8 + 2
                    if nxt < 8:
                        w1slab(nxt)
                    elif tile < NTQ - 1:
                        w1load[nxt - 8] = None
                if tile < NTQ - 1:
                    w1slab(0); w1slab(1)
                mid_all = P.cnt["sc"]
                P.mark(f"cf_fn{r}", "tc", P.cnt["tc"])
                if tile + 1 < NTQ:
                    emit_ln_f(tile + 1)
                for i8 in range(8):
                    oc = i8
                    sname, nld = w2load[i8]
                    buf = i8 % 2
                    if True:
                        bank = g["pb"][i8 % 2]
                        P.waitk("tensor", f"pb{i8 % 2}")
                        P.wait("tensor", sname, nld)
                        P.wait("tensor", "sc", mid_all)
                        for ic in range(NCH_H):
                            P.op("tensor", lambda t, ic=ic, buf=buf, bank=bank: t.matmul(
                                bank[:, 0:TT],
                                w2sl[buf][:, ic * 128: ic * 128 + 128],
                                mid[:, ic * TT:(ic + 1) * TT],
                                start=(ic == 0), stop=(ic == NCH_H - 1)),
                                inc="tc" if ic == NCH_H - 1 else None)
                        nmm = P.cnt["tc"]
                        rr = oc % 2
                        P.waitk("sync", f"cf_hr{rr}")
                        nhl = P.op("sync", lambda sy, oc=oc, t0=t0, rr=rr: sy.dma_start(
                            out=hres2[rr][:, 0:TT], in_=s_h[oc, :, t0:t0 + TT]), inc=f"qd{6 + rr}", inc_by=16)
                        P.wait("vector", "tc", nmm)
                        P.wait("vector", f"qd{6 + rr}", nhl)
                        P.waitk("vector", "cf_ho")
                        nho = P.op("vector", lambda v, oc=oc, rr=rr, bank=bank: v.scalar_tensor_tensor(
                            out=hout2[:, 0:TT], in0=bank[:, 0:TT], scalar=b2c[:, oc:oc + 1],
                            in1=hres2[rr][:, 0:TT], op0=ALU.add, op1=ALU.add), inc="vc")
                        P.war[f"pb{i8 % 2}"] = {"vc": nho}
                        P.war[f"cf_hr{rr}"] = {"vc": nho}
                        if l == 3:
                            P.waitk("sync", "cf_x2")
                            nx2 = P.op("sync", lambda sy, oc=oc, t0=t0: sy.dma_start(
                                out=x2r[:, 0:TT], in_=x_src[oc, :, t0:t0 + TT]), inc="qd8", inc_by=16)
                            P.wait("vector", "vc", nho)
                            P.wait("vector", "qd8", nx2)
                            nho = P.op("vector", lambda v: v.tensor_tensor(
                                out=hout2[:, 0:TT], in0=hout2[:, 0:TT], in1=x2r[:, 0:TT],
                                op=ALU.add), inc="vc")
                            P.war["cf_x2"] = {"vc": nho}
                        P.war["cf_ho"] = {"vc": nho}
                        dst = x_dst if l == 3 else s_h
                        P.wait("sync", "vc", nho)
                        nst = P.op("sync", lambda sy, oc=oc, t0=t0, dst=dst: sy.dma_start(
                            out=dst[oc, :, t0:t0 + TT], in_=hout2[:, 0:TT]), inc="qd9", inc_by=16)
                        P.mark("cf_ho", "qd9", nst)
                    P.mark(f"cf_w2{buf}", "tc", P.cnt["tc"])
                    nxt = i8 + 2
                    if nxt < 8:
                        w2slab(nxt)
                if tile < NTQ - 1:
                    w2slab(0); w2slab(1)
            for sname in ("qd9",):
                P.wait("sync", sname, P.cnt[sname])
                P.wait("gpsimd", sname, P.cnt[sname])
            P.barrier()


    def emit_stage_c(self, P, g, x_src, x_dst):
        nc = self.nc
        TT = 512
        s_h = self.dram["s_h"]
        NQT = S // 128
        NTQ = S // TT

        with contextlib.ExitStack() as stk:
            ident = self.sbuf(stk, "c_id", [128, 128], BF16)
            pswap = self.sbuf(stk, "c_psw", [128, 128], BF16)
            bqe = self.sbuf(stk, "c_bqe", [128, NCH], F32)
            que = self.sbuf(stk, "c_que", [128, NCH], F32)
            bkc = self.sbuf(stk, "c_bkc", [128, NCH], F32)
            boc = self.sbuf(stk, "c_boc", [128, NCH], F32)
            b1c = self.sbuf(stk, "c_b1c", [128, NCH_H], F32)
            b2c = self.sbuf(stk, "c_b2c", [128, NCH], F32)
            l1w = self.sbuf(stk, "c_l1w", [128, NCH], F32)
            l1b = self.sbuf(stk, "c_l1b", [128, NCH], F32)
            l2w = self.sbuf(stk, "c_l2w", [128, NCH], F32)
            l2b = self.sbuf(stk, "c_l2b", [128, NCH], F32)
            ucol = self.sbuf(stk, "c_uc", [128, NCH], F32)
            bvr = self.sbuf(stk, "c_bvr", [1, D], F32)
            bvr16 = self.sbuf(stk, "c_bvr16", [1, D], BF16)
            P.op("sync", lambda sy: sy.dma_start(out=ident[:], in_=self.inp["ident"][:]), inc="qc", inc_by=16)
            P.op("sync", lambda sy: sy.dma_start(out=pswap[:], in_=self.inp["pswap"][:]), inc="qc", inc_by=16)
            nqc0 = P.cnt["qc"]
            P.wait("tensor", "qc", nqc0)
            P.wait("vector", "qc", nqc0)

            with contextlib.ExitStack() as hk:
                h0t = self.sbuf(hk, "c_h0", [128, NCH * TT], F32)
                for b in range(BPC):
                    for tile in range(NTQ):
                        t0 = b * S + tile * TT
                        haps = [h0t[:, c * TT:(c + 1) * TT] for c in range(NCH)]
                        P.waitk("vector", "c_h0")
                        outs = self.emit_ln_tile(P, g, x_src, t0, TT, haps, 1e-5, fsrc="dram")
                        stc = []
                        for c in range(NCH):
                            P.wait("sync", "vc", outs[c])
                            nst = P.op("sync", lambda sy, c=c, t0=t0: sy.dma_start(
                                out=s_h[c, :, t0:t0 + TT], in_=h0t[:, c * TT:(c + 1) * TT]),
                                inc="qd1", inc_by=16)
                            stc.append(nst)
                        P.war["c_h0"] = {"qd1": stc[-1]}
                P.wait("sync", "qd1", P.cnt["qd1"])
                P.wait("gpsimd", "qd1", P.cnt["qd1"])
                P.barrier()

            for l in range(self.c_layers):
                for n, sb in ((f"bbqc{l}", bqe), (f"buc{l}", ucol), (f"bbkc{l}", bkc),
                              (f"bboc{l}", boc), (f"bb1c{l}", b1c), (f"bb2c{l}", b2c),
                              (f"bln1wc{l}", l1w), (f"bln1bc{l}", l1b),
                              (f"bln2wc{l}", l2w), (f"bln2bc{l}", l2b)):
                    P.waitk("sync", "c_cols")
                    P.op("sync", lambda sy, n=n, sb=sb: sy.dma_start(out=sb[:], in_=self.inp[n][:]),
                         inc="qc", inc_by=16)
                P.waitk("sync", "c_cols")
                P.op("sync", lambda sy, l=l: sy.dma_start(out=bvr[:], in_=self.inp[f"bbvr{l}"][:]),
                     inc="qc", inc_by=16)
                nqc = P.cnt["qc"]
                P.wait("vector", "qc", nqc)
                nv1 = P.op("vector", lambda v: v.tensor_tensor(out=que[:], in0=bqe[:], in1=ucol[:], op=ALU.add), inc="vc")
                P.wait("vector", "vc", nv1)
                nv2 = P.op("vector", lambda v: v.tensor_scalar_mul(que[:], que[:], 0.125), inc="vc")
                nv3 = P.op("vector", lambda v: v.tensor_scalar_mul(bqe[:], bqe[:], 0.125), inc="vc")
                nv4 = P.op("vector", lambda v: v.tensor_copy(bvr16[:], bvr[:]), inc="vc")
                ncols = P.cnt["vc"]
                P.war["c_cols"] = {"vc": ncols}
                P.wait("scalar", "vc", ncols)
                P.wait("tensor", "vc", ncols)

                for b in range(BPC):
                    tb = b * S
                    qk_stack = contextlib.ExitStack()
                    Q = self.sbuf(qk_stack, "c_Q", [128, NCH * S], BF16)
                    K = self.sbuf(qk_stack, "c_K", [128, NCH * S], BF16)
                    Vx = self.sbuf(qk_stack, "c_V", [128, (S // 128) * (H * 64)], BF16)
                    T1b = self.sbuf(qk_stack, "c_T1", [128, NCH * 512], BF16)
                    T2b = self.sbuf(qk_stack, "c_T2", [128, NCH * 512], BF16)
                    qut = self.sbuf(qk_stack, "c_qu", [128, 512], BF16)
                    trg_s = self.sbuf(qk_stack, "c_ts", [128, 512], BF16)
                    trg_c = self.sbuf(qk_stack, "c_tc", [128, 512], BF16)
                    t12 = (T1b, T2b, qut, trg_s, trg_c)
                    ones_all = P.cnt["vc"]
                    q0_ready = self._c_proj(P, g, l, b, tb, Q, K, Vx, ones_all, bqe, que, bkc, bvr16, l1w, l1b, ident)
                    t12_ready0 = self._c_t12(P, g, 0, q0_ready, Q, que, t12)
                    P.barrier()
                    self._c_attn(P, g, l, b, tb, Q, K, Vx, ident, None, pswap, que, boc, s_h, t12, t12_ready0)
                    if self.c_attn_only:
                        P.mark("c_Vones", "tc", P.cnt["tc"])
                        P.mark("c_Vones", "vc", P.cnt["vc"])
                        qk_stack.close()
                        continue
                    P.mark("c_Vones", "tc", P.cnt["tc"])
                    P.mark("c_Vones", "vc", P.cnt["vc"])
                    qk_stack.close()
                    self._c_ffn(P, g, l, b, tb, b1c, b2c, l2w, l2b, s_h, x_src, x_dst)
        return

    def build(self):
        nc = self.nc
        self.declare()
        with contextlib.ExitStack() as stk:
            P = Prog(nc)
            semnames = ["tc", "vc", "sc", "gc", "qc"] + ["b_" + e for e in ENGS]
            semnames += [f"qx{i}" for i in range(NCH)]
            semnames += [f"qw{i}" for i in range(3)]
            semnames += ["qr0", "qr1", "qo0", "qo1", "qf0", "qf1"]
            semnames += [f"qd{i}" for i in range(16)]
            semnames += [f"qg{i}" for i in range(10)]
            for s in semnames:
                P.add_sem(stk, s)
            g = {}
            g["pb"] = [stk.enter_context(nc.psum_tensor(f"pb{i}", [128, 512], F32)) for i in range(7)]
            g["pb_t"] = stk.enter_context(nc.psum_tensor("pbt", [128, 1024], BF16))
            g["pb_sum"] = g["pb"][3]; g["pb_sq"] = g["pb"][4]
            g["pb_rb"] = g["pb"][5]; g["pb_mrb"] = g["pb"][6]
            g["ones_col"] = self.sbuf(stk, "g_ones_col", [128, 1], BF16)
            g["ones_f32"] = self.sbuf(stk, "g_ones_f32", [1, 128], F32)
            g["rows"] = self.sbuf(stk, "g_rows", [1, 2560], F32)
            g["lnt1f"] = self.sbuf(stk, "g_lnt1f", [128, 512], F32)
            g["rb_sb"] = self.sbuf(stk, "g_rbsb", [128, 512], F32)
            g["mrb_sb"] = self.sbuf(stk, "g_mrbsb", [128, 512], F32)
            g["lnt1"] = g["lnt1f"]
            g["ln_xf"] = [self.sbuf(stk, "g_lnxf0", [128, 512], F32)] * 2
            g["lnt2"] = self.sbuf(stk, "g_lnt2", [128, 512], BF16)
            g["ln_x"] = self.sbuf(stk, "g_lnx", [128, NCH * 512], BF16)
            g["ln_sq"] = [self.sbuf(stk, "g_lnsq0", [128, 512], BF16)] * 2

            g["ones_row"] = self.sbuf(stk, "g_ones_row", [1, 128], BF16)
            P.op("sync", lambda sy: sy.dma_start(out=g["ones_col"][:], in_=self.inp["ones_col"][:]), inc="qc", inc_by=16)
            P.op("sync", lambda sy: sy.dma_start(out=g["ones_f32"][:], in_=self.inp["ones_f32"][:]), inc="qc", inc_by=16)
            P.op("sync", lambda sy: sy.dma_start(out=g["ones_row"][:], in_=self.inp["ones_row"][:]), inc="qc", inc_by=16)

            if self.stages == ("B",):
                self.emit_ffn_ternary(P, g, "fB", self.inp["x_in"], self.y)
            elif self.stages == ("A",):
                self.emit_stage_a(P, g, self.inp["x_in"], self.y)
            elif self.stages == ("A", "B"):
                self.emit_stage_a(P, g, self.inp["x_in"], self.dram["s_x1"])
                self.emit_ffn_ternary(P, g, "fB", self.dram["s_x1"], self.y)
            elif self.stages == ("C",):
                self.emit_stage_c(P, g, self.inp["x_in"], self.y)
            elif self.stages == ("A", "B", "C", "D"):
                self.emit_stage_a(P, g, self.inp["x_in"], self.dram["s_x1"])
                self.emit_ffn_ternary(P, g, "fB", self.dram["s_x1"], self.dram["s_x2"])
                self.emit_stage_c(P, g, self.dram["s_x2"], self.dram["s_x3"])
                self.emit_ffn_ternary(P, g, "fD", self.dram["s_x3"], self.y)
            else:
                raise NotImplementedError(self.stages)

            for sname in ("qo0", "qo1", "qd14", "qd15"):
                if P.cnt.get(sname):
                    P.wait("sync", sname, P.cnt[sname])

            block = stk.enter_context(nc.Block())
            for e, meth in (("sync", block.sync), ("tensor", block.tensor),
                            ("vector", block.vector), ("scalar", block.scalar),
                            ("gpsimd", block.gpsimd)):
                def run(E, e=e):
                    for f in P.streams[e]:
                        f(E)
                meth(run)
        self.P = P
        return nc


def host_in_maps(inputs, stages=("A", "B", "C", "D"), x_override=None):
    w = inputs
    base = {}
    for n, a in CONSTS.items():
        base[n] = np.ascontiguousarray(a)
    if "A" in stages:
        base["aqkvT"] = wt_np(w["L0_attn_a_qkv_w"])
        base["aoutT"] = wt_np(w["L0_attn_a_out_w"])
        base["aw1T"] = wt_np(w["L0_attn_a_w1"])
        base["aw2T"] = wt_np(w["L0_attn_a_w2"])
        base["ab1c"] = cols_np(w["L0_attn_a_b1"])
        base["ab2c"] = cols_np(w["L0_attn_a_b2"])
        base["alngc"] = cols_np(w["L0_attn_a_ln_g"])
        base["alnbc"] = cols_np(w["L0_attn_a_ln_b"])
    if "B" in stages or "D" in stages:
        base["fw1T"] = slab_np(w["L0_ffn_w1_w"], 128)
        base["fw2T"] = slab_np(w["L0_ffn_w2_w"], 128)
        base["fh1T"] = slab_np(w["L0_ffn_h1_w"], 128)
        base["fh2T"] = slab_np(w["L0_ffn_h2_w"], 128)
        base["fw3T"] = slab_np(w["L0_ffn_w3_w"], 128)
        base["fs1c"] = cols_np(w["L0_ffn_w1_s"])
        base["fs2c"] = cols_np(w["L0_ffn_w2_s"])
        base["fhs1c"] = cols_np(w["L0_ffn_h1_s"])
        base["fhs2c"] = cols_np(w["L0_ffn_h2_s"])
        base["fs3c"] = cols_np(w["L0_ffn_w3_s"])
    if "C" in stages:
        for l in range(4):
            base[f"bWqT{l}"] = slab_np(w["L0_attn_b_Wq"][l], 512)
            base[f"bWkT{l}"] = slab_np(w["L0_attn_b_Wk"][l], 512)
            base[f"bWvT{l}"] = slab_np(w["L0_attn_b_Wv"][l], 512)
            base[f"bWoT{l}"] = slab_np(w["L0_attn_b_Wo"][l], 512)
            base[f"bbqc{l}"] = cols_np(w["L0_attn_b_bq"][l])
            base[f"bbkc{l}"] = cols_np(w["L0_attn_b_bk"][l])
            base[f"bbvr{l}"] = np.ascontiguousarray(
                np.asarray(w["L0_attn_b_bv"][l], np.float32).reshape(1, D))
            base[f"bboc{l}"] = cols_np(w["L0_attn_b_bo"][l])
            base[f"buc{l}"] = cols_np(np.asarray(w["L0_attn_b_u_bias"][l]).reshape(D))
            base[f"bW1T{l}"] = slab_np(w["L0_attn_b_W1"][l], 512)
            base[f"bW2T{l}"] = slab_np(w["L0_attn_b_W2"][l], 128)
            base[f"bb1c{l}"] = cols_np(w["L0_attn_b_b1"][l])
            base[f"bb2c{l}"] = cols_np(w["L0_attn_b_b2"][l])
            base[f"bln1wc{l}"] = cols_np(w["L0_attn_b_ln1_w"][l])
            base[f"bln1bc{l}"] = cols_np(w["L0_attn_b_ln1_b"][l])
            base[f"bln2wc{l}"] = cols_np(w["L0_attn_b_ln2_w"][l])
            base[f"bln2bc{l}"] = cols_np(w["L0_attn_b_ln2_b"][l])
    x = np.asarray(x_override if x_override is not None else w["L0_attn_a_x"], np.float32)
    maps = []
    for c in range(NCORE):
        m = dict(base)
        xs = x[c * BPC:(c + 1) * BPC]
        m["x_in"] = np.ascontiguousarray(
            xs.reshape(BPC, S, NCH, 128).transpose(2, 3, 0, 1).reshape(NCH, 128, T))
        maps.append(m)
    return maps


def gather_y(results):
    out = np.empty((NCORE * BPC, S, D), np.float32)
    for c, r in enumerate(results):
        y = r["y"]
        out[c * BPC:(c + 1) * BPC] = y.reshape(NCH, 128, BPC, S).transpose(2, 3, 0, 1).reshape(BPC, S, D)
    return out


def kernel(**inputs):
    m = MK(stages=("A", "B", "C", "D"))
    nc = m.build()
    maps = host_in_maps(inputs, stages=("A", "B", "C", "D"))
    res = run_bass_kernel_spmd(nc, maps, core_ids=list(range(NCORE)))
    return gather_y(res.results)
